# version 09
# baseline (speedup 1.0000x reference)
import numpy as np
import ml_dtypes
import concourse.bass as bass
import concourse.mybir as mybir
from concourse.ap import AP
from concourse.bass_utils import run_bass_kernel_spmd

F32 = mybir.dt.float32
BF16 = mybir.dt.bfloat16
I32 = mybir.dt.int32
AF = mybir.ActivationFunctionType
ALU = mybir.AluOpType
AXX = mybir.AxisListType.X
BF = ml_dtypes.bfloat16

NCORES = 8
BPC = 2
B, L, D = 16, 2048, 1024
NTOK = BPC * L
EPS_Q = 1e-8



from contextlib import ExitStack


class Alloc:

    def __init__(self, nc, es):
        self.nc, self.es = nc, es

    def sb(self, name, shape, dtype):
        return self.es.enter_context(self.nc.sbuf_tensor(name, shape, dtype))

    def ps(self, name, shape, dtype=F32):
        return self.es.enter_context(self.nc.psum_tensor(name, shape, dtype))

    def sem(self, name):
        return self.es.enter_context(self.nc.semaphore(name))

    def block(self):
        return self.es.enter_context(self.nc.Block())



_AW_OPS = {
    "tensor_tensor", "tensor_add", "tensor_sub", "tensor_mul", "tensor_max",
    "tensor_scalar", "tensor_single_scalar", "tensor_scalar_add", "tensor_scalar_sub",
    "tensor_scalar_mul", "tensor_scalar_max", "tensor_scalar_min", "tensor_relu",
    "tensor_copy", "reduce_sum", "reduce_max", "tensor_reduce", "reciprocal",
    "memset", "memzero", "activation", "copy", "mul", "add", "sign", "square",
    "sqrt", "engine_nop", "scalar_tensor_tensor", "select", "bn_stats", "bn_aggr",
    "transpose", "pool", "pool_max", "pool_avg", "iota",
}


class AutoWait:

    def __init__(self, eng, sem):
        self._e, self._sem = eng, sem
        self.n = 0

    def __getattr__(self, name):
        f = getattr(self._e, name)
        if name in _AW_OPS:
            def g(*a, **k):
                if self.n:
                    self._e.wait_ge(self._sem, self.n)
                self.n += 1
                return f(*a, **k)
            return g
        return f



class RSem:
    __slots__ = ("fx", "slot", "h", "base")

    def __init__(self, fx, slot):
        self.fx, self.slot = fx, slot
        self.h = fx.pool[slot]
        self.base = fx.count[slot]


class PSem:
    __slots__ = ("h", "base", "count")

    def __init__(self, h):
        self.h, self.base, self.count = h, 0, 0


class ROp:
    __slots__ = ("o",)

    def __init__(self, o):
        self.o = o

    def then_inc(self, rs, n):
        if isinstance(rs, RSem):
            rs.fx.count[rs.slot] += n
        else:
            rs.count += n
        return ROp(self.o.then_inc(rs.h, n))


class REng:

    def __init__(self, e):
        self._e = e

    def wait_ge(self, rs, v):
        self._e.wait_ge(rs.h, rs.base + v)

    def __getattr__(self, name):
        f = getattr(self._e, name)
        if not callable(f):
            return f

        def g(*a, **k):
            r = f(*a, **k)
            return ROp(r) if r is not None else r
        return g


class FBlock:
    def __init__(self, fx):
        self._fx = fx

    def __getattr__(self, engname):
        fx = self._fx
        dec = getattr(fx._block, engname)
        snap = list(fx._snap)

        drain = list(fx._drain)

        def mydec(fn):
            def wrapped(eng):
                for i, v in enumerate(snap):
                    if v > 0:
                        eng.wait_ge(fx.pool[i], v)
                for h, c in drain:
                    if c > 0:
                        eng.wait_ge(h, c)
                return fn(REng(eng))
            return dec(wrapped)
        return mydec


COMPUTE_SEMS = {"mm", "ve", "se", "ev"}


class FAlloc:
    def __init__(self, fx, es):
        self.fx, self.es = fx, es
        self._sidx = 0

    def sb(self, name, shape, dtype):
        return self.es.enter_context(
            self.fx.nc.sbuf_tensor(f"ph{self.fx._phno}_{name}", shape, dtype))

    def ps(self, name, shape, dtype=F32):
        return self.es.enter_context(
            self.fx.nc.psum_tensor(f"ph{self.fx._phno}_{name}", shape, dtype))

    def sem(self, name):
        if name.rstrip("0123456789") in COMPUTE_SEMS:
            i = self._sidx
            self._sidx += 1
            return RSem(self.fx, i)
        h = self.fx._ges.enter_context(
            self.fx.nc.semaphore(f"ph{self.fx._phno}_{name}"))
        ps = PSem(h)
        self.fx._phase_priv.append(ps)
        return ps

    def block(self):
        return FBlock(self.fx)


class _Phase:
    def __init__(self, fx):
        self.fx = fx

    def __enter__(self):
        self.fx._phno = getattr(self.fx, "_phno", -1) + 1
        self.fx._snap = list(self.fx.count)
        self.fx._drain = [(p.h, p.count) for p in self.fx._phase_priv]
        self.fx._phase_priv = []
        self.es = ExitStack()
        self.es.__enter__()
        return FAlloc(self.fx, self.es)

    def __exit__(self, *a):
        return self.es.__exit__(*a)


FUSED_INPUTS = {
    "uT", "u_tok", "wqkv0T", "bqkv0_row", "ssmkkT", "dkrep", "fft_wa2", "fft_va2", "fft_wbst", "fft_va3",
    "idn_bf16", "toep", "drep", "woT",
    "bo_row", "w1_0", "b1_0pt", "idn_f32", "w2_0", "b2_0pt", "qkvw1T", "qkvb1_pt",
    "bv1_row", "maskB", "maskA", "projT", "projb_row", "projb_rep", "w1T_1",
    "w1_1", "b1_1pt", "w2T_1", "w2_1", "b2_1rep",
}
FUSED_OUTPUTS = {"out_s"}


class FX:
    NPOOL = 4

    def __init__(self, nc, mode):
        self.nc = nc
        self.mode = mode
        self._dram = {}
        self.pool = []
        self.count = []
        self._snap = [0] * self.NPOOL
        self._drain = []
        self._phase_priv = []
        self._block = None

    def open(self, es):
        self._ges = es
        self.pool = [es.enter_context(self.nc.semaphore(f"gp{i}")) for i in range(self.NPOOL)]
        self.count = [0] * self.NPOOL
        self._block = es.enter_context(self.nc.Block())

    def dram(self, name, shape, dtype, kind=None):
        if name in self._dram:
            return self._dram[name]
        k = kind
        if self.mode == "fused" and kind is not None:
            if kind == "ExternalInput" and name in FUSED_INPUTS:
                k = "ExternalInput"
            elif kind == "ExternalOutput" and name in FUSED_OUTPUTS:
                k = "ExternalOutput"
            else:
                k = "Internal"
        h = (self.nc.dram_tensor(name, shape, dtype, kind=k) if k is not None
             else self.nc.dram_tensor(name, shape, dtype))
        self._dram[name] = h
        return h

    def phase(self):
        return _Phase(self)


def _standalone(emit):
    nc = bass.Bass("TRN2", target_bir_lowering=False, debug=False)
    fx = FX(nc, "standalone")
    with ExitStack() as es:
        fx.open(es)
        emit(fx)
    return nc


def ap_of(handle, offset, dims):
    return AP(handle, offset, [list(d) for d in dims])


def sb_ap(handle, offset, dims, base_part=0, nparts=None):
    nat = handle.ap().ap
    row = nat[0][0]
    np_ = nparts if nparts is not None else nat[0][1]
    a = AP(handle, base_part * row + offset, [[row, np_]] + [list(d) for d in dims])
    return a



def emit_p1(fx):
    uT = fx.dram("uT", [BPC, D, L], F32, kind="ExternalInput")
    wT = fx.dram("wqkv0T", [D, 3 * D], F32, kind="ExternalInput")
    brow = fx.dram("bqkv0_row", [1, 3 * D], F32, kind="ExternalInput")
    q_s = fx.dram("q_s", [BPC, L, D], BF16, kind="ExternalOutput")
    k_s = fx.dram("k_s", [BPC, L, D], BF16, kind="ExternalOutput")
    v_s = fx.dram("v_s", [BPC, L, D], BF16, kind="ExternalOutput")
    outs = [q_s, k_s, v_s]

    NT = L // 128
    chunks = [(b, tc) for b in range(BPC) for tc in range(NT)]
    NCH = len(chunks)

    with fx.phase() as al:
        w_sb = al.sb("w_sb", [128, 8 * 3 * D], BF16)
        brow_sb = al.sb("brow_sb", [1, 3 * D], BF16)
        ones_sb = al.sb("ones_sb", [1, 128], BF16)
        ut0 = al.sb("ut0", [128, 8 * 128], BF16)
        ut1 = al.sb("ut1", [128, 8 * 128], BF16)
        ev0 = al.sb("ev0", [128, 3 * D], BF16)
        ev1 = al.sb("ev1", [128, 3 * D], BF16)
        ps0 = al.ps("ps0", [128, 1536])
        ps1 = al.ps("ps1", [128, 1536])
        ini, ld, mm, ev = [al.sem(n) for n in ("ini", "ld", "mm", "ev")]
        lu = [al.sem(f"lu{j}") for j in range(2)]
        st = [al.sem(f"st{j}") for j in range(2)]
        block = al.block()
        uts = [ut0, ut1]
        evs = [ev0, ev1]
        pss = [(ps0, ps1)]

        @block.gpsimd
        def _(g):
            g.memset(ones_sb[:, :], 1.0).then_inc(ini, 1)
            src = ap_of(wT, 0, [[3 * D, 128], [128 * 3 * D, 8], [1, 3 * D]])
            g.dma_start(out=w_sb[:, :].rearrange("p (c n) -> p c n", c=8), in_=src).then_inc(ld, 16)
            g.wait_ge(ld, 16)
            g.dma_start(out=brow_sb[0:1, :], in_=brow.ap()[0:1, :]).then_inc(ld, 16)
            for i, (b, tc) in enumerate(chunks):
                if i >= 2:
                    g.wait_ge(mm, i - 1)
                src = ap_of(uT, b * D * L + tc * 128, [[L, 128], [128 * L, 8], [1, 128]])
                g.dma_start(out=uts[i % 2][:, :].rearrange("p (c n) -> p c n", c=8), in_=src).then_inc(lu[i % 2], 16)

        @block.tensor
        def _(t):
            t.wait_ge(ini, 1)
            t.wait_ge(ld, 32)
            for i, (b, tc) in enumerate(chunks):
                t.wait_ge(lu[i % 2], 16 * (i // 2 + 1))
                if i >= 1:
                    t.wait_ge(ev, 2 * i)
                ut = uts[i % 2]
                pa, pb = pss[0]
                for h, p in ((0, pa), (1, pb)):
                    for cc in range(3):
                        t.matmul(p[:, cc * 512:(cc + 1) * 512], ones_sb[0:1, :],
                                 brow_sb[0:1, h * 1536 + cc * 512:h * 1536 + (cc + 1) * 512],
                                 start=True, stop=False)
                last = None
                for dc in range(8):
                    lhsT = sb_ap(ut, dc * 128, [[1, 128]])
                    for h, p in ((0, pa), (1, pb)):
                        for cc in range(3):
                            last = t.matmul(p[:, cc * 512:(cc + 1) * 512], lhsT,
                                            sb_ap(w_sb, dc * 3 * D + h * 1536 + cc * 512, [[1, 512]]),
                                            start=False, stop=(dc == 7))
                last.then_inc(mm, 1)

        @block.scalar
        def _(s):
            s = AutoWait(s, ev)
            for i in range(NCH):
                s.wait_ge(mm, i + 1)
                if i >= 2:
                    s.wait_ge(st[i % 2], 48 * (i // 2))
                pa, pb = pss[0]
                e = evs[i % 2]
                s.copy(e[:, 0:1536], pa[:, :]).then_inc(ev, 1)
                s.copy(e[:, 1536:3072], pb[:, :]).then_inc(ev, 1)

        @block.sync
        def _(sy):
            for i, (b, tc) in enumerate(chunks):
                sy.wait_ge(ev, 2 * i + 2)
                e = evs[i % 2]
                for oi, o in enumerate(outs):
                    dst = ap_of(o, b * L * D + tc * 128 * D, [[D, 128], [1, D]])
                    sy.dma_start(out=dst, in_=e[:, oi * D:(oi + 1) * D]).then_inc(st[i % 2], 16)
            sy.wait_ge(st[0], 48 * (NCH // 2))
            sy.wait_ge(st[1], 48 * (NCH // 2))


def build_p1():
    return _standalone(emit_p1)



def emit_p4(fx):
    y_s = fx.dram("y_s", [BPC, L, D], BF16, kind="ExternalInput")
    woT = fx.dram("woT", [D, D], F32, kind="ExternalInput")
    borow = fx.dram("bo_row", [1, D], F32, kind="ExternalInput")
    u_tok = fx.dram("u_tok", [BPC, L, D], F32, kind="ExternalInput")
    x1_s = fx.dram("x1_s", [BPC, L, D], BF16, kind="ExternalOutput")
    x1max = fx.dram("x1max_local", [128, 16], F32, kind="ExternalOutput")

    NT = L // 128
    chunks = [(b, tc) for b in range(BPC) for tc in range(NT)]
    NCH = len(chunks)
    VPC, SPC = 10, 4

    with fx.phase() as al:
        yt_sb = al.sb("yt_sb", [128, 8 * BPC * L], BF16)
        wo_sb = al.sb("wo_sb", [128, 8 * D], BF16)
        borow_sb = al.sb("borow_sb", [1, D], BF16)
        ones_sb = al.sb("ones_sb", [1, 128], BF16)
        ut0 = al.sb("ut0", [128, D], F32)
        ut1 = al.sb("ut1", [128, D], F32)
        t1_0 = al.sb("t1_0", [128, D], F32)
        t1_1 = al.sb("t1_1", [128, D], F32)
        xo0 = al.sb("xo0", [128, D], BF16)
        xo1 = al.sb("xo1", [128, D], BF16)
        srx = al.sb("srx", [128, 16], F32)
        maxcol = al.sb("maxcol", [128, NCH + 1], F32)
        maxb = al.sb("maxb", [128, 16], F32)
        pmA = al.ps("pmA", [128, D])
        pmB = al.ps("pmB", [128, D])
        trash = al.ps("trash", [128, D])
        ini, ld, lw4, mm, ve, se = [al.sem(n) for n in ("ini", "ld", "lw4", "mm", "ve", "se")]
        lu = [al.sem(f"lu4{j}") for j in range(2)]
        st = [al.sem(f"st4{j}") for j in range(2)]
        block = al.block()
        uts = [ut0, ut1]
        t1s = [t1_0, t1_1]
        xos = [xo0, xo1]
        pms = [pmA, pmB]

        @block.sync
        def _(sy):
            for b in range(BPC):
                for dc in range(8):
                    dst = sb_ap(yt_sb, dc * BPC * L + b * L, [[1, L]])
                    sy.dma_start(out=dst, in_=y_s.ap()[b, :, dc * 128:(dc + 1) * 128], transpose=True).then_inc(ld, 16)
            for i in range(NCH):
                b, tc = chunks[i]
                sy.wait_ge(se, SPC * i + 4)
                sy.dma_start(out=ap_of(x1_s, b * L * D + tc * 128 * D, [[D, 128], [1, D]]),
                             in_=xos[i % 2][:, :]).then_inc(st[i % 2], 16)
            sy.wait_ge(ve, VPC * NCH + 2)
            sy.dma_start(out=x1max.ap()[:, :], in_=maxb[:, :]).then_inc(st[0], 16)
            sy.wait_ge(st[0], 16 * (NCH // 2 + 1))
            sy.wait_ge(st[1], 16 * (NCH // 2))

        @block.gpsimd
        def _(g):
            g.memset(ones_sb[:, :], 1.0).then_inc(ini, 1)
            src = ap_of(woT, 0, [[D, 128], [128 * D, 8], [1, D]])
            g.dma_start(out=wo_sb[:, :].rearrange("p (c n) -> p c n", c=8), in_=src).then_inc(lw4, 16)
            g.wait_ge(lw4, 16)
            g.dma_start(out=borow_sb[0:1, :], in_=borow.ap()[0:1, :]).then_inc(lw4, 16)
            for i, (b, tc) in enumerate(chunks):
                if i >= 2:
                    g.wait_ge(ve, VPC * (i - 2) + 1)
                g.dma_start(out=uts[i % 2][:, :],
                            in_=ap_of(u_tok, b * L * D + tc * 128 * D, [[D, 128], [1, D]])).then_inc(lu[i % 2], 16)

        @block.tensor
        def _(t):
            t.wait_ge(ini, 1)
            t.wait_ge(ld, 16 * 16)
            t.wait_ge(lw4, 32)
            for i, (b, tc) in enumerate(chunks):
                if i >= 2:
                    t.wait_ge(ve, VPC * (i - 2) + 1)
                p = pms[i % 2]
                t.matmul(p[:, 0:512], ones_sb[0:1, :], borow_sb[0:1, 0:512], start=True, stop=False)
                t.matmul(p[:, 512:1024], ones_sb[0:1, :], borow_sb[0:1, 512:1024], start=True, stop=False)
                last = None
                for dc in range(8):
                    lhsT = sb_ap(yt_sb, dc * BPC * L + b * L + tc * 128, [[1, 128]])
                    for cc in range(2):
                        last = t.matmul(p[:, cc * 512:(cc + 1) * 512], lhsT,
                                        sb_ap(wo_sb, dc * D + cc * 512, [[1, 512]]),
                                        start=False, stop=(dc == 7))
                last.then_inc(mm, 1)

        @block.vector
        def _(v):
            v = AutoWait(v, ve)
            for i in range(NCH):
                vb, sb_, = VPC * i, SPC * i
                v.wait_ge(mm, i + 1)
                v.wait_ge(lu[i % 2], 16 * (i // 2 + 1))
                if i >= 1:
                    v.wait_ge(se, SPC * i)
                p, t1 = pms[i % 2], t1s[i % 2]
                v.tensor_add(t1[:, :], p[:, :], uts[i % 2][:, :]).then_inc(ve, 1)
                v.wait_ge(ve, vb + 1)
                v.reduce_sum(srx[:, 0:1], t1[:, :], axis=AXX).then_inc(ve, 1)
                v.wait_ge(ve, vb + 2)
                v.tensor_scalar_mul(srx[:, 2:3], srx[:, 0:1], 1.0 / D).then_inc(ve, 1)
                v.wait_ge(ve, vb + 3)
                v.tensor_mul(srx[:, 3:4], srx[:, 2:3], srx[:, 2:3]).then_inc(ve, 1)
                v.wait_ge(se, sb_ + 1)
                v.tensor_scalar_mul(srx[:, 4:5], srx[:, 1:2], 1.0 / D).then_inc(ve, 1)
                v.wait_ge(ve, vb + 5)
                v.tensor_sub(srx[:, 5:6], srx[:, 4:5], srx[:, 3:4]).then_inc(ve, 1)
                v.wait_ge(ve, vb + 6)
                v.tensor_scalar_add(srx[:, 6:7], srx[:, 5:6], 1e-5).then_inc(ve, 1)
                v.wait_ge(se, sb_ + 3)
                v.tensor_mul(srx[:, 8:9], srx[:, 7:8], srx[:, 2:3]).then_inc(ve, 1)
                v.wait_ge(ve, vb + 8)
                v.tensor_scalar_mul(srx[:, 9:10], srx[:, 8:9], -1.0).then_inc(ve, 1)
                v.wait_ge(se, sb_ + 4)
                v.reduce_max(maxcol[:, i:i + 1], xos[i % 2][:, :], axis=AXX).then_inc(ve, 1)
            v.wait_ge(ve, VPC * NCH)
            v.reduce_max(maxcol[:, NCH:NCH + 1], maxcol[:, 0:NCH], axis=AXX).then_inc(ve, 1)
            v.wait_ge(ve, VPC * NCH + 1)
            v.tensor_copy(maxb[:, :], sb_ap(maxcol, NCH, [[0, 16]])).then_inc(ve, 1)

        @block.scalar
        def _(s):
            s = AutoWait(s, se)
            for i in range(NCH):
                vb, sb_ = VPC * i, SPC * i
                s.wait_ge(ve, vb + 1)
                t1 = t1s[i % 2]
                s.activation(trash[:, :], t1[:, :], AF.Square, accum_out=srx[:, 1:2]).then_inc(se, 1)
                s.wait_ge(ve, vb + 7)
                s.activation(srx[:, 7:8], srx[:, 6:7], AF.Ln).then_inc(se, 1)
                s.wait_ge(se, sb_ + 2)
                s.activation(srx[:, 7:8], srx[:, 7:8], AF.Exp, scale=-0.5).then_inc(se, 1)
                s.wait_ge(ve, vb + 9)
                if i >= 2:
                    s.wait_ge(st[i % 2], 16 * (i // 2))
                s.activation(xos[i % 2][:, :], t1[:, :], AF.Identity,
                             scale=srx[:, 7:8], bias=srx[:, 9:10]).then_inc(se, 1)

def build_p4():
    return _standalone(emit_p4)



def emit_p5(fx):
    x1_s = fx.dram("x1_s", [BPC, L, D], BF16, kind="ExternalInput")
    x1max = fx.dram("x1max_local", [128, 16], F32, kind="ExternalInput")
    w1 = fx.dram("w1_0", [4 * D, D], F32, kind="ExternalInput")
    b1pt = fx.dram("b1_0pt", [128, 32], F32, kind="ExternalInput")
    idn = fx.dram("idn_f32", [128, 128], F32, kind="ExternalInput")
    h_s = fx.dram("h_s", [4 * D, NTOK], BF16, kind="ExternalOutput")
    hmax = fx.dram("hmax_local", [128, 16], F32, kind="ExternalOutput")
    wqb = fx.dram("w1q_bounce", [4 * D, D], BF16)
    cc_in = fx.dram("cc_in5", [128, 16], F32)
    cc_out = fx.dram("cc_out5", [128, 16], F32)
    F = 4 * D
    NFC = 32

    with fx.phase() as al:
        mx_sb = al.sb("mx_sb", [128, 16], F32)
        row_sb = al.sb("row_sb", [1, 128], F32)
        sc_sb = al.sb("sc_sb", [1, 8], F32)
        onesf = al.sb("onesf", [1, 128], F32)
        idn_sb = al.sb("idn_sb", [128, 128], F32)
        s1pp = al.sb("s1pp", [128, 1], F32)
        rs1pp = al.sb("rs1pp", [128, 1], F32)
        rsw = al.sb("rsw", [128, 1], F32)
        rs1sw = al.sb("rs1sw", [128, 32], F32)
        b1pt_sb = al.sb("b1pt_sb", [128, 32], F32)
        wna0 = al.sb("wna0", [128, D], F32)
        wna1 = al.sb("wna1", [128, D], F32)
        wdiv = al.sb("wdiv", [128, D], F32)
        wi32 = al.sb("wi32", [128, D], I32)
        wbf0 = al.sb("wbf0", [128, D], BF16)
        wbf1 = al.sb("wbf1", [128, D], BF16)
        swcol = al.sb("swcol", [128, 32], F32)
        s1sw = al.sb("s1sw", [128, 32], F32)
        bq1 = al.sb("bq1", [128, 32], F32)
        bscr = al.sb("bscr", [128, 32], F32)
        bi32 = al.sb("bi32", [128, 32], I32)
        wqT = al.sb("wqT", [128, 8 * F], BF16)
        xqT = al.sb("xqT", [128, 8 * NTOK], BF16)
        xscr = al.sb("xscr", [128, 2048], F32)
        xi32 = al.sb("xi32", [128, 2048], I32)
        ht = [al.sb(f"ht{j}", [128, 512], BF16) for j in range(4)]
        hmaxcol = al.sb("hmaxcol", [128, 257], F32)
        hmaxb = al.sb("hmaxb", [128, 16], F32)
        pscr = al.ps("pscr", [128, 128])
        hp = [al.ps(f"hp{j}", [128, 512]) for j in range(4)]
        ini, ld, ltx, ltw, mm, ve, se, cc, ccd = [al.sem(n) for n in
            ("ini", "ld", "ltx", "ltw", "mm", "ve", "se", "cc", "ccd")]
        lw = [al.sem(f"lw5{j}") for j in range(2)]
        bs = [al.sem(f"bs5{j}") for j in range(2)]
        st = [al.sem(f"st5{j}") for j in range(4)]
        block = al.block()

        V_S0, V_W, V_SC, V_X = 5, 6, 6, 4
        v_after_w = V_S0 + NFC * V_W
        v_after_sc = v_after_w + V_SC
        v_after_x = v_after_sc + 16 * V_X
        T_S0 = 2
        NTILE = NFC * 8

        @block.gpsimd
        def _(g):
            g.memset(onesf[:, :], 1.0).then_inc(ini, 1)
            g.dma_start(out=cc_in.ap()[:, :], in_=x1max.ap()[:, :]).then_inc(cc, 16)
            g.wait_ge(cc, 16)
            g.collective_compute("AllReduce", ALU.max, replica_groups=[list(range(NCORES))],
                                 ins=[cc_in.ap().opt()], outs=[cc_out.ap().opt()]).then_inc(ccd, 1)
            g.wait_ge(ccd, 1)
            g.dma_start(out=mx_sb[:, :], in_=cc_out.ap()[:, :]).then_inc(ld, 16)
            g.wait_ge(ld, 16)
            g.dma_start(out=idn_sb[:, :], in_=idn.ap()[:, :]).then_inc(ld, 16)
            g.wait_ge(ld, 32)
            g.dma_start(out=b1pt_sb[:, :], in_=b1pt.ap()[:, :]).then_inc(ld, 16)
            for fc in range(NFC):
                if fc >= 2:
                    g.wait_ge(ve, V_S0 + (fc - 1) * V_W)
                g.dma_start(out=[wna0, wna1][fc % 2][:, :],
                            in_=w1.ap()[fc * 128:(fc + 1) * 128, :]).then_inc(lw[fc % 2], 16)
                g.wait_ge(ve, V_S0 + fc * V_W + 6)
                g.dma_start(out=wqb.ap()[fc * 128:(fc + 1) * 128, :],
                            in_=[wbf0, wbf1][fc % 2][:, :]).then_inc(bs[fc % 2], 16)

        @block.sync
        def _(sy):
            for b in range(BPC):
                for dc in range(8):
                    dst = sb_ap(xqT, dc * NTOK + b * L, [[1, L]])
                    sy.dma_start(out=dst, in_=x1_s.ap()[b, :, dc * 128:(dc + 1) * 128],
                                 transpose=True).then_inc(ltx, 16)
            sy.wait_ge(bs[0], 16 * (NFC // 2))
            sy.wait_ge(bs[1], 16 * (NFC // 2))
            for dc in range(8):
                dst = sb_ap(wqT, dc * F, [[1, F]])
                sy.dma_start(out=dst, in_=wqb.ap()[:, dc * 128:(dc + 1) * 128],
                             transpose=True).then_inc(ltw, 16)
            for i in range(NTILE):
                sy.wait_ge(se, i + 1)
                fc, th, tt = i // 8, (i % 8) // 4, i % 4
                col = th * 2048 + tt * 512
                sy.dma_start(out=ap_of(h_s, fc * 128 * NTOK + col, [[NTOK, 128], [1, 512]]),
                             in_=ht[i % 4][:, :]).then_inc(st[i % 4], 16)
            sy.wait_ge(ve, v_after_x + NTILE + 2)
            sy.dma_start(out=hmax.ap()[:, :], in_=hmaxb[:, :]).then_inc(st[0], 16)
            sy.wait_ge(st[0], 16 * (NTILE // 4 + 1))
            for j in range(1, 4):
                sy.wait_ge(st[j], 16 * (NTILE // 4))

        @block.tensor
        def _(t):
            t.wait_ge(ini, 1)
            t.wait_ge(ld, 48)
            t.transpose(pscr[0:1, 0:128], mx_sb[:, 0:1], idn_sb[:, :]).then_inc(mm, 1)
            t.wait_ge(ve, 3)
            t.matmul(pscr[:, 0:1], onesf[0:1, :], sc_sb[0:1, 1:2], start=True, stop=True).then_inc(mm, 1)
            t.wait_ge(ltx, 16 * 16)
            t.wait_ge(ltw, 16 * 8)
            t.wait_ge(ve, v_after_x)
            n = 0
            for fc in range(NFC):
                for th in range(2):
                    for dc in range(8):
                        lhsT = sb_ap(wqT, dc * F + fc * 128, [[1, 128]])
                        for tt in range(4):
                            nt = fc * 8 + th * 4 + tt
                            if dc == 0 and nt >= 4:
                                t.wait_ge(se, nt - 3)
                            last = t.matmul(hp[tt][:, :], lhsT,
                                            sb_ap(xqT, dc * NTOK + th * 2048 + tt * 512, [[1, 512]]),
                                            start=(dc == 0), stop=(dc == 7))
                            if dc == 7:
                                last.then_inc(mm, 1)

        @block.vector
        def _(v):
            v = AutoWait(v, ve)
            v.wait_ge(mm, 1)
            v.tensor_copy(row_sb[0:1, :], pscr[0:1, 0:128]).then_inc(ve, 1)
            v.wait_ge(ve, 1)
            v.reduce_max(sc_sb[0:1, 0:1], row_sb[0:1, :], axis=AXX).then_inc(ve, 1)
            v.wait_ge(ve, 2)
            v.tensor_scalar(sc_sb[0:1, 1:2], sc_sb[0:1, 0:1], EPS_Q, 1.0 / 255.0,
                            ALU.max, ALU.mult).then_inc(ve, 1)
            v.wait_ge(mm, 2)
            v.tensor_copy(s1pp[:, :], pscr[:, 0:1]).then_inc(ve, 1)
            v.wait_ge(ve, 4)
            v.reciprocal(rs1pp[:, :], s1pp[:, :]).then_inc(ve, 1)
            for fc in range(NFC):
                vb = V_S0 + fc * V_W
                v.wait_ge(lw[fc % 2], 16 * (fc // 2 + 1))
                wna = [wna0, wna1][fc % 2]
                v.reduce_max(swcol[:, fc:fc + 1], wna[:, :], axis=AXX,
                             apply_absolute_value=True).then_inc(ve, 1)
                v.wait_ge(ve, vb + 1)
                v.tensor_scalar(swcol[:, fc:fc + 1], swcol[:, fc:fc + 1], EPS_Q, 1.0 / 127.0,
                                ALU.max, ALU.mult).then_inc(ve, 1)
                v.wait_ge(ve, vb + 2)
                v.reciprocal(rsw[:, :], swcol[:, fc:fc + 1]).then_inc(ve, 1)
                v.wait_ge(ve, vb + 3)
                v.tensor_scalar(wdiv[:, :], wna[:, :], rsw[:, 0:1], None,
                                ALU.mult).then_inc(ve, 1)
                v.wait_ge(ve, vb + 4)
                v.tensor_copy(wi32[:, :], wdiv[:, :]).then_inc(ve, 1)
                v.wait_ge(ve, vb + 5)
                if fc >= 2:
                    v.wait_ge(bs[fc % 2], 16 * (fc // 2))
                v.tensor_copy([wbf0, wbf1][fc % 2][:, :], wi32[:, :]).then_inc(ve, 1)
            vb = v_after_w
            v.wait_ge(ld, 48)
            v.tensor_tensor(s1sw[:, :], swcol[:, :], sb_ap(s1pp, 0, [[0, 32]]),
                            ALU.mult).then_inc(ve, 1)
            v.wait_ge(ve, vb + 1)
            v.reciprocal(rs1sw[:, :], s1sw[:, :]).then_inc(ve, 1)
            v.wait_ge(ve, vb + 2)
            v.tensor_tensor(bscr[:, :], b1pt_sb[:, :], rs1sw[:, :], ALU.mult).then_inc(ve, 1)
            v.wait_ge(ve, vb + 3)
            v.tensor_copy(bi32[:, :], bscr[:, :]).then_inc(ve, 1)
            v.wait_ge(ve, vb + 4)
            v.tensor_copy(bscr[:, :], bi32[:, :]).then_inc(ve, 1)
            v.wait_ge(ve, vb + 5)
            v.tensor_tensor(bq1[:, :], bscr[:, :], s1sw[:, :], ALU.mult).then_inc(ve, 1)
            for xt in range(16):
                vb = v_after_sc + xt * V_X
                off = (xt // 2) * NTOK + (xt % 2) * L
                v.wait_ge(ltx, 16 * 16)
                v.tensor_scalar(xscr[:, :], sb_ap(xqT, off, [[1, L]]), rs1pp[:, 0:1], None,
                                ALU.mult).then_inc(ve, 1)
                v.wait_ge(ve, vb + 1)
                v.tensor_copy(xi32[:, :], xscr[:, :]).then_inc(ve, 1)
                v.wait_ge(ve, vb + 2)
                v.tensor_copy(xscr[:, :], xi32[:, :]).then_inc(ve, 1)
                v.wait_ge(ve, vb + 3)
                v.tensor_scalar(sb_ap(xqT, off, [[1, L]]), xscr[:, :], 255.0, 0.0,
                                ALU.min, ALU.max).then_inc(ve, 1)
            for i in range(NTILE):
                v.wait_ge(se, i + 1)
                v.reduce_max(hmaxcol[:, i:i + 1], ht[i % 4][:, :], axis=AXX).then_inc(ve, 1)
            vb = v_after_x + NTILE
            v.wait_ge(ve, vb)
            v.reduce_max(hmaxcol[:, NTILE:NTILE + 1], hmaxcol[:, 0:NTILE], axis=AXX).then_inc(ve, 1)
            v.wait_ge(ve, vb + 1)
            v.tensor_copy(hmaxb[:, :], sb_ap(hmaxcol, NTILE, [[0, 16]])).then_inc(ve, 1)

        @block.scalar
        def _(s):
            s = AutoWait(s, se)
            for i in range(NTILE):
                fc = i // 8
                s.wait_ge(mm, T_S0 + i + 1)
                if i >= 4:
                    s.wait_ge(st[i % 4], 16 * (i // 4))
                    s.wait_ge(ve, v_after_x + (i - 4) + 1)
                s.activation(ht[i % 4][:, :], hp[i % 4][:, :], AF.Relu,
                             scale=s1sw[:, fc:fc + 1], bias=bq1[:, fc:fc + 1]).then_inc(se, 1)


def build_p5():
    return _standalone(emit_p5)



def emit_p6(fx):
    h_s = fx.dram("h_s", [4 * D, NTOK], BF16, kind="ExternalInput")
    hmax = fx.dram("hmax_local", [128, 16], F32, kind="ExternalInput")
    w2 = fx.dram("w2_0", [D, 4 * D], F32, kind="ExternalInput")
    b2pt = fx.dram("b2_0pt", [128, 8], F32, kind="ExternalInput")
    idn = fx.dram("idn_f32", [128, 128], F32, kind="ExternalInput")
    x1_s = fx.dram("x1_s", [BPC, L, D], BF16, kind="ExternalInput")
    x2_s = fx.dram("x2_s", [BPC, L, D], BF16, kind="ExternalOutput")
    wqb = fx.dram("w2q_bounce", [D, 4 * D], BF16)
    rowb = fx.dram("rowb6", [16, 128], F32)
    cc_in = fx.dram("cc_in6", [128, 16], F32)
    cc_out = fx.dram("cc_out6", [128, 16], F32)
    F = 4 * D
    NTC = NTOK // 128

    V0, V_W2, V_SC2 = 5, 9, 8
    NWC = 8
    vq_done, vln = {}, {}
    vC = V0 + NWC * V_W2 + V_SC2 + 1 + 1
    vC = V0 + NWC * V_W2 + V_SC2 + 3
    v_pre = vC
    VQ, VLN = 8, 11
    vq_done[0] = vC + VQ
    vC += VQ
    for tc in range(NTC):
        if tc + 1 < NTC:
            vq_done[tc + 1] = vC + VQ
            vC += VQ
        vln[tc] = (vC, vC + VLN)
        vC += VLN
    V_TOTAL = vC

    with fx.phase() as al:
        mx_sb = al.sb("mx_sb", [128, 16], F32)
        row_sb = al.sb("row_sb", [1, 128], F32)
        sc_sb = al.sb("sc_sb", [1, 8], F32)
        onesf = al.sb("onesf", [1, 128], F32)
        idn_sb = al.sb("idn_sb", [128, 128], F32)
        s2pp = al.sb("s2pp", [128, 1], F32)
        rs2pp = al.sb("rs2pp", [128, 1], F32)
        b2pt_sb = al.sb("b2pt_sb", [128, 8], F32)
        wna = al.sb("wna", [128, F], F32)
        wdiv = al.sb("wdiv", [128, F // 2], F32)
        wi32 = al.sb("wi32", [128, F // 2], I32)
        wbf = al.sb("wbf", [128, F], BF16)
        swcol = al.sb("swcol", [128, 8], F32)
        rsw = al.sb("rsw", [128, 1], F32)
        s2sw = al.sb("s2sw", [128, 8], F32)
        rs2sw = al.sb("rs2sw", [128, 8], F32)
        bq2 = al.sb("bq2", [128, 8], F32)
        bscr = al.sb("bscr", [128, 8], F32)
        bi32 = al.sb("bi32", [128, 8], I32)
        comb = al.sb("comb", [128, 16], F32)
        combT = al.sb("combT", [16, 128], F32)
        rows2 = al.sb("rows2", [1, 2048], F32)
        scale2rep = al.sb("scale2rep", [128, D], F32)
        bq2rep = al.sb("bq2rep", [128, D], F32)
        w2T_sb = al.sb("w2T_sb", [128, 32 * D], BF16)
        hstrip = [al.sb(f"hstrip{j}", [128, F], BF16) for j in range(2)]
        hscr = al.sb("hscr", [128, F // 2], F32)
        hi32 = al.sb("hi32", [128, F // 2], I32)
        hq = [al.sb(f"hq{j}", [128, F], BF16) for j in range(2)]
        x1t = [al.sb(f"x1t{j}", [128, D], BF16) for j in range(2)]
        t1_ = [al.sb(f"t6a{j}", [128, D], F32) for j in range(2)]
        t2_ = al.sb("t6b", [128, D], F32)
        xo = [al.sb(f"xo6{j}", [128, D], BF16) for j in range(2)]
        srx = al.sb("srx6", [128, 16], F32)
        pscr = al.ps("pscr6", [128, 128])
        prow = al.ps("prow6", [128, D])
        po = [al.ps(f"po6{j}", [128, D]) for j in range(2)]
        trash = prow
        (ini, cc, ccd, rbs, ld, lw, bs, lt, mm, ve, se) = [al.sem(n) for n in
            ("ini", "cc", "ccd", "rbs", "ld", "lw", "bs", "lt", "mm", "ve", "se")]
        lh = [al.sem(f"lh6{j}") for j in range(2)]
        lx = [al.sem(f"lx6{j}") for j in range(2)]
        st = [al.sem(f"st6{j}") for j in range(2)]
        block = al.block()

        @block.gpsimd
        def _(g):
            g.memset(onesf[:, :], 1.0).then_inc(ini, 1)
            g.dma_start(out=cc_in.ap()[:, :], in_=hmax.ap()[:, :]).then_inc(cc, 16)
            g.wait_ge(cc, 16)
            g.collective_compute("AllReduce", ALU.max, replica_groups=[list(range(NCORES))],
                                 ins=[cc_in.ap().opt()], outs=[cc_out.ap().opt()]).then_inc(ccd, 1)
            g.wait_ge(ccd, 1)
            g.dma_start(out=mx_sb[:, :], in_=cc_out.ap()[:, :]).then_inc(ld, 16)
            g.wait_ge(ld, 16)
            g.dma_start(out=idn_sb[:, :], in_=idn.ap()[:, :]).then_inc(ld, 16)
            g.wait_ge(ld, 32)
            g.dma_start(out=b2pt_sb[:, :], in_=b2pt.ap()[:, :]).then_inc(ld, 16)
            for wc in range(NWC):
                if wc >= 1:
                    g.wait_ge(ve, V0 + wc * V_W2)
                g.dma_start(out=wna[:, :],
                            in_=w2.ap()[wc * 128:(wc + 1) * 128, :]).then_inc(lw, 16)
                g.wait_ge(ve, V0 + wc * V_W2 + 9)
                g.dma_start(out=wqb.ap()[wc * 128:(wc + 1) * 128, :],
                            in_=wbf[:, :]).then_inc(bs, 16)
            g.wait_ge(ve, v_pre - 2)
            g.dma_start(out=rowb.ap()[:, :], in_=combT[:, :]).then_inc(rbs, 16)
            g.wait_ge(rbs, 16)
            g.dma_start(out=rows2[0:1, :], in_=ap_of(rowb, 0, [[2048, 1], [1, 2048]])).then_inc(rbs, 16)
            for tc in range(NTC):
                if tc >= 2:
                    g.wait_ge(ve, vq_done[tc - 1])
                g.dma_start(out=hstrip[tc % 2][:, :].rearrange("p (c n) -> p c n", c=32),
                            in_=ap_of(h_s, tc * 128, [[NTOK, 128], [128 * NTOK, 32], [1, 128]])).then_inc(lh[tc % 2], 16)
                if tc >= 2:
                    g.wait_ge(ve, vln[tc - 2][0] + 3)
                b, t_ = tc // (NTC // BPC), tc % (NTC // BPC)
                g.dma_start(out=x1t[tc % 2][:, :],
                            in_=ap_of(x1_s, b * L * D + t_ * 128 * D, [[D, 128], [1, D]])).then_inc(lx[tc % 2], 16)

        @block.sync
        def _(sy):
            sy.wait_ge(bs, 16 * NWC)
            for fc in range(32):
                dst = sb_ap(w2T_sb, fc * D, [[1, D]])
                sy.dma_start(out=dst, in_=wqb.ap()[:, fc * 128:(fc + 1) * 128],
                             transpose=True).then_inc(lt, 16)
            for tc in range(NTC):
                b, t_ = tc // (NTC // BPC), tc % (NTC // BPC)
                sy.wait_ge(se, 4 * tc + 4)
                sy.dma_start(out=ap_of(x2_s, b * L * D + t_ * 128 * D, [[D, 128], [1, D]]),
                             in_=xo[tc % 2][:, :]).then_inc(st[tc % 2], 16)
            sy.wait_ge(st[0], 16 * (NTC // 2))
            sy.wait_ge(st[1], 16 * (NTC // 2))

        @block.tensor
        def _(t):
            t.wait_ge(ini, 1)
            t.wait_ge(ld, 48)
            t.transpose(pscr[0:1, 0:128], mx_sb[:, 0:1], idn_sb[:, :]).then_inc(mm, 1)
            t.wait_ge(ve, 3)
            t.matmul(pscr[:, 0:1], onesf[0:1, :], sc_sb[0:1, 1:2], start=True, stop=True).then_inc(mm, 1)
            t.wait_ge(ve, v_pre - 3 + 0)
            t.transpose(pscr[0:16, 0:128], comb[:, 0:16], idn_sb[:, :]).then_inc(mm, 1)
            t.wait_ge(rbs, 32)
            t.matmul(prow[:, 0:512], onesf[0:1, :], rows2[0:1, 0:512], start=True, stop=True)
            t.matmul(prow[:, 512:1024], onesf[0:1, :], rows2[0:1, 512:1024], start=True, stop=True).then_inc(mm, 1)
            t.wait_ge(ve, v_pre - 1)
            t.matmul(prow[:, 0:512], onesf[0:1, :], rows2[0:1, 1024:1536], start=True, stop=True)
            t.matmul(prow[:, 512:1024], onesf[0:1, :], rows2[0:1, 1536:2048], start=True, stop=True).then_inc(mm, 1)
            t.wait_ge(lt, 16 * 32)
            for tc in range(NTC):
                t.wait_ge(ve, vq_done[tc])
                if tc >= 2:
                    t.wait_ge(ve, vln[tc - 2][0] + 1)
                last = None
                for fc in range(32):
                    lhsT = sb_ap(hq[tc % 2], fc * 128, [[1, 128]])
                    for cchunk in range(2):
                        last = t.matmul(po[tc % 2][:, cchunk * 512:(cchunk + 1) * 512], lhsT,
                                        sb_ap(w2T_sb, fc * D + cchunk * 512, [[1, 512]]),
                                        start=(fc == 0), stop=(fc == 31))
                last.then_inc(mm, 1)

        @block.vector
        def _(v):
            v = AutoWait(v, ve)
            v.wait_ge(mm, 1)
            v.tensor_copy(row_sb[0:1, :], pscr[0:1, 0:128]).then_inc(ve, 1)
            v.wait_ge(ve, 1)
            v.reduce_max(sc_sb[0:1, 0:1], row_sb[0:1, :], axis=AXX).then_inc(ve, 1)
            v.wait_ge(ve, 2)
            v.tensor_scalar(sc_sb[0:1, 1:2], sc_sb[0:1, 0:1], EPS_Q, 1.0 / 255.0,
                            ALU.max, ALU.mult).then_inc(ve, 1)
            v.wait_ge(mm, 2)
            v.tensor_copy(s2pp[:, :], pscr[:, 0:1]).then_inc(ve, 1)
            v.wait_ge(ve, 4)
            v.reciprocal(rs2pp[:, :], s2pp[:, :]).then_inc(ve, 1)
            for wc in range(NWC):
                vb = V0 + wc * V_W2
                v.wait_ge(lw, 16 * (wc + 1))
                v.reduce_max(swcol[:, wc:wc + 1], wna[:, :], axis=AXX,
                             apply_absolute_value=True).then_inc(ve, 1)
                v.wait_ge(ve, vb + 1)
                v.tensor_scalar(swcol[:, wc:wc + 1], swcol[:, wc:wc + 1], EPS_Q, 1.0 / 127.0,
                                ALU.max, ALU.mult).then_inc(ve, 1)
                v.wait_ge(ve, vb + 2)
                v.reciprocal(rsw[:, :], swcol[:, wc:wc + 1]).then_inc(ve, 1)
                for hh in range(2):
                    o = hh * (F // 2)
                    v.wait_ge(ve, vb + 3 + 3 * hh)
                    v.tensor_scalar(wdiv[:, :], wna[:, o:o + F // 2], rsw[:, 0:1], None,
                                    ALU.mult).then_inc(ve, 1)
                    v.wait_ge(ve, vb + 4 + 3 * hh)
                    v.tensor_copy(wi32[:, :], wdiv[:, :]).then_inc(ve, 1)
                    v.wait_ge(ve, vb + 5 + 3 * hh)
                    if wc >= 1 and hh == 0:
                        v.wait_ge(bs, 16 * wc)
                    v.tensor_copy(wbf[:, o:o + F // 2], wi32[:, :]).then_inc(ve, 1)
            vb = V0 + NWC * V_W2
            v.wait_ge(ld, 48)
            v.tensor_tensor(s2sw[:, :], swcol[:, :], sb_ap(s2pp, 0, [[0, 8]]), ALU.mult).then_inc(ve, 1)
            v.wait_ge(ve, vb + 1)
            v.reciprocal(rs2sw[:, :], s2sw[:, :]).then_inc(ve, 1)
            v.wait_ge(ve, vb + 2)
            v.tensor_tensor(bscr[:, :], b2pt_sb[:, :], rs2sw[:, :], ALU.mult).then_inc(ve, 1)
            v.wait_ge(ve, vb + 3)
            v.tensor_copy(bi32[:, :], bscr[:, :]).then_inc(ve, 1)
            v.wait_ge(ve, vb + 4)
            v.tensor_copy(bscr[:, :], bi32[:, :]).then_inc(ve, 1)
            v.wait_ge(ve, vb + 5)
            v.tensor_tensor(bq2[:, :], bscr[:, :], s2sw[:, :], ALU.mult).then_inc(ve, 1)
            v.wait_ge(ve, vb + 6)
            v.tensor_copy(comb[:, 0:8], s2sw[:, :]).then_inc(ve, 1)
            v.wait_ge(ve, vb + 7)
            v.tensor_copy(comb[:, 8:16], bq2[:, :]).then_inc(ve, 1)
            v.wait_ge(mm, 3)
            v.tensor_copy(combT[:, :], pscr[0:16, 0:128]).then_inc(ve, 1)
            v.wait_ge(mm, 4)
            v.tensor_copy(scale2rep[:, :], prow[:, :]).then_inc(ve, 1)
            v.wait_ge(mm, 5)
            v.tensor_copy(bq2rep[:, :], prow[:, :]).then_inc(ve, 1)
            def q_ops(tc):
                vbq = vq_done[tc] - VQ
                v.wait_ge(lh[tc % 2], 16 * (tc // 2 + 1))
                for hh in range(2):
                    o = hh * (F // 2)
                    vbh = vbq + 4 * hh
                    if hh:
                        v.wait_ge(ve, vbh)
                    v.tensor_scalar(hscr[:, :], sb_ap(hstrip[tc % 2], o, [[1, F // 2]]),
                                    rs2pp[:, 0:1], None, ALU.mult).then_inc(ve, 1)
                    v.wait_ge(ve, vbh + 1)
                    v.tensor_copy(hi32[:, :], hscr[:, :]).then_inc(ve, 1)
                    v.wait_ge(ve, vbh + 2)
                    v.tensor_copy(hscr[:, :], hi32[:, :]).then_inc(ve, 1)
                    v.wait_ge(ve, vbh + 3)
                    if tc >= 2 and hh == 0:
                        v.wait_ge(mm, 5 + tc - 1)
                    v.tensor_scalar(sb_ap(hq[tc % 2], o, [[1, F // 2]]), hscr[:, :], 255.0, 0.0,
                                    ALU.min, ALU.max).then_inc(ve, 1)

            def ln_ops(tc):
                vb0 = vln[tc][0]
                v.wait_ge(mm, 5 + tc + 1)
                v.tensor_tensor(t1_[tc % 2][:, :], po[tc % 2][:, :], scale2rep[:, :],
                                ALU.mult).then_inc(ve, 1)
                v.wait_ge(ve, vb0 + 1)
                v.tensor_add(t2_[:, :], t1_[tc % 2][:, :], bq2rep[:, :]).then_inc(ve, 1)
                v.wait_ge(ve, vb0 + 2)
                v.wait_ge(lx[tc % 2], 16 * (tc // 2 + 1))
                v.tensor_add(t1_[tc % 2][:, :], t2_[:, :], x1t[tc % 2][:, :]).then_inc(ve, 1)
                v.wait_ge(ve, vb0 + 3)
                v.reduce_sum(srx[:, 0:1], t1_[tc % 2][:, :], axis=AXX).then_inc(ve, 1)
                v.wait_ge(ve, vb0 + 4)
                v.wait_ge(ve, vb0 + 4)
                v.tensor_scalar_mul(srx[:, 2:3], srx[:, 0:1], 1.0 / D).then_inc(ve, 1)
                v.wait_ge(ve, vb0 + 5)
                v.tensor_mul(srx[:, 3:4], srx[:, 2:3], srx[:, 2:3]).then_inc(ve, 1)
                v.wait_ge(se, 4 * tc + 1)
                v.tensor_scalar_mul(srx[:, 4:5], srx[:, 1:2], 1.0 / D).then_inc(ve, 1)
                v.wait_ge(ve, vb0 + 7)
                v.tensor_sub(srx[:, 5:6], srx[:, 4:5], srx[:, 3:4]).then_inc(ve, 1)
                v.wait_ge(ve, vb0 + 8)
                v.tensor_scalar_add(srx[:, 6:7], srx[:, 5:6], 1e-5).then_inc(ve, 1)
                v.wait_ge(se, 4 * tc + 3)
                v.tensor_mul(srx[:, 8:9], srx[:, 7:8], srx[:, 2:3]).then_inc(ve, 1)
                v.wait_ge(ve, vb0 + 10)
                v.tensor_scalar_mul(srx[:, 9:10], srx[:, 8:9], -1.0).then_inc(ve, 1)

            q_ops(0)
            for tc in range(NTC):
                if tc + 1 < NTC:
                    q_ops(tc + 1)
                ln_ops(tc)

        @block.scalar
        def _(s):
            s = AutoWait(s, se)
            for tc in range(NTC):
                vb0 = vln[tc][0]
                s.wait_ge(ve, vb0 + 3)
                s.activation(trash[:, :], t1_[tc % 2][:, :], AF.Square,
                             accum_out=srx[:, 1:2]).then_inc(se, 1)
                s.wait_ge(ve, vb0 + 9)
                s.activation(srx[:, 7:8], srx[:, 6:7], AF.Ln).then_inc(se, 1)
                s.wait_ge(se, 4 * tc + 2)
                s.activation(srx[:, 7:8], srx[:, 7:8], AF.Exp, scale=-0.5).then_inc(se, 1)
                s.wait_ge(ve, vb0 + 11)
                if tc >= 2:
                    s.wait_ge(st[tc % 2], 16 * (tc // 2))
                s.activation(xo[tc % 2][:, :], t1_[tc % 2][:, :], AF.Identity,
                             scale=srx[:, 7:8], bias=srx[:, 9:10]).then_inc(se, 1)


def build_p6():
    return _standalone(emit_p6)



def emit_p7(fx):
    x2_s = fx.dram("x2_s", [BPC, L, D], BF16, kind="ExternalInput")
    wT = fx.dram("qkvw1T", [D, 3 * D], F32, kind="ExternalInput")
    bpt = fx.dram("qkvb1_pt", [128, 16], F32, kind="ExternalInput")
    bvrow = fx.dram("bv1_row", [1, D], F32, kind="ExternalInput")
    qk1_s = fx.dram("qk1_s", [2 * D, NTOK], BF16, kind="ExternalOutput")
    v1_s = fx.dram("v1_s", [BPC, L, D], BF16, kind="ExternalOutput")

    with fx.phase() as al:
        x2T = al.sb("x2T", [128, 8 * NTOK], BF16)
        w_sb = al.sb("w7", [128, 8 * 3 * D], BF16)
        bpt_sb = al.sb("bpt7", [128, 16], F32)
        bvrow_sb = al.sb("bvrow7", [1, D], BF16)
        ones_sb = al.sb("ones7", [1, 128], BF16)
        qe = [al.sb(f"qe{j}", [128, 512], BF16) for j in range(4)]
        vev = [al.sb(f"vev{j}", [128, D], BF16) for j in range(2)]
        pq = [al.ps(f"pq{j}", [128, 512]) for j in range(4)]
        pv = [al.ps(f"pv{j}", [128, D]) for j in range(2)]
        ini, ld, lt, mm, se = [al.sem(n) for n in ("ini", "ld", "lt", "mm", "se")]
        stq = [al.sem(f"stq{j}") for j in range(4)]
        stv = [al.sem(f"stv{j}") for j in range(2)]
        block = al.block()
        NDOC = 16

        @block.gpsimd
        def _(g):
            g.memset(ones_sb[:, :], 1.0).then_inc(ini, 1)
            src = ap_of(wT, 0, [[3 * D, 128], [128 * 3 * D, 8], [1, 3 * D]])
            g.dma_start(out=w_sb[:, :].rearrange("p (c n) -> p c n", c=8), in_=src).then_inc(ld, 16)
            g.wait_ge(ld, 16)
            g.dma_start(out=bpt_sb[:, :], in_=bpt.ap()[:, :]).then_inc(ld, 16)
            g.wait_ge(ld, 32)
            g.dma_start(out=bvrow_sb[0:1, :], in_=bvrow.ap()[0:1, :]).then_inc(ld, 16)

        @block.sync
        def _(sy):
            for b in range(BPC):
                for dc in range(8):
                    dst = sb_ap(x2T, dc * NTOK + b * L, [[1, L]])
                    sy.dma_start(out=dst, in_=x2_s.ap()[b, :, dc * 128:(dc + 1) * 128],
                                 transpose=True).then_inc(lt, 16)
            nse = 0
            for doc in range(NDOC):
                for half in range(2):
                    for q4 in range(4):
                        nse += 1
                        sy.wait_ge(se, nse)
                        col = half * 2048 + q4 * 512
                        sy.dma_start(out=ap_of(qk1_s, doc * 128 * NTOK + col, [[NTOK, 128], [1, 512]]),
                                     in_=qe[q4][:, :]).then_inc(stq[q4], 16)
            for i in range(32):
                nse += 1
                sy.wait_ge(se, nse)
                b, tc = i // 16, i % 16
                sy.dma_start(out=ap_of(v1_s, b * L * D + tc * 128 * D, [[D, 128], [1, D]]),
                             in_=vev[i % 2][:, :]).then_inc(stv[i % 2], 16)
            for j in range(4):
                sy.wait_ge(stq[j], 16 * 32)
            sy.wait_ge(stv[0], 16 * 16)
            sy.wait_ge(stv[1], 16 * 16)

        @block.tensor
        def _(t):
            t.wait_ge(ini, 1)
            t.wait_ge(ld, 48)
            t.wait_ge(lt, 16 * 16)
            nmm = 0
            for doc in range(NDOC):
                for half in range(2):
                    for dc in range(8):
                        lhsT = sb_ap(w_sb, dc * 3 * D + doc * 128, [[1, 128]])
                        for q4 in range(4):
                            n = doc * 2 + half
                            if dc == 0 and n >= 1:
                                t.wait_ge(se, (n - 1) * 4 + q4 + 1)
                            last = t.matmul(pq[q4][:, :], lhsT,
                                            sb_ap(x2T, dc * NTOK + half * 2048 + q4 * 512, [[1, 512]]),
                                            start=(dc == 0), stop=(dc == 7))
                    nmm += 1
                    last.then_inc(mm, 1)
            for i in range(32):
                b, tc = i // 16, i % 16
                if i >= 2:
                    t.wait_ge(se, 128 + (i - 1))
                p = pv[i % 2]
                t.matmul(p[:, 0:512], ones_sb[0:1, :], bvrow_sb[0:1, 0:512], start=True, stop=False)
                t.matmul(p[:, 512:1024], ones_sb[0:1, :], bvrow_sb[0:1, 512:1024], start=True, stop=False)
                for dc in range(8):
                    lhsT = sb_ap(x2T, dc * NTOK + b * L + tc * 128, [[1, 128]])
                    for cchunk in range(2):
                        last = t.matmul(p[:, cchunk * 512:(cchunk + 1) * 512], lhsT,
                                        sb_ap(w_sb, dc * 3 * D + 2 * D + cchunk * 512, [[1, 512]]),
                                        start=False, stop=(dc == 7))
                last.then_inc(mm, 1)

        @block.scalar
        def _(s):
            s = AutoWait(s, se)
            nse = 0
            for doc in range(NDOC):
                for half in range(2):
                    n = doc * 2 + half
                    s.wait_ge(mm, n + 1)
                    n2 = doc * 2 + half
                    for q4 in range(4):
                        nse += 1
                        if n2 >= 1:
                            s.wait_ge(stq[q4], 16 * n2)
                        s.activation(qe[q4][:, :], pq[q4][:, :], AF.Identity,
                                     bias=bpt_sb[:, doc:doc + 1]).then_inc(se, 1)
            for i in range(32):
                s.wait_ge(mm, 32 + i + 1)
                nse += 1
                if i >= 2:
                    s.wait_ge(stv[i % 2], 16 * (i // 2))
                s.copy(vev[i % 2][:, :], pv[i % 2][:, :]).then_inc(se, 1)


def build_p7():
    return _standalone(emit_p7)



def emit_p9(fx):
    oT_s = fx.dram("oT_s", [D, BPC * 512], BF16, kind="ExternalInput")
    pT = fx.dram("projT", [D, D], F32, kind="ExternalInput")
    pbrow = fx.dram("projb_row", [1, D], F32, kind="ExternalInput")
    pbrep = fx.dram("projb_rep", [128, D], F32, kind="ExternalInput")
    x2_s = fx.dram("x2_s", [BPC, L, D], BF16, kind="ExternalInput")
    x3_s = fx.dram("x3_s", [BPC, L, D], BF16, kind="ExternalOutput")

    NT = L // 128
    chunks = [(b, tc) for b in range(BPC) for tc in range(NT)]
    NCH = len(chunks)
    VPC, SPC = 9, 4

    with fx.phase() as al:
        ot_sb = al.sb("ot_sb", [128, 8 * BPC * 512], BF16)
        w_sb = al.sb("w9", [128, 8 * D], BF16)
        pbrow_sb = al.sb("pbrow9", [1, D], BF16)
        pbrep_sb = al.sb("pbrep9", [128, D], F32)
        ones_sb = al.sb("ones9", [1, 128], BF16)
        x2t = [al.sb(f"x2t9{j}", [128, D], BF16) for j in range(2)]
        t1_ = [al.sb(f"t9a{j}", [128, D], F32) for j in range(2)]
        xo = [al.sb(f"xo9{j}", [128, D], BF16) for j in range(2)]
        srx = al.sb("srx9", [128, 16], F32)
        pm = [al.ps(f"pm9{j}", [128, D]) for j in range(2)]
        trash = al.ps("trash9", [128, D])
        ini, ld, mm, ve, se = [al.sem(n) for n in ("ini", "ld", "mm", "ve", "se")]
        lx = [al.sem(f"lx9{j}") for j in range(2)]
        st = [al.sem(f"st9{j}") for j in range(2)]
        block = al.block()
        is_mm = [tc >= 12 for (b, tc) in chunks]
        mm_idx = np.cumsum([1 if m else 0 for m in is_mm]).tolist()

        @block.gpsimd
        def _(g):
            g.memset(ones_sb[:, :], 1.0).then_inc(ini, 1)
            src = ap_of(pT, 0, [[D, 128], [128 * D, 8], [1, D]])
            g.dma_start(out=w_sb[:, :].rearrange("p (c n) -> p c n", c=8), in_=src).then_inc(ld, 16)
            g.wait_ge(ld, 16)
            g.dma_start(out=pbrow_sb[0:1, :], in_=pbrow.ap()[0:1, :]).then_inc(ld, 16)
            g.wait_ge(ld, 32)
            g.dma_start(out=pbrep_sb[:, :], in_=pbrep.ap()[:, :]).then_inc(ld, 16)
            g.wait_ge(ld, 48)
            src = ap_of(oT_s, 0, [[BPC * 512, 128], [128 * BPC * 512, 8], [1, BPC * 512]])
            g.dma_start(out=ot_sb[:, :].rearrange("p (c n) -> p c n", c=8), in_=src).then_inc(ld, 16)
            for i, (b, tc) in enumerate(chunks):
                if i >= 2:
                    g.wait_ge(ve, VPC * (i - 2) + 1)
                g.dma_start(out=x2t[i % 2][:, :],
                            in_=ap_of(x2_s, b * L * D + tc * 128 * D, [[D, 128], [1, D]])).then_inc(lx[i % 2], 16)

        @block.tensor
        def _(t):
            t.wait_ge(ini, 1)
            t.wait_ge(ld, 64)
            for i, (b, tc) in enumerate(chunks):
                if not is_mm[i]:
                    continue
                if i >= 2:
                    t.wait_ge(ve, VPC * (i - 2) + 1)
                p = pm[i % 2]
                t.matmul(p[:, 0:512], ones_sb[0:1, :], pbrow_sb[0:1, 0:512], start=True, stop=False)
                t.matmul(p[:, 512:1024], ones_sb[0:1, :], pbrow_sb[0:1, 512:1024], start=True, stop=False)
                for dc in range(8):
                    col = b * 512 + (tc - 12) * 128
                    lhsT = sb_ap(ot_sb, dc * BPC * 512 + col, [[1, 128]])
                    for cchunk in range(2):
                        last = t.matmul(p[:, cchunk * 512:(cchunk + 1) * 512], lhsT,
                                        sb_ap(w_sb, dc * D + cchunk * 512, [[1, 512]]),
                                        start=False, stop=(dc == 7))
                last.then_inc(mm, 1)

        @block.vector
        def _(v):
            v = AutoWait(v, ve)
            v.wait_ge(ld, 64)
            for i in range(NCH):
                vb, sb_ = VPC * i, SPC * i
                v.wait_ge(lx[i % 2], 16 * (i // 2 + 1))
                if i >= 1:
                    v.wait_ge(se, SPC * i)
                t1 = t1_[i % 2]
                if is_mm[i]:
                    v.wait_ge(mm, mm_idx[i])
                    v.tensor_add(t1[:, :], pm[i % 2][:, :], x2t[i % 2][:, :]).then_inc(ve, 1)
                else:
                    v.tensor_add(t1[:, :], pbrep_sb[:, :], x2t[i % 2][:, :]).then_inc(ve, 1)
                v.wait_ge(ve, vb + 1)
                v.reduce_sum(srx[:, 0:1], t1[:, :], axis=AXX).then_inc(ve, 1)
                v.wait_ge(ve, vb + 2)
                v.tensor_scalar_mul(srx[:, 2:3], srx[:, 0:1], 1.0 / D).then_inc(ve, 1)
                v.wait_ge(ve, vb + 3)
                v.tensor_mul(srx[:, 3:4], srx[:, 2:3], srx[:, 2:3]).then_inc(ve, 1)
                v.wait_ge(se, sb_ + 1)
                v.tensor_scalar_mul(srx[:, 4:5], srx[:, 1:2], 1.0 / D).then_inc(ve, 1)
                v.wait_ge(ve, vb + 5)
                v.tensor_sub(srx[:, 5:6], srx[:, 4:5], srx[:, 3:4]).then_inc(ve, 1)
                v.wait_ge(ve, vb + 6)
                v.tensor_scalar_add(srx[:, 6:7], srx[:, 5:6], 1e-5).then_inc(ve, 1)
                v.wait_ge(se, sb_ + 3)
                v.tensor_mul(srx[:, 8:9], srx[:, 7:8], srx[:, 2:3]).then_inc(ve, 1)
                v.wait_ge(ve, vb + 8)
                v.tensor_scalar_mul(srx[:, 9:10], srx[:, 8:9], -1.0).then_inc(ve, 1)

        @block.scalar
        def _(s):
            s = AutoWait(s, se)
            for i in range(NCH):
                vb, sb_ = VPC * i, SPC * i
                s.wait_ge(ve, vb + 1)
                s.activation(trash[:, :], t1_[i % 2][:, :], AF.Square,
                             accum_out=srx[:, 1:2]).then_inc(se, 1)
                s.wait_ge(ve, vb + 7)
                s.activation(srx[:, 7:8], srx[:, 6:7], AF.Ln).then_inc(se, 1)
                s.wait_ge(se, sb_ + 2)
                s.activation(srx[:, 7:8], srx[:, 7:8], AF.Exp, scale=-0.5).then_inc(se, 1)
                s.wait_ge(ve, vb + 9)
                if i >= 2:
                    s.wait_ge(st[i % 2], 16 * (i // 2))
                s.activation(xo[i % 2][:, :], t1_[i % 2][:, :], AF.Identity,
                             scale=srx[:, 7:8], bias=srx[:, 9:10]).then_inc(se, 1)

        @block.sync
        def _(sy):
            for i, (b, tc) in enumerate(chunks):
                sy.wait_ge(se, SPC * i + 4)
                sy.dma_start(out=ap_of(x3_s, b * L * D + tc * 128 * D, [[D, 128], [1, D]]),
                             in_=xo[i % 2][:, :]).then_inc(st[i % 2], 16)
            sy.wait_ge(st[0], 16 * (NCH // 2))
            sy.wait_ge(st[1], 16 * (NCH // 2))


def build_p9():
    return _standalone(emit_p9)



def emit_p10(fx):
    x3_s = fx.dram("x3_s", [BPC, L, D], BF16, kind="ExternalInput")
    w1T = fx.dram("w1T_1", [D, 4 * D], F32, kind="ExternalInput")
    w1n = fx.dram("w1_1", [4 * D, D], F32, kind="ExternalInput")
    b1pt = fx.dram("b1_1pt", [128, 32], F32, kind="ExternalInput")
    w2T = fx.dram("w2T_1", [4 * D, D], F32, kind="ExternalInput")
    w2n = fx.dram("w2_1", [D, 4 * D], F32, kind="ExternalInput")
    b2rep = fx.dram("b2_1rep", [128, D], F32, kind="ExternalInput")
    idn = fx.dram("idn_f32", [128, 128], F32, kind="ExternalInput")
    out_s = fx.dram("out_s", [BPC, L, D], F32, kind="ExternalOutput")
    rowb = fx.dram("rowb10", [8, 128], F32)
    F = 4 * D
    TT_ = 256
    NTT = NTOK // TT_

    v_cnt = 0
    va1 = {}
    for fc in range(32):
        va1[fc] = v_cnt + 2
        v_cnt += 2
    va2 = {}
    for wc in range(8):
        va2[wc] = v_cnt + 4
        v_cnt += 4
    v_a2T = v_cnt + 1
    v_a2rep = v_cnt + 2
    v_cnt += 2
    VLN = 11
    vln = {}
    for n in range(NTT * 2):
        vln[n] = v_cnt
        v_cnt += VLN
    s_sign = 48

    def gel_cnt(gidx):
        ti, fc = gidx // 32, gidx % 32
        return s_sign + ti * 40 + fc + 1

    sgel = {ti: s_sign + ti * 40 + 32 for ti in range(NTT)}
    sln = {}
    for ti in range(NTT):
        for sub in range(2):
            sln[ti * 2 + sub] = s_sign + ti * 40 + 32 + sub * 4
    mm_t = {ti: 2 + ti * 34 for ti in range(NTT)}

    with fx.phase() as al:
        w1s = al.sb("w1s", [128, 8 * F], BF16)
        w2s = al.sb("w2s", [128, 32 * D], BF16)
        wld = [al.sb(f"wld10{j}", [128, 2048], F32) for j in range(2)]
        a1col = al.sb("a1col", [128, 32], F32)
        a2col = al.sb("a2col", [128, 8], F32)
        a2colT = al.sb("a2colT", [8, 128], F32)
        a2row = al.sb("a2row", [1, D], F32)
        a2rep = al.sb("a2rep", [128, D], F32)
        b1pt_sb = al.sb("b1pt10", [128, 32], F32)
        b2rep_sb = al.sb("b2rep10", [128, D], F32)
        idn_sb = al.sb("idn10", [128, 128], F32)
        onesf = al.sb("onesf10", [1, 128], F32)
        x3T = [al.sb(f"x3T{j}", [128, 8 * TT_], BF16) for j in range(2)]
        hstrip = al.sb("hst10", [128, 32 * TT_], BF16)
        x3t = [al.sb(f"x3t10{j}", [128, D], BF16) for j in range(2)]
        t1_ = [al.sb(f"t10a{j}", [128, D], F32) for j in range(2)]
        xo = [al.sb(f"xo10{j}", [128, D], F32) for j in range(2)]
        srx = al.sb("srx10", [128, 16], F32)
        strash = al.sb("strash10", [128, D], F32)
        pscr = al.ps("pscr10", [128, 128])
        ph = [al.ps(f"ph10{j}", [128, TT_]) for j in range(2)]
        po = [al.ps(f"po10{j}", [128, D]) for j in range(2)]
        (ini, ld, mm, ve, se, rb) = [al.sem(n) for n in
            ("ini", "ld", "mm", "ve", "se", "rb")]
        lw = [al.sem(f"lw10{j}") for j in range(2)]
        lt = [al.sem(f"lt10{j}") for j in range(2)]
        lx = [al.sem(f"lx10{j}") for j in range(2)]
        st = [al.sem(f"st10{j}") for j in range(2)]
        block = al.block()

        @block.gpsimd
        def _(g):
            g.memset(onesf[:, :], 1.0).then_inc(ini, 1)
            g.dma_start(out=idn_sb[:, :], in_=idn.ap()[:, :]).then_inc(ld, 16)
            g.wait_ge(ld, 16)
            g.dma_start(out=b1pt_sb[:, :], in_=b1pt.ap()[:, :]).then_inc(ld, 16)
            g.wait_ge(ld, 32)
            g.dma_start(out=b2rep_sb[:, :], in_=b2rep.ap()[:, :]).then_inc(ld, 16)
            for hdc in range(16):
                if hdc >= 2:
                    g.wait_ge(se, hdc - 1)
                dc, hh = hdc // 2, hdc % 2
                g.dma_start(out=wld[hdc % 2][:, :], in_=w1T.ap()[dc * 128:(dc + 1) * 128,
                            hh * 2048:(hh + 1) * 2048]).then_inc(lw[hdc % 2], 16)
            for fc in range(32):
                g.wait_ge(se, 15 + fc)
                g.dma_start(out=wld[fc % 2][:, 0:D],
                            in_=w2T.ap()[fc * 128:(fc + 1) * 128, :]).then_inc(lw[fc % 2], 16)
            for fc in range(32):
                if fc < 2:
                    g.wait_ge(se, 48)
                else:
                    g.wait_ge(ve, va1[fc - 2])
                g.dma_start(out=wld[fc % 2][:, D:2 * D],
                            in_=w1n.ap()[fc * 128:(fc + 1) * 128, :]).then_inc(lw[fc % 2], 16)
            for hwc in range(16):
                wc, hh = hwc // 2, hwc % 2
                if hwc < 2:
                    g.wait_ge(ve, va1[30 + hwc])
                else:
                    pwc, phh = (hwc - 2) // 2, (hwc - 2) % 2
                    g.wait_ge(ve, va2[pwc] - 3 + phh)
                g.dma_start(out=wld[hwc % 2][:, :], in_=w2n.ap()[wc * 128:(wc + 1) * 128,
                            hh * 2048:(hh + 1) * 2048]).then_inc(lw[hwc % 2], 16)
            g.wait_ge(ve, v_a2T)
            g.dma_start(out=rowb.ap()[:, :], in_=a2colT[:, :]).then_inc(rb, 16)
            g.wait_ge(rb, 16)
            g.dma_start(out=a2row[0:1, :], in_=ap_of(rowb, 0, [[D, 1], [1, D]])).then_inc(rb, 16)
            def emit_store(n):
                g.wait_ge(se, sln[n] + 4)
                tok = n * 128
                b, t_ = tok // L, tok % L
                g.dma_start(out=ap_of(out_s, b * L * D + t_ * D, [[D, 128], [1, D]]),
                            in_=xo[n % 2][:, :]).then_inc(st[n % 2], 16)
            for n in range(NTT * 2):
                if n >= 2:
                    g.wait_ge(ve, vln[n - 2] + 3)
                tok = n * 128
                b, t_ = tok // L, tok % L
                g.dma_start(out=x3t[n % 2][:, :],
                            in_=ap_of(x3_s, b * L * D + t_ * D, [[D, 128], [1, D]])).then_inc(lx[n % 2], 16)
                if n >= 1:
                    emit_store(n - 1)
            emit_store(NTT * 2 - 1)
            g.wait_ge(st[0], 16 * NTT)
            g.wait_ge(st[1], 16 * NTT)

        @block.sync
        def _(sy):
            for ti in range(NTT):
                tok = ti * TT_
                b, t_ = tok // L, tok % L
                if ti >= 2:
                    sy.wait_ge(mm, mm_t[ti - 2] + 32)
                for dc in range(8):
                    dst = sb_ap(x3T[ti % 2], dc * TT_, [[1, TT_]])
                    sy.dma_start(out=dst, in_=x3_s.ap()[b, t_:t_ + TT_, dc * 128:(dc + 1) * 128],
                                 transpose=True).then_inc(lt[ti % 2], 16)


        @block.tensor
        def _(t):
            t.wait_ge(ini, 1)
            t.wait_ge(ld, 48)
            t.wait_ge(ve, va2[7])
            t.transpose(pscr[0:8, 0:128], a2col[:, 0:8], idn_sb[:, :]).then_inc(mm, 1)
            t.wait_ge(rb, 32)
            t.matmul(po[0][:, 0:512], onesf[0:1, :], a2row[0:1, 0:512], start=True, stop=True)
            t.matmul(po[0][:, 512:1024], onesf[0:1, :], a2row[0:1, 512:1024], start=True, stop=True).then_inc(mm, 1)
            t.wait_ge(se, s_sign)
            t.wait_ge(ve, v_a2rep)
            for ti in range(NTT):
                t.wait_ge(lt[ti % 2], 16 * 8 * (ti // 2 + 1))
                for fc in range(32):
                    gidx = ti * 32 + fc
                    if gidx >= 2:
                        t.wait_ge(se, gel_cnt(gidx - 2))
                    last = None
                    for dc in range(8):
                        last = t.matmul(ph[fc % 2][:, :],
                                        sb_ap(w1s, dc * F + fc * 128, [[1, 128]]),
                                        sb_ap(x3T[ti % 2], dc * TT_, [[1, TT_]]),
                                        start=(dc == 0), stop=(dc == 7))
                    last.then_inc(mm, 1)
                for sub in range(2):
                    n = ti * 2 + sub
                    t.wait_ge(se, sgel[ti])
                    if n >= 2:
                        t.wait_ge(ve, vln[n - 2] + 1)
                    last = None
                    for fc in range(32):
                        lhsT = sb_ap(hstrip, fc * TT_ + sub * 128, [[1, 128]])
                        for cchunk in range(2):
                            last = t.matmul(po[n % 2][:, cchunk * 512:(cchunk + 1) * 512], lhsT,
                                            sb_ap(w2s, fc * D + cchunk * 512, [[1, 512]]),
                                            start=(fc == 0), stop=(fc == 31))
                    last.then_inc(mm, 1)

        @block.vector
        def _(v):
            v = AutoWait(v, ve)
            for fc in range(32):
                v.wait_ge(lw[fc % 2], 16 * ((48 + fc) // 2 + 1))
                v.reduce_sum(a1col[:, fc:fc + 1], wld[fc % 2][:, D:2 * D], axis=AXX,
                             apply_absolute_value=True).then_inc(ve, 1)
                v.wait_ge(ve, va1[fc] - 1)
                v.tensor_scalar_mul(a1col[:, fc:fc + 1], a1col[:, fc:fc + 1], 1.0 / D).then_inc(ve, 1)
            for wc in range(8):
                v.wait_ge(lw[0], 16 * ((80 + 2 * wc) // 2 + 1))
                v.reduce_sum(srx[:, 10:11], wld[0][:, :], axis=AXX,
                             apply_absolute_value=True).then_inc(ve, 1)
                v.wait_ge(lw[1], 16 * ((81 + 2 * wc) // 2 + 1))
                v.wait_ge(ve, va2[wc] - 3)
                v.reduce_sum(srx[:, 11:12], wld[1][:, :], axis=AXX,
                             apply_absolute_value=True).then_inc(ve, 1)
                v.wait_ge(ve, va2[wc] - 2)
                v.tensor_add(a2col[:, wc:wc + 1], srx[:, 10:11], srx[:, 11:12]).then_inc(ve, 1)
                v.wait_ge(ve, va2[wc] - 1)
                v.tensor_scalar_mul(a2col[:, wc:wc + 1], a2col[:, wc:wc + 1], 1.0 / F).then_inc(ve, 1)
            v.wait_ge(mm, 1)
            v.tensor_copy(a2colT[:, :], pscr[0:8, 0:128]).then_inc(ve, 1)
            v.wait_ge(mm, 2)
            v.wait_ge(ld, 48)
            v.tensor_copy(a2rep[:, :], po[0][:, :]).then_inc(ve, 1)
            for n in range(NTT * 2):
                ti, sub = n // 2, n % 2
                vb = vln[n]
                sb_ = sln[n]
                v.wait_ge(mm, mm_t[ti] + 32 + sub + 1)
                v.wait_ge(lx[n % 2], 16 * (n // 2 + 1))
                if n >= 1:
                    v.wait_ge(se, sln[n - 1] + 4)
                t1 = t1_[n % 2]
                v.tensor_mul(t1[:, :], po[n % 2][:, :], a2rep[:, :]).then_inc(ve, 1)
                v.wait_ge(ve, vb + 1)
                v.tensor_add(strash[:, :], t1[:, :], b2rep_sb[:, :]).then_inc(ve, 1)
                v.wait_ge(ve, vb + 2)
                v.tensor_add(t1[:, :], strash[:, :], x3t[n % 2][:, :]).then_inc(ve, 1)
                v.wait_ge(ve, vb + 3)
                v.reduce_sum(srx[:, 0:1], t1[:, :], axis=AXX).then_inc(ve, 1)
                v.wait_ge(ve, vb + 4)
                v.tensor_scalar_mul(srx[:, 2:3], srx[:, 0:1], 1.0 / D).then_inc(ve, 1)
                v.wait_ge(ve, vb + 5)
                v.tensor_mul(srx[:, 3:4], srx[:, 2:3], srx[:, 2:3]).then_inc(ve, 1)
                v.wait_ge(se, sb_ + 1)
                v.tensor_scalar_mul(srx[:, 4:5], srx[:, 1:2], 1.0 / D).then_inc(ve, 1)
                v.wait_ge(ve, vb + 7)
                v.tensor_sub(srx[:, 5:6], srx[:, 4:5], srx[:, 3:4]).then_inc(ve, 1)
                v.wait_ge(ve, vb + 8)
                v.tensor_scalar_add(srx[:, 6:7], srx[:, 5:6], 1e-5).then_inc(ve, 1)
                v.wait_ge(se, sb_ + 3)
                v.tensor_mul(srx[:, 8:9], srx[:, 7:8], srx[:, 2:3]).then_inc(ve, 1)
                v.wait_ge(ve, vb + 10)
                v.tensor_scalar_mul(srx[:, 9:10], srx[:, 8:9], -1.0).then_inc(ve, 1)

        @block.scalar
        def _(s):
            s = AutoWait(s, se)
            for hdc in range(16):
                s.wait_ge(lw[hdc % 2], 16 * (hdc // 2 + 1))
                dc, hh = hdc // 2, hdc % 2
                s.activation(sb_ap(w1s, dc * F + hh * 2048, [[1, 2048]]), wld[hdc % 2][:, :],
                             AF.Sign).then_inc(se, 1)
            for fc in range(32):
                s.wait_ge(lw[fc % 2], 16 * ((16 + fc) // 2 + 1))
                s.activation(sb_ap(w2s, fc * D, [[1, D]]), wld[fc % 2][:, 0:D],
                             AF.Sign).then_inc(se, 1)
            s.wait_ge(ld, 48)
            for ti in range(NTT):
                for fc in range(32):
                    gidx = ti * 32 + fc
                    s.wait_ge(mm, mm_t[ti] + fc + 1)
                    if ti >= 1 and fc == 0:
                        s.wait_ge(mm, mm_t[ti - 1] + 34)
                    s.activation(sb_ap(hstrip, fc * TT_, [[1, TT_]]), ph[fc % 2][:, :],
                                 AF.Gelu, scale=a1col[:, fc:fc + 1],
                                 bias=b1pt_sb[:, fc:fc + 1]).then_inc(se, 1)
                for sub in range(2):
                    n = ti * 2 + sub
                    vb = vln[n]
                    sb_ = sln[n]
                    s.wait_ge(ve, vb + 3)
                    s.activation(strash[:, :], t1_[n % 2][:, :], AF.Square,
                                 accum_out=srx[:, 1:2]).then_inc(se, 1)
                    s.wait_ge(ve, vb + 9)
                    s.activation(srx[:, 7:8], srx[:, 6:7], AF.Ln).then_inc(se, 1)
                    s.wait_ge(se, sb_ + 2)
                    s.activation(srx[:, 7:8], srx[:, 7:8], AF.Exp, scale=-0.5).then_inc(se, 1)
                    s.wait_ge(ve, vb + 11)
                    if n >= 2:
                        s.wait_ge(st[n % 2], 16 * (n // 2))
                    s.activation(xo[n % 2][:, :], t1_[n % 2][:, :], AF.Identity,
                                 scale=srx[:, 7:8], bias=srx[:, 9:10]).then_inc(se, 1)


def build_p10():
    return _standalone(emit_p10)



def emit_p3(fx):
    kp_s = fx.dram("kp_s", [BPC, L, D], BF16, kind="ExternalInput")
    v_s = fx.dram("v_s", [BPC, L, D], BF16, kind="ExternalInput")
    q_s = fx.dram("q_s", [BPC, L, D], BF16, kind="ExternalInput")
    toep = fx.dram("toep", [128, 16, 128, 128], F32, kind="ExternalInput")
    drep = fx.dram("drep", [128, 128], F32, kind="ExternalInput")
    y_s = fx.dram("y_s", [BPC, L, D], BF16, kind="ExternalOutput")

    NBLK = 8
    NQB = L // 128
    CPB = 16

    def chunks_for(delta):
        cols = (16 - delta) * 128
        return (cols + 511) // 512
    NMM_C = sum(chunks_for(d_) for d_ in range(16))

    V_BLK = 6
    V_C = 6
    vdiag = {}
    vc_end = {}
    vc_prod = {}
    vc_kv = {}
    v_cnt = 2
    v_cnt += 1
    vc_kv[(0, 0)] = v_cnt
    for m in range(NBLK):
        vdiag[m] = v_cnt + V_BLK
        v_cnt += V_BLK
        for cl in range(CPB):
            v_cnt += 1
            vc_prod[(m, cl)] = v_cnt
            if cl < CPB - 1:
                v_cnt += 1
                vc_kv[(m, cl + 1)] = v_cnt
            v_cnt += 4
            vc_end[(m, cl)] = v_cnt
        if m < NBLK - 1:
            v_cnt += 1
            vc_kv[(m + 1, 0)] = v_cnt
    V_TOTAL = v_cnt

    with fx.phase() as al:
        ybuf = al.sb("ybuf", [128, NQB * BPC * D], BF16)
        kt = [al.sb(f"kt{j}", [128, NQB * BPC * 128], BF16) for j in range(2)]
        vt = [al.sb(f"vt{j}", [128, NQB * BPC * 128], BF16) for j in range(2)]
        qt = [al.sb(f"qt{j}", [128, NQB * BPC * 128], BF16) for j in range(2)]
        tp = [al.sb(f"tp{j}", [128, 16 * 128], BF16) for j in range(2)]
        drep_sb = al.sb("drep_sb", [128, 128], F32)
        kv = [al.sb(f"kv_sb{j}", [128, 4096], BF16) for j in range(2)]
        prod = al.sb("prod", [128, 4096], BF16)
        tr1 = al.sb("tr1", [128, 2048], BF16)
        tr2 = al.sb("tr2", [128, 1024], BF16)
        tr3 = al.sb("tr3", [128, 256], BF16)
        qks = al.sb("qks", [128, 512], BF16)
        qks2 = al.sb("qks2", [128, 512], BF16)
        py = [al.ps(f"py{j}", [128, 2048]) for j in range(2)]
        ini, ld, mm, ve, st = [al.sem(n) for n in ("ini", "ld", "mm", "ve", "st")]
        lk = [al.sem(f"lk3{j}") for j in range(2)]
        ltp = [al.sem(f"ltp{j}") for j in range(2)]
        block = al.block()

        @block.sync
        def _(sy):
            for m in range(NBLK):
                if m >= 2:
                    sy.wait_ge(ve, vc_end[(m - 2, CPB - 1)])
                for (buf, src) in ((kt, kp_s), (vt, v_s), (qt, q_s)):
                    for b in range(BPC):
                        ap = ap_of(src, b * L * D + m * 128, [[D, 128], [128 * D, NQB], [1, 128]])
                        dst = ap_of(buf[m % 2], b * 128, [[NQB * BPC * 128, 128], [BPC * 128, NQB], [1, 128]])
                        sy.dma_start(out=dst, in_=ap).then_inc(lk[m % 2], 16)
            sy.wait_ge(ve, V_TOTAL)
            for b in range(BPC):
                dst = ap_of(y_s, b * L * D, [[D, 128], [128 * D, NQB], [1, D]])
                src = ap_of(ybuf, b * D, [[NQB * BPC * D, 128], [BPC * D, NQB], [1, D]])
                sy.dma_start(out=dst, in_=src).then_inc(st, 16)
            sy.wait_ge(st, 32)

        @block.gpsimd
        def _(g):
            g.dma_start(out=drep_sb[:, :], in_=drep.ap()[:, :]).then_inc(ld, 16)
            for m in range(NBLK):
                for cl in range(CPB):
                    ci = m * CPB + cl
                    n = m * CPB + cl
                    if n >= 2:
                        g.wait_ge(mm, n - 1)
                    src = ap_of(toep, ci * 16 * 128 * 128, [[128, 128], [128 * 128, 16], [1, 128]])
                    g.dma_start(out=tp[n % 2][:, :].rearrange("p (d n) -> p d n", d=16),
                                in_=src).then_inc(ltp[n % 2], 16)

        @block.tensor
        def _(t):
            for m in range(NBLK):
                for cl in range(CPB):
                    n = m * CPB + cl
                    t.wait_ge(ltp[n % 2], 16 * (n // 2 + 1))
                    t.wait_ge(ve, vc_kv[(m, cl)])
                    if n >= 2:
                        t.wait_ge(ve, vc_prod[(m, cl - 2) if cl >= 2 else (m - 1, cl + CPB - 2)])
                    last = None
                    for delta in range(16):
                        lhsT = sb_ap(tp[n % 2], delta * 128, [[1, 128]])
                        for bk in range(4):
                            if delta > 4 * bk + 3:
                                continue
                            o0 = bk * 512
                            last = t.matmul(py[n % 2][:, o0:o0 + 512],
                                            lhsT, sb_ap(kv[n % 2], 2048 + o0 - delta * 128, [[1, 512]]),
                                            start=(delta == 0), stop=(delta == 4 * bk + 3))
                    last.then_inc(mm, 1)

        @block.vector
        def _(v):
            v = AutoWait(v, ve)
            v.wait_ge(ld, 16)
            v.memset(kv[0][:, 0:2048], 0.0).then_inc(ve, 1)
            v.memset(kv[1][:, 0:2048], 0.0).then_inc(ve, 1)

            def kv_build(m2, cl2):
                n2 = m2 * CPB + cl2
                k2_, v2_ = kt[m2 % 2], vt[m2 % 2]
                if cl2 == 0:
                    v.wait_ge(lk[m2 % 2], 96 * (m2 // 2 + 1))
                v.tensor_tensor(kv[n2 % 2][:, 2048:4096].rearrange("p (q b j i) -> p q b j i", q=16, b=2, j=8),
                                ap_of(k2_, cl2 * 8, [[NQB * BPC * 128, 128], [256, 16], [128, 2], [0, 8], [1, 8]]),
                                ap_of(v2_, cl2 * 8, [[NQB * BPC * 128, 128], [256, 16], [128, 2], [1, 8], [0, 8]]),
                                ALU.mult).then_inc(ve, 1)

            kv_build(0, 0)
            for m in range(NBLK):
                k_, v_, q_ = kt[m % 2], vt[m % 2], qt[m % 2]
                v.wait_ge(lk[m % 2], 96 * (m // 2 + 1))
                v.tensor_tensor(prod[:, 0:4096], sb_ap(k_, 0, [[1, 4096]]),
                                sb_ap(q_, 0, [[1, 4096]]), ALU.mult).then_inc(ve, 1)
                v.tensor_tensor(ap_of(tr1, 0, [[2048, 128], [4, 512], [1, 4]]),
                                ap_of(prod, 0, [[4096, 128], [8, 512], [1, 4]]),
                                ap_of(prod, 4, [[4096, 128], [8, 512], [1, 4]]), ALU.add).then_inc(ve, 1)
                v.tensor_tensor(ap_of(tr2, 0, [[1024, 128], [2, 512], [1, 2]]),
                                ap_of(tr1, 0, [[2048, 128], [4, 512], [1, 2]]),
                                ap_of(tr1, 2, [[2048, 128], [4, 512], [1, 2]]), ALU.add).then_inc(ve, 1)
                v.tensor_tensor(qks[:, 0:512],
                                ap_of(tr2, 0, [[1024, 128], [2, 512]]),
                                ap_of(tr2, 1, [[1024, 128], [2, 512]]), ALU.add).then_inc(ve, 1)
                v.tensor_tensor(ap_of(qks2, 0, [[512, 128], [16, 32], [1, 16]]),
                                ap_of(qks, 0, [[512, 128], [16, 32], [1, 16]]),
                                ap_of(drep_sb, m * 16, [[128, 128], [0, 32], [1, 16]]), ALU.mult).then_inc(ve, 1)
                v.tensor_tensor(ap_of(ybuf, m * 16, [[NQB * BPC * D, 128], [BPC * D, 16], [D, 2], [1, 16], [128, 8]]),
                                ap_of(v_, 0, [[NQB * BPC * 128, 128], [256, 16], [128, 2], [8, 16], [1, 8]]),
                                ap_of(qks2, 0, [[512, 128], [32, 16], [16, 2], [1, 16], [0, 8]]), ALU.mult).then_inc(ve, 1)
                for cl in range(CPB):
                    n = m * CPB + cl
                    v.wait_ge(mm, n + 1)
                    v.tensor_tensor(prod[:, 0:2048].rearrange("p (q b j i) -> p q b j i", q=16, b=2, j=8),
                                    py[n % 2][:, :].rearrange("p (q b j i) -> p q b j i", q=16, b=2, j=8),
                                    ap_of(q_, cl * 8, [[NQB * BPC * 128, 128], [256, 16], [128, 2], [0, 8], [1, 8]]),
                                    ALU.mult).then_inc(ve, 1)
                    if cl < CPB - 1:
                        kv_build(m, cl + 1)
                    elif m < NBLK - 1:
                        pass
                    v.tensor_tensor(ap_of(tr1, 0, [[2048, 128], [4, 256], [1, 4]]),
                                    ap_of(prod, 0, [[4096, 128], [8, 256], [1, 4]]),
                                    ap_of(prod, 4, [[4096, 128], [8, 256], [1, 4]]), ALU.add).then_inc(ve, 1)
                    v.tensor_tensor(ap_of(tr2, 0, [[1024, 128], [2, 256], [1, 2]]),
                                    ap_of(tr1, 0, [[2048, 128], [4, 256], [1, 2]]),
                                    ap_of(tr1, 2, [[2048, 128], [4, 256], [1, 2]]), ALU.add).then_inc(ve, 1)
                    v.tensor_tensor(tr3[:, 0:256],
                                    ap_of(tr2, 0, [[1024, 128], [2, 256]]),
                                    ap_of(tr2, 1, [[1024, 128], [2, 256]]), ALU.add).then_inc(ve, 1)
                    ysl = ap_of(ybuf, m * 16 + cl, [[NQB * BPC * D, 128], [BPC * D, 16], [D, 2], [128, 8]])
                    v.tensor_tensor(ysl, ysl,
                                    ap_of(tr3, 0, [[256, 128], [16, 16], [8, 2], [1, 8]]), ALU.add).then_inc(ve, 1)
                if m < NBLK - 1:
                    kv_build(m + 1, 0)


def build_p3():
    return _standalone(emit_p3)


def prod3_ap(tr3):
    return ap_of(tr3, 0, [[256, 128], [16, 16], [8, 2], [1, 8]])


P2_NCH = 8
P2_TPC, P2_VPC, P2_SPC = 112, 140, 56


def emit_p2(fx):
    k_s = fx.dram("k_s", [BPC, L, D], BF16, kind="ExternalInput")
    kkT = fx.dram("ssmkkT", [L, D], F32, kind="ExternalInput")
    dkrep = fx.dram("dkrep", [128, D], F32, kind="ExternalInput")
    wa2 = fx.dram("fft_wa2", [32, 16384], BF16, kind="ExternalInput")
    va2 = fx.dram("fft_va2", [64, 16384], BF16, kind="ExternalInput")
    wbst = fx.dram("fft_wbst", [64, 256], BF16, kind="ExternalInput")
    va3 = fx.dram("fft_va3", [64, 128], BF16, kind="ExternalInput")
    idn = fx.dram("idn_bf16", [128, 128], BF16, kind="ExternalInput")
    kp_s = fx.dram("kp_s", [BPC, L, D], BF16, kind="ExternalOutput")

    with fx.phase() as al:
        xre = al.sb("xre", [32, 8192], BF16)
        xim = al.sb("xim", [32, 8192], BF16)
        xkr = al.sb("xkr", [32, 8192], BF16)
        wa2_sb = al.sb("wa2_sb", [32, 16384], BF16)
        va2_sb = al.sb("va2_sb", [64, 16384], BF16)
        wb_sb = al.sb("wbst_sb", [64, 256], BF16)
        va3_sb = al.sb("va3_sb", [64, 128], BF16)
        idn_sb = al.sb("idn_p2", [128, 128], BF16)
        dk_sb = al.sb("dk_p2", [128, D], F32)
        af = al.sb("af", [128, 8192], BF16)
        at2r = al.sb("at2r", [64, 8192], BF16)
        at2i = al.sb("at2i", [64, 8192], BF16)
        kff = al.sb("kff", [128, 8192], BF16)
        zko2sb = al.sb("zko2sb", [32, 8192], BF16)
        pa = al.ps("pa_p2", [128, 1024])
        pb = al.ps("pb_p2", [128, 1024])
        pt = [al.ps(f"pt_p2{j}", [64, 1024], dtype=BF16) for j in range(2)]
        pc = [al.ps(f"pc_p2{j}", [128, 512]) for j in range(2)]
        ld, lx, mm, ve, se, st = [al.sem(n) for n in ("ld", "lx", "mm", "ve", "se", "st")]
        block = al.block()

        zko = xkr

        def tKA(ch, f): return P2_TPC * ch + f + 1
        def tKT(ch, p): return P2_TPC * ch + 8 + p + 1
        def tKB(ch, j): return P2_TPC * ch + 24 + j + 1
        def tDA(ch, f): return P2_TPC * ch + 40 + f + 1
        def tDT(ch, p): return P2_TPC * ch + 48 + p + 1
        def tDB(ch, j): return P2_TPC * ch + 64 + j + 1
        def tI1(ch, f): return P2_TPC * ch + 80 + f + 1
        def tIT(ch, p): return P2_TPC * ch + 88 + p + 1
        def tI2(ch, j): return P2_TPC * ch + 104 + j + 1

        def sKA(ch, f): return P2_SPC * ch + f + 1
        def sDA(ch, f): return P2_SPC * ch + 8 + f + 1
        def sI1(ch, f): return P2_SPC * ch + 16 + f + 1
        def sI2(ch, j): return P2_SPC * ch + 24 + 2 * (j + 1)

        def vKT(ch, i): return P2_VPC * ch + i + 1
        def vKB(ch, j): return P2_VPC * ch + 32 + j + 1
        def vDT(ch, i): return P2_VPC * ch + 48 + i + 1
        def vDB(ch, j): return P2_VPC * ch + 80 + j + 1
        def vPROD(ch): return P2_VPC * ch + 104
        def vIT(ch, i): return P2_VPC * ch + 104 + i + 1

        @block.gpsimd
        def _(g):
            consts = [(wa2_sb, wa2), (va2_sb, va2), (wb_sb, wbst), (va3_sb, va3),
                      (idn_sb, idn), (dk_sb, dkrep)]
            for i, (dst, src) in enumerate(consts):
                if i:
                    g.wait_ge(ld, 16 * i)
                g.dma_start(out=dst[0:src.shape[0], :], in_=src.ap()[:, :]).then_inc(ld, 16)
            for ch in range(P2_NCH):
                if ch > 0:
                    g.wait_ge(ve, P2_VPC * ch)
                    g.wait_ge(st, 32 * ch)
                for j, (bi, dst) in enumerate(((0, xre), (1, xim))):
                    if ch or j:
                        g.wait_ge(lx, 48 * ch + 16 * j)
                    src = ap_of(k_s, bi * L * D + ch * 128, [[64 * D, 32], [D, 64], [1, 128]])
                    g.dma_start(out=dst[:, :].rearrange("p (a b) -> p a b", a=64), in_=src).then_inc(lx, 16)
                g.wait_ge(lx, 48 * ch + 32)
                src = ap_of(kkT, ch * 128, [[64 * D, 32], [D, 64], [1, 128]])
                g.dma_start(out=xkr[:, :].rearrange("p (a b) -> p a b", a=64), in_=src).then_inc(lx, 16)

        @block.sync
        def _(sy):
            for ch in range(P2_NCH):
                sy.wait_ge(ve, P2_VPC * (ch + 1))
                src0 = ap_of(zko, 0, [[8192, 32], [128, 64], [1, 128]])
                dst0 = ap_of(kp_s, ch * 128, [[64 * D, 32], [D, 64], [1, 128]])
                sy.dma_start(out=dst0, in_=src0).then_inc(st, 16)
                src1 = ap_of(zko2sb, 0, [[8192, 32], [128, 64], [1, 128]])
                dst1 = ap_of(kp_s, L * D + ch * 128, [[64 * D, 32], [D, 64], [1, 128]])
                sy.dma_start(out=dst1, in_=src1).then_inc(st, 16)
            sy.wait_ge(st, 32 * P2_NCH)

        @block.tensor
        def _(t):
            def wa_st(n1, half):
                return sb_ap(wa2_sb, n1 * 256 + half * 128, [[1, 128]], nparts=32)

            def va2_st(k1, half):
                return sb_ap(va2_sb, k1 * 256 + half * 128, [[1, 128]], nparts=64)

            def stageA(ch, kernel_only):
                for f in range(8):
                    if kernel_only:
                        if f >= 2:
                            t.wait_ge(se, sKA(ch, f - 2))
                    else:
                        t.wait_ge(se, sDA(ch, f - 2) if f >= 2 else sKA(ch, 7))
                    P = (pa, pb)[f % 2]
                    for l in range(8):
                        n1 = 8 * f + l
                        out = P[:, l * 128:(l + 1) * 128]
                        if kernel_only:
                            r = sb_ap(xkr, n1 * 128, [[1, 128]], nparts=32)
                            op = t.matmul(out, wa_st(n1, 0), r, start=True, stop=True)
                        else:
                            r0 = sb_ap(xre, n1 * 128, [[1, 128]], nparts=32)
                            r1 = sb_ap(xim, n1 * 128, [[1, 128]], nparts=32)
                            t.matmul(out, wa_st(n1, 0), r0, start=True, stop=False)
                            op = t.matmul(out, wa_st(n1, 1), r1, start=False, stop=True)
                    op.then_inc(mm, 1)

            def stageT(ch, sewait, vfun, pfun):
                for p in range(16):
                    t.wait_ge(se, sewait)
                    t.wait_ge(ve, pfun(p))
                    op = None
                    for j in range(8):
                        d_ = 8 * p + j
                        src = AP(af, d_, [[8192, 128], [128, 64]])
                        op = t.transpose(pt[p % 2][:, j * 128:(j + 1) * 128], src, idn_sb[:, :])
                    op.then_inc(mm, 1)

            def stageB(ch, first_wait, pcfun):
                for j in range(16):
                    if j == 0:
                        t.wait_ge(ve, first_wait)
                    if j >= 2:
                        t.wait_ge(ve, pcfun(j))
                    rre = sb_ap(at2r, j * 512, [[1, 512]], nparts=64)
                    rim = sb_ap(at2i, j * 512, [[1, 512]], nparts=64)
                    t.matmul(pc[j % 2][:, :], sb_ap(wb_sb, 0, [[1, 128]], nparts=64),
                             rre, start=True, stop=False)
                    t.matmul(pc[j % 2][:, :], sb_ap(wb_sb, 128, [[1, 128]], nparts=64),
                             rim, start=False, stop=True).then_inc(mm, 1)

            t.wait_ge(ld, 96)
            for ch in range(P2_NCH):
                t.wait_ge(lx, 48 * (ch + 1))
                if ch > 0:
                    t.wait_ge(ve, P2_VPC * ch)
                    t.wait_ge(se, P2_SPC * ch)
                stageA(ch, True)
                stageT(ch, sKA(ch, 7), vKT,
                       lambda p: (vIT(ch - 1, 2 * (14 + p) + 1) if ch > 0 else 0)
                       if p < 2 else vKT(ch, 2 * (p - 2) + 1))
                stageB(ch, vKT(ch, 31), lambda j: vKB(ch, j - 2))
                stageA(ch, False)
                stageT(ch, sDA(ch, 7), vDT,
                       lambda p: vKT(ch, 2 * (14 + p) + 1) if p < 2
                       else vDT(ch, 2 * (p - 2) + 1))
                stageB(ch, vDT(ch, 31), lambda j: vDB(ch, j - 2))
                for f in range(8):
                    if f == 0:
                        t.wait_ge(ve, vPROD(ch))
                    elif f >= 2:
                        t.wait_ge(se, sI1(ch, f - 2))
                    P = (pa, pb)[f % 2]
                    for l in range(8):
                        k1 = 8 * f + l
                        out = P[:, l * 128:(l + 1) * 128]
                        r0 = sb_ap(at2r, k1 * 128, [[1, 128]], nparts=64)
                        r1 = sb_ap(at2i, k1 * 128, [[1, 128]], nparts=64)
                        t.matmul(out, va2_st(k1, 0), r0, start=True, stop=False)
                        op = t.matmul(out, va2_st(k1, 1), r1, start=False, stop=True)
                    op.then_inc(mm, 1)
                stageT(ch, sI1(ch, 7), vIT,
                       lambda p: vDT(ch, 2 * (14 + p) + 1) if p < 2
                       else vIT(ch, 2 * (p - 2) + 1))
                for f in range(8):
                    if f == 0:
                        t.wait_ge(ve, vIT(ch, 31))
                    elif f >= 2:
                        t.wait_ge(se, sI2(ch, 2 * (f - 2) + 1))
                    P = (pa, pb)[f % 2]
                    for j2 in range(2):
                        j = 2 * f + j2
                        out = P[0:64, j2 * 512:(j2 + 1) * 512]
                        rre = sb_ap(at2r, j * 512, [[1, 512]], nparts=64)
                        rim = sb_ap(at2i, j * 512, [[1, 512]], nparts=64)
                        t.matmul(out, sb_ap(va3_sb, 0, [[1, 64]], nparts=64),
                                 rre, start=True, stop=False)
                        op = t.matmul(out, sb_ap(va3_sb, 64, [[1, 64]], nparts=64),
                                      rim, start=False, stop=True)
                    op.then_inc(mm, 1)

        @block.scalar
        def _(s):
            s = AutoWait(s, se)
            for ch in range(P2_NCH):
                if ch > 0:
                    s.wait_ge(st, 32 * ch)
                for f in range(8):
                    s.wait_ge(mm, tKA(ch, f))
                    s.copy(sb_ap(af, f * 1024, [[1, 1024]]),
                           (pa, pb)[f % 2][:, :]).then_inc(se, 1)
                for f in range(8):
                    s.wait_ge(mm, tDA(ch, f))
                    s.copy(sb_ap(af, f * 1024, [[1, 1024]]),
                           (pa, pb)[f % 2][:, :]).then_inc(se, 1)
                for f in range(8):
                    s.wait_ge(mm, tI1(ch, f))
                    s.copy(sb_ap(af, f * 1024, [[1, 1024]]),
                           (pa, pb)[f % 2][:, :]).then_inc(se, 1)
                for j in range(16):
                    f, j2 = j // 2, j % 2
                    s.wait_ge(mm, tI2(ch, f))
                    P = (pa, pb)[f % 2]
                    src_re = sb_ap(P, j2 * 512, [[1, 512]], base_part=0, nparts=32)
                    src_im = sb_ap(P, j2 * 512, [[1, 512]], base_part=32, nparts=32)
                    dre = ap_of(xkr, j * 8, [[8192, 32], [1, 8], [128, 64]])
                    dim = ap_of(zko2sb, j * 8, [[8192, 32], [1, 8], [128, 64]])
                    s.copy(dre, src_re.rearrange("p (a b) -> p a b", a=8)).then_inc(se, 1)
                    s.copy(dim, src_im.rearrange("p (a b) -> p a b", a=8)).then_inc(se, 1)

        @block.vector
        def _(v):
            v = AutoWait(v, ve)

            def tevac(p, tfun, ch):
                v.wait_ge(mm, tfun(ch, p))
                src_re = ap_of(pt[p % 2], 0, [[1024, 64], [128, 8], [1, 64]])
                src_im = ap_of(pt[p % 2], 64, [[1024, 64], [128, 8], [1, 64]])
                dre = sb_ap(at2r, p * 512, [[64, 8], [1, 64]], nparts=64)
                dim = sb_ap(at2i, p * 512, [[64, 8], [1, 64]], nparts=64)
                v.tensor_copy(dre, src_re).then_inc(ve, 1)
                v.tensor_copy(dim, src_im).then_inc(ve, 1)

            for ch in range(P2_NCH):
                for p in range(16):
                    tevac(p, tKT, ch)
                for j in range(16):
                    v.wait_ge(mm, tKB(ch, j))
                    dful = ap_of(kff, j * 8, [[8192, 128], [1, 8], [128, 64]])
                    v.tensor_copy(dful, pc[j % 2][:, :].rearrange("p (a b) -> p a b", a=8)).then_inc(ve, 1)
                for p in range(16):
                    tevac(p, tDT, ch)
                for j in range(16):
                    v.wait_ge(mm, tDB(ch, j))
                    dd = ap_of(af, j * 8, [[8192, 128], [1, 8], [128, 64]])
                    v.tensor_copy(dd, pc[j % 2][:, :].rearrange("p (a b) -> p a b", a=8)).then_inc(ve, 1)
                v.wait_ge(mm, tDB(ch, 15))
                dr = sb_ap(af, 0, [[1, 8192]], base_part=0, nparts=64)
                di = sb_ap(af, 0, [[1, 8192]], base_part=64, nparts=64)
                kfr0 = sb_ap(kff, 0, [[1, 8192]], base_part=0, nparts=64)
                kfi64 = sb_ap(kff, 0, [[1, 8192]], base_part=64, nparts=64)
                v.tensor_tensor(at2r[:, :], dr, kfr0, ALU.mult).then_inc(ve, 1)
                v.tensor_tensor(at2i[:, :], di, kfi64, ALU.mult).then_inc(ve, 1)
                v.tensor_tensor(at2r[:, :], at2r[:, :], at2i[:, :], ALU.subtract).then_inc(ve, 1)
                v.tensor_copy(at2i[:, :], kfi64).then_inc(ve, 1)
                v.tensor_tensor(at2i[:, :], dr, at2i[:, :], ALU.mult).then_inc(ve, 1)
                v.tensor_copy(kfi64, kfr0).then_inc(ve, 1)
                v.tensor_tensor(kfr0, di, kfi64, ALU.mult).then_inc(ve, 1)
                v.tensor_tensor(at2i[:, :], at2i[:, :], kfr0, ALU.add).then_inc(ve, 1)
                for p in range(16):
                    tevac(p, tIT, ch)
                v.wait_ge(se, sI2(ch, 15))
                zv = sb_ap(af, 0, [[128, 64], [1, 128]], base_part=0, nparts=32)
                xr = ap_of(xre, 0, [[8192, 32], [128, 64], [1, 128]])
                xi = ap_of(xim, 0, [[8192, 32], [128, 64], [1, 128]])
                dk1 = ap_of(dk_sb, ch * 128, [[1024, 32], [0, 64], [1, 128]])
                zk = ap_of(zko, 0, [[8192, 32], [128, 64], [1, 128]])
                zk2 = ap_of(zko2sb, 0, [[8192, 32], [128, 64], [1, 128]])
                v.tensor_tensor(zv, xr, dk1, ALU.mult).then_inc(ve, 1)
                v.tensor_tensor(zk, zk, zv, ALU.add).then_inc(ve, 1)
                v.tensor_tensor(zv, xi, dk1, ALU.mult).then_inc(ve, 1)
                v.tensor_tensor(zk2, zk2, zv, ALU.add).then_inc(ve, 1)


def build_p2():
    return _standalone(emit_p2)


NBH = BPC * 16
TPC8, VPC8, SPC8 = 42, 81, 30
SCALE8 = float(64 ** -0.5)


def emit_p8(fx):
    qk1_s = fx.dram("qk1_s", [2 * D, NTOK], BF16, kind="ExternalInput")
    v1_s = fx.dram("v1_s", [BPC, L, D], BF16, kind="ExternalInput")
    maskB = fx.dram("maskB", [128, 2048], F32, kind="ExternalInput")
    maskA = fx.dram("maskA", [128, 2048], F32, kind="ExternalInput")
    idn = fx.dram("idn_f32", [128, 128], F32, kind="ExternalInput")
    oT_s = fx.dram("oT_s", [D, BPC * 512], BF16, kind="ExternalOutput")
    rowb = fx.dram("rowb8", [12, 128], F32)

    with fx.phase() as al:
        qt = [al.sb(f"qt8{j}", [64, 512], BF16) for j in range(2)]
        kt = [al.sb(f"kt8{j}", [64, 2048], BF16) for j in range(2)]
        vt = [al.sb(f"vt8{j}", [128, 1024], BF16) for j in range(2)]
        mB_sb = al.sb("mB_sb", [128, 2048], F32)
        mA_sb = al.sb("mA_sb", [128, 2048], F32)
        idn_sb = al.sb("idn8", [128, 128], F32)
        onesf = al.sb("onesf8", [1, 128], F32)
        pAt = [al.sb(f"pAt8{j}", [128, 512], BF16) for j in range(2)]
        mx = al.sb("mx8", [128, 16], F32)
        M_ = al.sb("M8", [128, 16], F32)
        nM = al.sb("nM8", [128, 16], F32)
        eM = al.sb("eM8", [128, 16], F32)
        P_ = al.sb("P8c", [128, 16], F32)
        dM = al.sb("dM8", [128, 16], F32)
        ee = al.sb("ee8", [128, 16], F32)
        lc = al.sb("lc8", [128, 16], F32)
        ri_ = al.sb("ri8", [128, 16], F32)
        rr = al.sb("rr8", [128, 16], F32)
        qq_ = al.sb("qq8", [128, 16], F32)
        ww = al.sb("ww8", [128, 16], F32)
        sc_ = al.sb("sc8", [128, 16], F32)
        scol = al.sb("scol8", [128, 12], F32)
        scolT = al.sb("scolT8", [12, 128], F32)
        srow = al.sb("srow8", [1, 1536], F32)
        preps = al.sb("preps8", [64, 1536], F32)
        etr = al.sb("etr8", [128, 512], F32)
        oacc = al.sb("oacc8", [64, 512], F32)
        otmp = al.sb("otmp8", [64, 512], F32)
        otile = [al.sb(f"otile8{j}", [64, 512], BF16) for j in range(2)]
        pscr = al.ps("pscr8", [128, 1536])
        pA = al.ps("pA8", [128, 1024])
        po = [al.ps(f"po8{j}", [64, 512]) for j in range(3)]
        (mm, ve, se, rb, ldc, ini8) = [al.sem(n) for n in ("mm", "ve", "se", "rb", "ldc", "ini8")]
        stp = [al.sem(f"stp8{j}") for j in range(2)]
        lqp = [al.sem(f"lqp{j}") for j in range(2)]
        block = al.block()

        def tSB(n, tqc, j): return TPC8 * n + 4 * tqc + j + 1
        def tTR(n): return TPC8 * n + 17
        def tRP(n): return TPC8 * n + 18
        def tSA(n, idx): return TPC8 * n + 18 + 2 * idx + 1
        def tPV(n, idx): return TPC8 * n + 18 + 2 * idx + 2

        def vstat(n, tqc): return VPC8 * n + 13 * tqc
        def vSTe(n): return VPC8 * n + 52
        def vTL(n, i): return VPC8 * n + 52 + i + 1
        def vTRe(n): return VPC8 * n + 70
        def vMK(n, tkc): return VPC8 * n + 70 + tkc + 1
        def vPR(n): return VPC8 * n + 75
        def vCB(n, i): return VPC8 * n + 75 + i + 1

        def sP(n, tqc, j): return SPC8 * n + 4 * tqc + j + 1
        def sEE(n): return SPC8 * n + 17
        def sEM(n): return SPC8 * n + 18
        def sX(n, idx): return SPC8 * n + 18 + idx + 1

        @block.gpsimd
        def _(g):
            for i, (dst, src) in enumerate(((mB_sb, maskB), (mA_sb, maskA), (idn_sb, idn))):
                if i:
                    g.wait_ge(ldc, 16 * i)
                g.dma_start(out=dst[:, :], in_=src.ap()[:, :]).then_inc(ldc, 16)
            g.wait_ge(ldc, 48)
            g.memset(onesf[:, :], 1.0).then_inc(ini8, 1)
            for n in range(NBH):
                b, h = n // 16, n % 16
                if n >= 2:
                    g.wait_ge(mm, tPV(n - 2, 11))
                g.dma_start(out=qt[n % 2][:, :],
                            in_=ap_of(qk1_s, (h * 64) * NTOK + b * L + 1536,
                                      [[NTOK, 64], [1, 512]])).then_inc(lqp[n % 2], 16)
                g.wait_ge(lqp[n % 2], 48 * (n // 2) + 16)
                g.dma_start(out=kt[n % 2][:, :],
                            in_=ap_of(qk1_s, (D + h * 64) * NTOK + b * L,
                                      [[NTOK, 64], [1, 2048]])).then_inc(lqp[n % 2], 16)
                g.wait_ge(lqp[n % 2], 48 * (n // 2) + 32)
                g.dma_start(out=vt[n % 2][:, :].rearrange("p (a b) -> p a b", a=16),
                            in_=ap_of(v1_s, b * L * D + h * 64,
                                      [[D, 128], [128 * D, 16], [1, 64]])).then_inc(lqp[n % 2], 16)
                g.wait_ge(ve, vTRe(n))
                g.dma_start(out=rowb.ap()[:, :], in_=scolT[:, :]).then_inc(rb, 16)
                g.wait_ge(rb, 32 * n + 16)
                g.dma_start(out=srow[0:1, :],
                            in_=ap_of(rowb, 0, [[1536, 1], [1, 1536]])).then_inc(rb, 16)

        @block.tensor
        def _(t):
            t.wait_ge(ldc, 48)
            t.wait_ge(ini8, 1)
            for n in range(NBH):
                t.wait_ge(lqp[n % 2], 48 * (n // 2 + 1))
                if n >= 1:
                    t.wait_ge(ve, vPR(n - 1))
                for tqc in range(4):
                    for j in range(4):
                        i16 = 4 * tqc + j
                        if n > 0 or i16 >= 2:
                            p16, pn = i16 - 2, n
                            if p16 < 0:
                                p16, pn = 14 + i16, n - 1
                            t.wait_ge(se, sP(pn, p16 // 4, p16 % 4))
                        bk = (i16 % 2) * 512
                        t.matmul(pscr[:, bk:bk + 512], sb_ap(qt[n % 2], tqc * 128, [[1, 128]], nparts=64),
                                 sb_ap(kt[n % 2], j * 512, [[1, 512]], nparts=64),
                                 start=True, stop=True).then_inc(mm, 1)
                t.wait_ge(ve, vTL(n, 16))
                t.transpose(pscr[0:12, 1024:1152], scol[:, 0:12], idn_sb[:, :]).then_inc(mm, 1)
                t.wait_ge(rb, 32 * (n + 1))
                t.wait_ge(se, sP(n, 3, 3))
                t.matmul(pscr[0:64, 0:512], onesf[0:1, 0:64], srow[0:1, 0:512], start=True, stop=True)
                t.matmul(pscr[0:64, 512:1024], onesf[0:1, 0:64], srow[0:1, 512:1024], start=True, stop=True)
                t.matmul(pscr[0:64, 1024:1536], onesf[0:1, 0:64], srow[0:1, 1024:1536],
                         start=True, stop=True).then_inc(mm, 1)
                for idx in range(12):
                    j, tkc = idx // 4 + 1, idx % 4
                    if idx >= 2:
                        t.wait_ge(se, sX(n, idx - 2))
                    elif n > 0:
                        t.wait_ge(se, sX(n - 1, 10 + idx))
                    if idx == 0:
                        t.wait_ge(ve, VPC8 * n)
                    bkA = (idx % 2) * 512
                    t.matmul(pA[:, bkA:bkA + 512],
                             sb_ap(kt[n % 2], j * 512 + tkc * 128, [[1, 128]], nparts=64),
                             sb_ap(qt[n % 2], 0, [[1, 512]], nparts=64),
                             start=True, stop=True).then_inc(mm, 1)
                    t.wait_ge(se, sX(n, idx))
                    t.matmul(po[j - 1][:, :], sb_ap(vt[n % 2], (j * 4 + tkc) * 64, [[1, 64]]),
                             pAt[idx % 2][:, :], start=(tkc == 0), stop=(tkc == 3)).then_inc(mm, 1)

        @block.vector
        def _(v):
            v = AutoWait(v, ve)
            AV = lambda t_, o=0: ap_of(t_, o, [[16, 128], [4, 4], [1, 3]])
            BV = lambda t_, o=1: ap_of(t_, o, [[16, 128], [4, 4], [1, 3]])
            for n in range(NBH):
                for tqc in range(4):
                    for j in range(4):
                        c = 4 * tqc + j
                        v.wait_ge(mm, tSB(n, tqc, j))
                        bk = (c % 2) * 512
                        if j == 3:
                            v.tensor_tensor(pscr[:, bk:bk + 512], pscr[:, bk:bk + 512],
                                            sb_ap(mB_sb, tqc * 512, [[1, 512]]), ALU.add).then_inc(ve, 1)
                        v.reduce_max(mx[:, c:c + 1], pscr[:, bk:bk + 512], axis=AXX).then_inc(ve, 1)
                        if j == 0:
                            v.tensor_copy(M_[:, c:c + 1], mx[:, c:c + 1]).then_inc(ve, 1)
                        else:
                            v.tensor_tensor(M_[:, c:c + 1], M_[:, c - 1:c], mx[:, c:c + 1],
                                            ALU.max).then_inc(ve, 1)
                        v.tensor_scalar_mul(nM[:, c:c + 1], M_[:, c:c + 1], -SCALE8).then_inc(ve, 1)
                v.tensor_tensor(AV(dM), AV(M_), BV(M_), ALU.subtract).then_inc(ve, 1)
                v.wait_ge(se, sP(n, 3, 3))
                v.tensor_copy(ap_of(lc, 0, [[16, 128], [4, 4], [1, 1]]),
                              ap_of(P_, 0, [[16, 128], [4, 4], [1, 1]])).then_inc(ve, 1)
                v.wait_ge(se, sEE(n))
                cw0 = lambda t_, j: ap_of(t_, j, [[16, 128], [4, 4], [1, 1]])
                for j in range(3):
                    v.tensor_tensor(cw0(mx, j), cw0(lc, j), cw0(ee, j), ALU.mult).then_inc(ve, 1)
                    v.tensor_tensor(cw0(lc, j + 1), cw0(mx, j), cw0(P_, j + 1), ALU.add).then_inc(ve, 1)
                v.reciprocal(AV(ri_), BV(lc)).then_inc(ve, 1)
                v.tensor_tensor(AV(rr), AV(lc), AV(ri_), ALU.mult).then_inc(ve, 1)
                v.tensor_scalar(AV(qq_), AV(rr), -1.0, 1.0, ALU.mult, ALU.add).then_inc(ve, 1)
                cw = lambda t_, j: ap_of(t_, j, [[16, 128], [4, 4], [1, 1]])
                v.tensor_copy(cw(ww, 2), cw(rr, 2)).then_inc(ve, 1)
                v.tensor_tensor(cw(ww, 1), cw(rr, 1), cw(qq_, 2), ALU.mult).then_inc(ve, 1)
                v.tensor_tensor(cw(ww, 0), cw(rr, 0), cw(qq_, 1), ALU.mult).then_inc(ve, 1)
                v.tensor_tensor(cw(ww, 0), cw(ww, 0), cw(qq_, 2), ALU.mult).then_inc(ve, 1)
                v.wait_ge(se, sEM(n))
                v.tensor_tensor(AV(sc_), AV(ww), BV(eM), ALU.mult).then_inc(ve, 1)
                v.tensor_tensor(ap_of(scol, 0, [[12, 128], [1, 4], [4, 3]]),
                                AV(sc_), ap_of(ri_, 2, [[16, 128], [4, 4], [0, 3]]),
                                ALU.mult).then_inc(ve, 1)
                v.wait_ge(mm, tTR(n))
                v.tensor_copy(scolT[:, :], pscr[0:12, 1024:1152]).then_inc(ve, 1)
                for tkc in range(4):
                    v.wait_ge(mm, tSA(n, 8 + tkc))
                    bkA = ((8 + tkc) % 2) * 512
                    v.tensor_tensor(pA[:, bkA:bkA + 512], pA[:, bkA:bkA + 512],
                                    sb_ap(mA_sb, tkc * 512, [[1, 512]]), ALU.add).then_inc(ve, 1)
                v.wait_ge(mm, tRP(n))
                v.tensor_copy(preps[:, :], pscr[0:64, 0:1536]).then_inc(ve, 1)
                v.wait_ge(mm, tPV(n, 11))
                v.tensor_tensor(oacc[:, :], po[0][:, :], sb_ap(preps, 0, [[1, 512]], nparts=64),
                                ALU.mult).then_inc(ve, 1)
                v.tensor_tensor(otmp[:, :], po[1][:, :], sb_ap(preps, 512, [[1, 512]], nparts=64),
                                ALU.mult).then_inc(ve, 1)
                v.tensor_tensor(oacc[:, :], oacc[:, :], otmp[:, :], ALU.add).then_inc(ve, 1)
                v.tensor_tensor(otmp[:, :], po[2][:, :], sb_ap(preps, 1024, [[1, 512]], nparts=64),
                                ALU.mult).then_inc(ve, 1)
                v.tensor_tensor(oacc[:, :], oacc[:, :], otmp[:, :], ALU.add).then_inc(ve, 1)
                if n >= 2:
                    v.wait_ge(stp[n % 2], 16 * (n // 2))
                v.tensor_copy(otile[n % 2][:, :], oacc[:, :]).then_inc(ve, 1)

        @block.scalar
        def _(s):
            s = AutoWait(s, se)
            for n in range(NBH):
                for tqc in range(4):
                    vb = vstat(n, tqc)
                    for j in range(4):
                        c = 4 * tqc + j
                        v_ready = vb + (3 * (j + 1) if j < 3 else 13)
                        s.wait_ge(ve, v_ready)
                        bk = (c % 2) * 512
                        s.activation(etr[:, :], pscr[:, bk:bk + 512], AF.Exp,
                                     bias=nM[:, c:c + 1], scale=SCALE8,
                                     accum_out=P_[:, c:c + 1]).then_inc(se, 1)
                s.wait_ge(ve, vTL(n, 0))
                s.activation(ap_of(ee, 0, [[16, 128], [4, 4], [1, 3]]),
                             ap_of(dM, 0, [[16, 128], [4, 4], [1, 3]]),
                             AF.Exp, scale=SCALE8).then_inc(se, 1)
                s.wait_ge(ve, vSTe(n))
                s.activation(eM[:, :], nM[:, :], AF.Exp).then_inc(se, 1)
                for idx in range(12):
                    j, tkc = idx // 4 + 1, idx % 4
                    s.wait_ge(mm, tSA(n, idx))
                    if j == 3:
                        s.wait_ge(ve, vMK(n, tkc))
                    if idx >= 2:
                        s.wait_ge(mm, tPV(n, idx - 2))
                    elif n > 0:
                        s.wait_ge(mm, tPV(n - 1, 10 + idx))
                    bkA = (idx % 2) * 512
                    s.activation(pAt[idx % 2][:, :], pA[:, bkA:bkA + 512], AF.Exp,
                                 scale=SCALE8).then_inc(se, 1)

        @block.sync
        def _(sy):
            for n in range(NBH):
                b, h = n // 16, n % 16
                sy.wait_ge(ve, vCB(n, 5))
                sy.dma_start(out=ap_of(oT_s, (h * 64) * BPC * 512 + b * 512,
                                       [[BPC * 512, 64], [1, 512]]),
                             in_=otile[n % 2][:, :]).then_inc(stp[n % 2], 16)
            sy.wait_ge(stp[0], 16 * (NBH // 2))
            sy.wait_ge(stp[1], 16 * (NBH // 2))


def build_p8():
    return _standalone(emit_p8)



def stage_inputs(inp):
    f32 = np.float32
    u = np.ascontiguousarray(np.asarray(inp["L0_attn_u"], f32))
    shared = {}
    wqkv0 = np.concatenate([np.asarray(inp["L0_attn_Wq"], f32),
                            np.asarray(inp["L0_attn_Wk"], f32),
                            np.asarray(inp["L0_attn_Wv"], f32)], 0)
    shared["wqkv0T"] = np.ascontiguousarray(wqkv0.T)
    shared["bqkv0_row"] = np.ascontiguousarray(np.concatenate(
        [np.asarray(inp["L0_attn_bq"], f32), np.asarray(inp["L0_attn_bk"], f32),
         np.asarray(inp["L0_attn_bv"], f32)])[None, :])
    shared["woT"] = np.ascontiguousarray(np.asarray(inp["L0_attn_Wo"], f32).T)
    shared["bo_row"] = np.ascontiguousarray(np.asarray(inp["L0_attn_bo"], f32)[None, :])
    shared["idn_f32"] = np.eye(128, dtype=f32)
    shared["w1_0"] = np.ascontiguousarray(np.asarray(inp["L0_ffn_w1"], f32))
    shared["b1_0pt"] = np.ascontiguousarray(np.asarray(inp["L0_ffn_b1"], f32).reshape(32, 128).T)
    shared["w2_0"] = np.ascontiguousarray(np.asarray(inp["L0_ffn_w2"], f32))
    shared["b2_0pt"] = np.ascontiguousarray(np.asarray(inp["L0_ffn_b2"], f32).reshape(8, 128).T)
    qkvb = np.asarray(inp["L1_attn_qkv_b"], f32)
    shared["qkvw1T"] = np.ascontiguousarray(np.asarray(inp["L1_attn_qkv_w"], f32).T)
    shared["qkvb1_pt"] = np.ascontiguousarray(qkvb[:2048].reshape(16, 128).T)
    shared["bv1_row"] = np.ascontiguousarray(qkvb[2048:][None, :])
    shared["projT"] = np.ascontiguousarray(np.asarray(inp["L1_attn_proj_w"], f32).T)
    shared["projb_row"] = np.ascontiguousarray(np.asarray(inp["L1_attn_proj_b"], f32)[None, :])
    shared["projb_rep"] = np.ascontiguousarray(
        np.broadcast_to(np.asarray(inp["L1_attn_proj_b"], f32)[None, :], (128, D)))
    shared["w1_1"] = np.ascontiguousarray(np.asarray(inp["L1_ffn_w1"], f32))
    shared["w1T_1"] = np.ascontiguousarray(np.asarray(inp["L1_ffn_w1"], f32).T)
    shared["b1_1pt"] = np.ascontiguousarray(np.asarray(inp["L1_ffn_b1"], f32).reshape(32, 128).T)
    shared["w2_1"] = np.ascontiguousarray(np.asarray(inp["L1_ffn_w2"], f32))
    shared["w2T_1"] = np.ascontiguousarray(np.asarray(inp["L1_ffn_w2"], f32).T)
    shared["b2_1rep"] = np.ascontiguousarray(
        np.broadcast_to(np.asarray(inp["L1_ffn_b2"], f32)[None, :], (128, D)))
    per_core = []
    for c in range(NCORES):
        sl = u[c * BPC:(c + 1) * BPC]
        per_core.append({
            "uT": np.ascontiguousarray(sl.transpose(0, 2, 1)),
            "u_tok": np.ascontiguousarray(sl),
        })
    shared["ssmkkT"] = np.ascontiguousarray(np.asarray(inp["L0_attn_ssm_k_kernel"], f32).T)
    shared["dkrep"] = np.ascontiguousarray(
        np.broadcast_to(np.asarray(inp["L0_attn_ssm_k_D"], f32)[None, :], (128, D)))
    kern = np.asarray(inp["L0_attn_ssm_kernel"], f32)
    dt_i = np.arange(128)[None, None, :]
    ds_i = np.arange(128)[None, :, None]
    dl_i = np.arange(16)[:, None, None]
    idx = 128 * dl_i + dt_i - ds_i
    valid = (idx >= 0) & (idx < L)
    shared["toep"] = np.ascontiguousarray(
        np.where(valid[None], kern[:, np.clip(idx, 0, L - 1)], 0.0).astype(f32))
    shared["drep"] = np.ascontiguousarray(
        np.broadcast_to(np.asarray(inp["L0_attn_D"], f32)[None, :], (128, 128)))
    n2a = np.arange(32)[:, None, None]
    n1a = np.arange(64)[None, :, None]
    k1a = np.arange(64)[None, None, :]
    Wf = np.exp(-2j * np.pi * (n2a * k1a / 64.0 + n1a * k1a / 4096.0))
    wa2 = np.zeros((32, 16384), np.float64)
    for a_ in range(64):
        wa2[:, a_*256:a_*256+64] = Wf[:, a_, :].real
        wa2[:, a_*256+64:a_*256+128] = Wf[:, a_, :].imag
        wa2[:, a_*256+128:a_*256+192] = -Wf[:, a_, :].imag
        wa2[:, a_*256+192:a_*256+256] = Wf[:, a_, :].real
    shared["fft_wa2"] = wa2.astype(BF)
    k2a = np.arange(64)[:, None, None]
    n1b = np.arange(64)[None, None, :]
    k1b = np.arange(64)[None, :, None]
    Vf = np.exp(2j * np.pi * (k2a * n1b / 64.0 + n1b * k1b / 4096.0))
    va2 = np.zeros((64, 16384), np.float64)
    for a_ in range(64):
        va2[:, a_*256:a_*256+64] = Vf[:, a_, :].real
        va2[:, a_*256+64:a_*256+128] = Vf[:, a_, :].imag
        va2[:, a_*256+128:a_*256+192] = -Vf[:, a_, :].imag
        va2[:, a_*256+192:a_*256+256] = Vf[:, a_, :].real
    shared["fft_va2"] = va2.astype(BF)
    Bf = np.exp(-2j * np.pi * np.arange(64)[:, None] * np.arange(64)[None, :] / 64.0)
    shared["fft_wbst"] = np.ascontiguousarray(
        np.concatenate([Bf.real, Bf.imag, -Bf.imag, Bf.real], 1).astype(BF))
    Cf = np.exp(2j * np.pi * np.arange(64)[:, None] * np.arange(32)[None, :] / 64.0) / 4096.0
    shared["fft_va3"] = np.ascontiguousarray(
        np.concatenate([Cf.real, Cf.imag, -Cf.imag, Cf.real], 1).astype(BF))
    shared["idn_bf16"] = np.eye(128).astype(BF)
    p_i = np.arange(128)[:, None]; x_i = np.arange(512)[None, :]
    mB = np.zeros((128, 2048), f32)
    mA = np.zeros((128, 2048), f32)
    for c4 in range(4):
        mB[:, c4 * 512:(c4 + 1) * 512] = np.where(x_i <= c4 * 128 + p_i, 0.0, -1e9)
        mA[:, c4 * 512:(c4 + 1) * 512] = np.where(c4 * 128 + p_i <= x_i, 0.0, -1e9)
    shared["maskB"] = mB
    shared["maskA"] = mA
    return shared, per_core


def _launch(nc, in_maps):
    return run_bass_kernel_spmd(nc, in_maps, core_ids=list(range(NCORES))).results


PHASES = None


def build_fused():
    nc = bass.Bass("TRN2", target_bir_lowering=False, debug=False)
    fx = FX(nc, "fused")
    with ExitStack() as es:
        fx.open(es)
        for emit in (emit_p1, emit_p2, emit_p3, emit_p4, emit_p5,
                     emit_p6, emit_p7, emit_p8, emit_p9, emit_p10):
            emit(fx)
    return nc


def kernel(**inputs):
    shared, per_core = stage_inputs(inputs)
    keys = sorted(FUSED_INPUTS - {"uT", "u_tok"})
    in_maps = [{**{k: shared[k] for k in keys}, **per_core[c]} for c in range(NCORES)]
    res = _launch(build_fused(), in_maps)
    out = np.concatenate([np.asarray(res[c]["out_s"]) for c in range(NCORES)], 0)
    return out.astype(np.float32)


# version 10 (final)
# speedup vs baseline: 1.0700x; 1.0700x over previous
import numpy as np
import ml_dtypes
import concourse.bass as bass
import concourse.mybir as mybir
from concourse.ap import AP
from concourse.bass_utils import run_bass_kernel_spmd

F32 = mybir.dt.float32
BF16 = mybir.dt.bfloat16
I32 = mybir.dt.int32
AF = mybir.ActivationFunctionType
ALU = mybir.AluOpType
AXX = mybir.AxisListType.X
BF = ml_dtypes.bfloat16

NCORES = 8
BPC = 2
B, L, D = 16, 2048, 1024
NTOK = BPC * L
EPS_Q = 1e-8



from contextlib import ExitStack


class Alloc:

    def __init__(self, nc, es):
        self.nc, self.es = nc, es

    def sb(self, name, shape, dtype):
        return self.es.enter_context(self.nc.sbuf_tensor(name, shape, dtype))

    def ps(self, name, shape, dtype=F32):
        return self.es.enter_context(self.nc.psum_tensor(name, shape, dtype))

    def sem(self, name):
        return self.es.enter_context(self.nc.semaphore(name))

    def block(self):
        return self.es.enter_context(self.nc.Block())



_AW_OPS = {
    "tensor_tensor", "tensor_add", "tensor_sub", "tensor_mul", "tensor_max",
    "tensor_scalar", "tensor_single_scalar", "tensor_scalar_add", "tensor_scalar_sub",
    "tensor_scalar_mul", "tensor_scalar_max", "tensor_scalar_min", "tensor_relu",
    "tensor_copy", "reduce_sum", "reduce_max", "tensor_reduce", "reciprocal",
    "memset", "memzero", "activation", "copy", "mul", "add", "sign", "square",
    "sqrt", "engine_nop", "scalar_tensor_tensor", "select", "bn_stats", "bn_aggr",
    "transpose", "pool", "pool_max", "pool_avg", "iota",
}


class AutoWait:

    def __init__(self, eng, sem):
        self._e, self._sem = eng, sem
        self.n = 0

    def __getattr__(self, name):
        f = getattr(self._e, name)
        if name in _AW_OPS:
            def g(*a, **k):
                if self.n:
                    self._e.wait_ge(self._sem, self.n)
                self.n += 1
                return f(*a, **k)
            return g
        return f



class RSem:
    __slots__ = ("fx", "slot", "h", "base")

    def __init__(self, fx, slot):
        self.fx, self.slot = fx, slot
        self.h = fx.pool[slot]
        self.base = fx.count[slot]


class PSem:
    __slots__ = ("h", "base", "count")

    def __init__(self, h):
        self.h, self.base, self.count = h, 0, 0


class ROp:
    __slots__ = ("o",)

    def __init__(self, o):
        self.o = o

    def then_inc(self, rs, n):
        if isinstance(rs, RSem):
            rs.fx.count[rs.slot] += n
        else:
            rs.count += n
        return ROp(self.o.then_inc(rs.h, n))


class REng:

    def __init__(self, e):
        self._e = e

    def wait_ge(self, rs, v):
        self._e.wait_ge(rs.h, rs.base + v)

    def __getattr__(self, name):
        f = getattr(self._e, name)
        if not callable(f):
            return f

        def g(*a, **k):
            r = f(*a, **k)
            return ROp(r) if r is not None else r
        return g


class FBlock:
    def __init__(self, fx):
        self._fx = fx

    def __getattr__(self, engname):
        fx = self._fx
        dec = getattr(fx._block, engname)
        snap = list(fx._snap)

        drain = list(fx._drain)

        def mydec(fn):
            def wrapped(eng):
                for i, v in enumerate(snap):
                    if v > 0:
                        eng.wait_ge(fx.pool[i], v)
                for h, c in drain:
                    if c > 0:
                        eng.wait_ge(h, c)
                return fn(REng(eng))
            return dec(wrapped)
        return mydec


COMPUTE_SEMS = {"mm", "ve", "se", "ev"}


class FAlloc:
    def __init__(self, fx, es):
        self.fx, self.es = fx, es
        self._sidx = 0

    def sb(self, name, shape, dtype):
        return self.es.enter_context(
            self.fx.nc.sbuf_tensor(f"ph{self.fx._phno}_{name}", shape, dtype))

    def ps(self, name, shape, dtype=F32):
        return self.es.enter_context(
            self.fx.nc.psum_tensor(f"ph{self.fx._phno}_{name}", shape, dtype))

    def sem(self, name):
        if name.rstrip("0123456789") in COMPUTE_SEMS:
            i = self._sidx
            self._sidx += 1
            return RSem(self.fx, i)
        h = self.fx._ges.enter_context(
            self.fx.nc.semaphore(f"ph{self.fx._phno}_{name}"))
        ps = PSem(h)
        self.fx._phase_priv.append(ps)
        return ps

    def block(self):
        return FBlock(self.fx)


class _Phase:
    def __init__(self, fx):
        self.fx = fx

    def __enter__(self):
        self.fx._phno = getattr(self.fx, "_phno", -1) + 1
        self.fx._snap = list(self.fx.count)
        self.fx._drain = [(p.h, p.count) for p in self.fx._phase_priv]
        self.fx._phase_priv = []
        self.es = ExitStack()
        self.es.__enter__()
        return FAlloc(self.fx, self.es)

    def __exit__(self, *a):
        return self.es.__exit__(*a)


FUSED_INPUTS = {
    "uT", "u_tok", "wqkv0T", "bqkv0_row", "ssmkkT", "dkrep", "fft_wa2", "fft_va2", "fft_wbst", "fft_va3",
    "idn_bf16", "toep", "drep", "woT",
    "bo_row", "w1_0", "b1_0pt", "idn_f32", "w2_0", "b2_0pt", "qkvw1T", "qkvb1_pt",
    "bv1_row", "maskB", "maskA", "projT", "projb_row", "projb_rep", "w1T_1",
    "w1_1", "b1_1pt", "w2T_1", "w2_1", "b2_1rep",
}
FUSED_OUTPUTS = {"out_s"}


class FX:
    NPOOL = 4

    def __init__(self, nc, mode):
        self.nc = nc
        self.mode = mode
        self._dram = {}
        self.pool = []
        self.count = []
        self._snap = [0] * self.NPOOL
        self._drain = []
        self._phase_priv = []
        self._block = None

    def open(self, es):
        self._ges = es
        self.pool = [es.enter_context(self.nc.semaphore(f"gp{i}")) for i in range(self.NPOOL)]
        self.count = [0] * self.NPOOL
        self._block = es.enter_context(self.nc.Block())

    def dram(self, name, shape, dtype, kind=None):
        if name in self._dram:
            return self._dram[name]
        k = kind
        if self.mode == "fused" and kind is not None:
            if kind == "ExternalInput" and name in FUSED_INPUTS:
                k = "ExternalInput"
            elif kind == "ExternalOutput" and name in FUSED_OUTPUTS:
                k = "ExternalOutput"
            else:
                k = "Internal"
        h = (self.nc.dram_tensor(name, shape, dtype, kind=k) if k is not None
             else self.nc.dram_tensor(name, shape, dtype))
        self._dram[name] = h
        return h

    def phase(self):
        return _Phase(self)


def _standalone(emit):
    nc = bass.Bass("TRN2", target_bir_lowering=False, debug=False)
    fx = FX(nc, "standalone")
    with ExitStack() as es:
        fx.open(es)
        emit(fx)
    return nc


def ap_of(handle, offset, dims):
    return AP(handle, offset, [list(d) for d in dims])


def sb_ap(handle, offset, dims, base_part=0, nparts=None):
    nat = handle.ap().ap
    row = nat[0][0]
    np_ = nparts if nparts is not None else nat[0][1]
    a = AP(handle, base_part * row + offset, [[row, np_]] + [list(d) for d in dims])
    return a



def emit_p1(fx):
    uT = fx.dram("uT", [BPC, D, L], F32, kind="ExternalInput")
    wT = fx.dram("wqkv0T", [D, 3 * D], F32, kind="ExternalInput")
    brow = fx.dram("bqkv0_row", [1, 3 * D], F32, kind="ExternalInput")
    q_s = fx.dram("q_s", [BPC, L, D], BF16, kind="ExternalOutput")
    k_s = fx.dram("k_s", [BPC, L, D], BF16, kind="ExternalOutput")
    v_s = fx.dram("v_s", [BPC, L, D], BF16, kind="ExternalOutput")
    outs = [q_s, k_s, v_s]

    NT = L // 128
    chunks = [(b, tc) for b in range(BPC) for tc in range(NT)]
    NCH = len(chunks)

    with fx.phase() as al:
        w_sb = al.sb("w_sb", [128, 8 * 3 * D], BF16)
        brow_sb = al.sb("brow_sb", [1, 3 * D], BF16)
        ones_sb = al.sb("ones_sb", [1, 128], BF16)
        ut0 = al.sb("ut0", [128, 8 * 128], BF16)
        ut1 = al.sb("ut1", [128, 8 * 128], BF16)
        ev0 = al.sb("ev0", [128, 3 * D], BF16)
        ev1 = al.sb("ev1", [128, 3 * D], BF16)
        ps0 = al.ps("ps0", [128, 1536])
        ps1 = al.ps("ps1", [128, 1536])
        ini, ld, mm, ev = [al.sem(n) for n in ("ini", "ld", "mm", "ev")]
        lu = [al.sem(f"lu{j}") for j in range(2)]
        st = [al.sem(f"st{j}") for j in range(2)]
        block = al.block()
        uts = [ut0, ut1]
        evs = [ev0, ev1]
        pss = [(ps0, ps1)]

        @block.gpsimd
        def _(g):
            g.memset(ones_sb[:, :], 1.0).then_inc(ini, 1)
            src = ap_of(wT, 0, [[3 * D, 128], [128 * 3 * D, 8], [1, 3 * D]])
            g.dma_start(out=w_sb[:, :].rearrange("p (c n) -> p c n", c=8), in_=src).then_inc(ld, 16)
            g.wait_ge(ld, 16)
            g.dma_start(out=brow_sb[0:1, :], in_=brow.ap()[0:1, :]).then_inc(ld, 16)
            for i, (b, tc) in enumerate(chunks):
                if i >= 2:
                    g.wait_ge(mm, i - 1)
                src = ap_of(uT, b * D * L + tc * 128, [[L, 128], [128 * L, 8], [1, 128]])
                g.dma_start(out=uts[i % 2][:, :].rearrange("p (c n) -> p c n", c=8), in_=src).then_inc(lu[i % 2], 16)

        @block.tensor
        def _(t):
            t.wait_ge(ini, 1)
            t.wait_ge(ld, 32)
            for i, (b, tc) in enumerate(chunks):
                t.wait_ge(lu[i % 2], 16 * (i // 2 + 1))
                if i >= 1:
                    t.wait_ge(ev, 2 * i)
                ut = uts[i % 2]
                pa, pb = pss[0]
                for h, p in ((0, pa), (1, pb)):
                    for cc in range(3):
                        t.matmul(p[:, cc * 512:(cc + 1) * 512], ones_sb[0:1, :],
                                 brow_sb[0:1, h * 1536 + cc * 512:h * 1536 + (cc + 1) * 512],
                                 start=True, stop=False)
                last = None
                for dc in range(8):
                    lhsT = sb_ap(ut, dc * 128, [[1, 128]])
                    for h, p in ((0, pa), (1, pb)):
                        for cc in range(3):
                            last = t.matmul(p[:, cc * 512:(cc + 1) * 512], lhsT,
                                            sb_ap(w_sb, dc * 3 * D + h * 1536 + cc * 512, [[1, 512]]),
                                            start=False, stop=(dc == 7))
                last.then_inc(mm, 1)

        @block.scalar
        def _(s):
            s = AutoWait(s, ev)
            for i in range(NCH):
                s.wait_ge(mm, i + 1)
                if i >= 2:
                    s.wait_ge(st[i % 2], 48 * (i // 2))
                pa, pb = pss[0]
                e = evs[i % 2]
                s.copy(e[:, 0:1536], pa[:, :]).then_inc(ev, 1)
                s.copy(e[:, 1536:3072], pb[:, :]).then_inc(ev, 1)

        @block.sync
        def _(sy):
            for i, (b, tc) in enumerate(chunks):
                sy.wait_ge(ev, 2 * i + 2)
                e = evs[i % 2]
                for oi, o in enumerate(outs):
                    dst = ap_of(o, b * L * D + tc * 128 * D, [[D, 128], [1, D]])
                    sy.dma_start(out=dst, in_=e[:, oi * D:(oi + 1) * D]).then_inc(st[i % 2], 16)
            sy.wait_ge(st[0], 48 * (NCH // 2))
            sy.wait_ge(st[1], 48 * (NCH // 2))


def build_p1():
    return _standalone(emit_p1)



def emit_p4(fx):
    y_s = fx.dram("y_s", [BPC, L, D], BF16, kind="ExternalInput")
    woT = fx.dram("woT", [D, D], F32, kind="ExternalInput")
    borow = fx.dram("bo_row", [1, D], F32, kind="ExternalInput")
    u_tok = fx.dram("u_tok", [BPC, L, D], F32, kind="ExternalInput")
    x1_s = fx.dram("x1_s", [BPC, L, D], BF16, kind="ExternalOutput")
    x1max = fx.dram("x1max_local", [128, 16], F32, kind="ExternalOutput")

    NT = L // 128
    chunks = [(b, tc) for b in range(BPC) for tc in range(NT)]
    NCH = len(chunks)
    VPC, SPC = 10, 4

    with fx.phase() as al:
        yt_sb = al.sb("yt_sb", [128, 8 * BPC * L], BF16)
        wo_sb = al.sb("wo_sb", [128, 8 * D], BF16)
        borow_sb = al.sb("borow_sb", [1, D], BF16)
        ones_sb = al.sb("ones_sb", [1, 128], BF16)
        ut0 = al.sb("ut0", [128, D], F32)
        ut1 = al.sb("ut1", [128, D], F32)
        t1_0 = al.sb("t1_0", [128, D], F32)
        t1_1 = al.sb("t1_1", [128, D], F32)
        xo0 = al.sb("xo0", [128, D], BF16)
        xo1 = al.sb("xo1", [128, D], BF16)
        srx = al.sb("srx", [128, 16], F32)
        maxcol = al.sb("maxcol", [128, NCH + 1], F32)
        maxb = al.sb("maxb", [128, 16], F32)
        pmA = al.ps("pmA", [128, D])
        pmB = al.ps("pmB", [128, D])
        trash = al.ps("trash", [128, D])
        ini, ld, lw4, mm, ve, se = [al.sem(n) for n in ("ini", "ld", "lw4", "mm", "ve", "se")]
        lu = [al.sem(f"lu4{j}") for j in range(2)]
        st = [al.sem(f"st4{j}") for j in range(2)]
        block = al.block()
        uts = [ut0, ut1]
        t1s = [t1_0, t1_1]
        xos = [xo0, xo1]
        pms = [pmA, pmB]

        @block.sync
        def _(sy):
            for b in range(BPC):
                for dc in range(8):
                    dst = sb_ap(yt_sb, dc * BPC * L + b * L, [[1, L]])
                    sy.dma_start(out=dst, in_=y_s.ap()[b, :, dc * 128:(dc + 1) * 128], transpose=True).then_inc(ld, 16)
            for i in range(NCH):
                b, tc = chunks[i]
                sy.wait_ge(se, SPC * i + 4)
                sy.dma_start(out=ap_of(x1_s, b * L * D + tc * 128 * D, [[D, 128], [1, D]]),
                             in_=xos[i % 2][:, :]).then_inc(st[i % 2], 16)
            sy.wait_ge(ve, VPC * NCH + 2)
            sy.dma_start(out=x1max.ap()[:, :], in_=maxb[:, :]).then_inc(st[0], 16)
            sy.wait_ge(st[0], 16 * (NCH // 2 + 1))
            sy.wait_ge(st[1], 16 * (NCH // 2))

        @block.gpsimd
        def _(g):
            g.memset(ones_sb[:, :], 1.0).then_inc(ini, 1)
            src = ap_of(woT, 0, [[D, 128], [128 * D, 8], [1, D]])
            g.dma_start(out=wo_sb[:, :].rearrange("p (c n) -> p c n", c=8), in_=src).then_inc(lw4, 16)
            g.wait_ge(lw4, 16)
            g.dma_start(out=borow_sb[0:1, :], in_=borow.ap()[0:1, :]).then_inc(lw4, 16)
            for i, (b, tc) in enumerate(chunks):
                if i >= 2:
                    g.wait_ge(ve, VPC * (i - 2) + 1)
                g.dma_start(out=uts[i % 2][:, :],
                            in_=ap_of(u_tok, b * L * D + tc * 128 * D, [[D, 128], [1, D]])).then_inc(lu[i % 2], 16)

        @block.tensor
        def _(t):
            t.wait_ge(ini, 1)
            t.wait_ge(ld, 16 * 16)
            t.wait_ge(lw4, 32)
            for i, (b, tc) in enumerate(chunks):
                if i >= 2:
                    t.wait_ge(ve, VPC * (i - 2) + 1)
                p = pms[i % 2]
                t.matmul(p[:, 0:512], ones_sb[0:1, :], borow_sb[0:1, 0:512], start=True, stop=False)
                t.matmul(p[:, 512:1024], ones_sb[0:1, :], borow_sb[0:1, 512:1024], start=True, stop=False)
                last = None
                for dc in range(8):
                    lhsT = sb_ap(yt_sb, dc * BPC * L + b * L + tc * 128, [[1, 128]])
                    for cc in range(2):
                        last = t.matmul(p[:, cc * 512:(cc + 1) * 512], lhsT,
                                        sb_ap(wo_sb, dc * D + cc * 512, [[1, 512]]),
                                        start=False, stop=(dc == 7))
                last.then_inc(mm, 1)

        @block.vector
        def _(v):
            v = AutoWait(v, ve)
            for i in range(NCH):
                vb, sb_, = VPC * i, SPC * i
                v.wait_ge(mm, i + 1)
                v.wait_ge(lu[i % 2], 16 * (i // 2 + 1))
                if i >= 1:
                    v.wait_ge(se, SPC * i)
                p, t1 = pms[i % 2], t1s[i % 2]
                v.tensor_add(t1[:, :], p[:, :], uts[i % 2][:, :]).then_inc(ve, 1)
                v.wait_ge(ve, vb + 1)
                v.reduce_sum(srx[:, 0:1], t1[:, :], axis=AXX).then_inc(ve, 1)
                v.wait_ge(ve, vb + 2)
                v.tensor_scalar_mul(srx[:, 2:3], srx[:, 0:1], 1.0 / D).then_inc(ve, 1)
                v.wait_ge(ve, vb + 3)
                v.tensor_mul(srx[:, 3:4], srx[:, 2:3], srx[:, 2:3]).then_inc(ve, 1)
                v.wait_ge(se, sb_ + 1)
                v.tensor_scalar_mul(srx[:, 4:5], srx[:, 1:2], 1.0 / D).then_inc(ve, 1)
                v.wait_ge(ve, vb + 5)
                v.tensor_sub(srx[:, 5:6], srx[:, 4:5], srx[:, 3:4]).then_inc(ve, 1)
                v.wait_ge(ve, vb + 6)
                v.tensor_scalar_add(srx[:, 6:7], srx[:, 5:6], 1e-5).then_inc(ve, 1)
                v.wait_ge(se, sb_ + 3)
                v.tensor_mul(srx[:, 8:9], srx[:, 7:8], srx[:, 2:3]).then_inc(ve, 1)
                v.wait_ge(ve, vb + 8)
                v.tensor_scalar_mul(srx[:, 9:10], srx[:, 8:9], -1.0).then_inc(ve, 1)
                v.wait_ge(se, sb_ + 4)
                v.reduce_max(maxcol[:, i:i + 1], xos[i % 2][:, :], axis=AXX).then_inc(ve, 1)
            v.wait_ge(ve, VPC * NCH)
            v.reduce_max(maxcol[:, NCH:NCH + 1], maxcol[:, 0:NCH], axis=AXX).then_inc(ve, 1)
            v.wait_ge(ve, VPC * NCH + 1)
            v.tensor_copy(maxb[:, :], sb_ap(maxcol, NCH, [[0, 16]])).then_inc(ve, 1)

        @block.scalar
        def _(s):
            s = AutoWait(s, se)
            for i in range(NCH):
                vb, sb_ = VPC * i, SPC * i
                s.wait_ge(ve, vb + 1)
                t1 = t1s[i % 2]
                s.activation(trash[:, :], t1[:, :], AF.Square, accum_out=srx[:, 1:2]).then_inc(se, 1)
                s.wait_ge(ve, vb + 7)
                s.activation(srx[:, 7:8], srx[:, 6:7], AF.Ln).then_inc(se, 1)
                s.wait_ge(se, sb_ + 2)
                s.activation(srx[:, 7:8], srx[:, 7:8], AF.Exp, scale=-0.5).then_inc(se, 1)
                s.wait_ge(ve, vb + 9)
                if i >= 2:
                    s.wait_ge(st[i % 2], 16 * (i // 2))
                s.activation(xos[i % 2][:, :], t1[:, :], AF.Identity,
                             scale=srx[:, 7:8], bias=srx[:, 9:10]).then_inc(se, 1)

def build_p4():
    return _standalone(emit_p4)



def emit_p5(fx):
    x1_s = fx.dram("x1_s", [BPC, L, D], BF16, kind="ExternalInput")
    x1max = fx.dram("x1max_local", [128, 16], F32, kind="ExternalInput")
    w1 = fx.dram("w1_0", [4 * D, D], F32, kind="ExternalInput")
    b1pt = fx.dram("b1_0pt", [128, 32], F32, kind="ExternalInput")
    idn = fx.dram("idn_f32", [128, 128], F32, kind="ExternalInput")
    h_s = fx.dram("h_s", [4 * D, NTOK], BF16, kind="ExternalOutput")
    hmax = fx.dram("hmax_local", [128, 16], F32, kind="ExternalOutput")
    wqb = fx.dram("w1q_bounce", [4 * D, D], BF16)
    cc_in = fx.dram("cc_in5", [128, 16], F32)
    cc_out = fx.dram("cc_out5", [128, 16], F32)
    F = 4 * D
    NFC = 32

    with fx.phase() as al:
        mx_sb = al.sb("mx_sb", [128, 16], F32)
        row_sb = al.sb("row_sb", [1, 128], F32)
        sc_sb = al.sb("sc_sb", [1, 8], F32)
        onesf = al.sb("onesf", [1, 128], F32)
        idn_sb = al.sb("idn_sb", [128, 128], F32)
        s1pp = al.sb("s1pp", [128, 1], F32)
        rs1pp = al.sb("rs1pp", [128, 1], F32)
        rsw = al.sb("rsw", [128, 1], F32)
        rs1sw = al.sb("rs1sw", [128, 32], F32)
        b1pt_sb = al.sb("b1pt_sb", [128, 32], F32)
        wna0 = al.sb("wna0", [128, D], F32)
        wna1 = al.sb("wna1", [128, D], F32)
        wdiv = al.sb("wdiv", [128, D], F32)
        wi32 = al.sb("wi32", [128, D], I32)
        wbf0 = al.sb("wbf0", [128, D], BF16)
        wbf1 = al.sb("wbf1", [128, D], BF16)
        swcol = al.sb("swcol", [128, 32], F32)
        s1sw = al.sb("s1sw", [128, 32], F32)
        bq1 = al.sb("bq1", [128, 32], F32)
        bscr = al.sb("bscr", [128, 32], F32)
        bi32 = al.sb("bi32", [128, 32], I32)
        wqT = al.sb("wqT", [128, 8 * F], BF16)
        xqT = al.sb("xqT", [128, 8 * NTOK], BF16)
        xscr = al.sb("xscr", [128, 2048], F32)
        xi32 = al.sb("xi32", [128, 2048], I32)
        ht = [al.sb(f"ht{j}", [128, 512], BF16) for j in range(4)]
        hmaxcol = al.sb("hmaxcol", [128, 257], F32)
        hmaxb = al.sb("hmaxb", [128, 16], F32)
        pscr = al.ps("pscr", [128, 128])
        hp = [al.ps(f"hp{j}", [128, 512]) for j in range(4)]
        ini, ld, ltx, ltw, mm, ve, se, cc, ccd = [al.sem(n) for n in
            ("ini", "ld", "ltx", "ltw", "mm", "ve", "se", "cc", "ccd")]
        lw = [al.sem(f"lw5{j}") for j in range(2)]
        bs = [al.sem(f"bs5{j}") for j in range(2)]
        st = [al.sem(f"st5{j}") for j in range(4)]
        block = al.block()

        V_S0, V_W, V_SC, V_X = 5, 6, 6, 4
        v_after_w = V_S0 + NFC * V_W
        v_after_sc = v_after_w + V_SC
        v_after_x = v_after_sc + 16 * V_X
        T_S0 = 2
        NTILE = NFC * 8

        @block.gpsimd
        def _(g):
            g.memset(onesf[:, :], 1.0).then_inc(ini, 1)
            g.dma_start(out=cc_in.ap()[:, :], in_=x1max.ap()[:, :]).then_inc(cc, 16)
            g.wait_ge(cc, 16)
            g.collective_compute("AllReduce", ALU.max, replica_groups=[list(range(NCORES))],
                                 ins=[cc_in.ap().opt()], outs=[cc_out.ap().opt()]).then_inc(ccd, 1)
            g.wait_ge(ccd, 1)
            g.dma_start(out=mx_sb[:, :], in_=cc_out.ap()[:, :]).then_inc(ld, 16)
            g.wait_ge(ld, 16)
            g.dma_start(out=idn_sb[:, :], in_=idn.ap()[:, :]).then_inc(ld, 16)
            g.wait_ge(ld, 32)
            g.dma_start(out=b1pt_sb[:, :], in_=b1pt.ap()[:, :]).then_inc(ld, 16)
            for fc in range(NFC):
                if fc >= 2:
                    g.wait_ge(ve, V_S0 + (fc - 1) * V_W)
                g.dma_start(out=[wna0, wna1][fc % 2][:, :],
                            in_=w1.ap()[fc * 128:(fc + 1) * 128, :]).then_inc(lw[fc % 2], 16)
                g.wait_ge(ve, V_S0 + fc * V_W + 6)
                g.dma_start(out=wqb.ap()[fc * 128:(fc + 1) * 128, :],
                            in_=[wbf0, wbf1][fc % 2][:, :]).then_inc(bs[fc % 2], 16)

        @block.sync
        def _(sy):
            for b in range(BPC):
                for dc in range(8):
                    dst = sb_ap(xqT, dc * NTOK + b * L, [[1, L]])
                    sy.dma_start(out=dst, in_=x1_s.ap()[b, :, dc * 128:(dc + 1) * 128],
                                 transpose=True).then_inc(ltx, 16)
            sy.wait_ge(bs[0], 16 * (NFC // 2))
            sy.wait_ge(bs[1], 16 * (NFC // 2))
            for dc in range(8):
                dst = sb_ap(wqT, dc * F, [[1, F]])
                sy.dma_start(out=dst, in_=wqb.ap()[:, dc * 128:(dc + 1) * 128],
                             transpose=True).then_inc(ltw, 16)
            for i in range(NTILE):
                sy.wait_ge(se, i + 1)
                fc, th, tt = i // 8, (i % 8) // 4, i % 4
                col = th * 2048 + tt * 512
                sy.dma_start(out=ap_of(h_s, fc * 128 * NTOK + col, [[NTOK, 128], [1, 512]]),
                             in_=ht[i % 4][:, :]).then_inc(st[i % 4], 16)
            sy.wait_ge(ve, v_after_x + NTILE + 2)
            sy.dma_start(out=hmax.ap()[:, :], in_=hmaxb[:, :]).then_inc(st[0], 16)
            sy.wait_ge(st[0], 16 * (NTILE // 4 + 1))
            for j in range(1, 4):
                sy.wait_ge(st[j], 16 * (NTILE // 4))

        @block.tensor
        def _(t):
            t.wait_ge(ini, 1)
            t.wait_ge(ld, 48)
            t.transpose(pscr[0:1, 0:128], mx_sb[:, 0:1], idn_sb[:, :]).then_inc(mm, 1)
            t.wait_ge(ve, 3)
            t.matmul(pscr[:, 0:1], onesf[0:1, :], sc_sb[0:1, 1:2], start=True, stop=True).then_inc(mm, 1)
            t.wait_ge(ltx, 16 * 16)
            t.wait_ge(ltw, 16 * 8)
            t.wait_ge(ve, v_after_x)
            n = 0
            for fc in range(NFC):
                for th in range(2):
                    for dc in range(8):
                        lhsT = sb_ap(wqT, dc * F + fc * 128, [[1, 128]])
                        for tt in range(4):
                            nt = fc * 8 + th * 4 + tt
                            if dc == 0 and nt >= 4:
                                t.wait_ge(se, nt - 3)
                            last = t.matmul(hp[tt][:, :], lhsT,
                                            sb_ap(xqT, dc * NTOK + th * 2048 + tt * 512, [[1, 512]]),
                                            start=(dc == 0), stop=(dc == 7))
                            if dc == 7:
                                last.then_inc(mm, 1)

        @block.vector
        def _(v):
            v = AutoWait(v, ve)
            v.wait_ge(mm, 1)
            v.tensor_copy(row_sb[0:1, :], pscr[0:1, 0:128]).then_inc(ve, 1)
            v.wait_ge(ve, 1)
            v.reduce_max(sc_sb[0:1, 0:1], row_sb[0:1, :], axis=AXX).then_inc(ve, 1)
            v.wait_ge(ve, 2)
            v.tensor_scalar(sc_sb[0:1, 1:2], sc_sb[0:1, 0:1], EPS_Q, 1.0 / 255.0,
                            ALU.max, ALU.mult).then_inc(ve, 1)
            v.wait_ge(mm, 2)
            v.tensor_copy(s1pp[:, :], pscr[:, 0:1]).then_inc(ve, 1)
            v.wait_ge(ve, 4)
            v.reciprocal(rs1pp[:, :], s1pp[:, :]).then_inc(ve, 1)
            for fc in range(NFC):
                vb = V_S0 + fc * V_W
                v.wait_ge(lw[fc % 2], 16 * (fc // 2 + 1))
                wna = [wna0, wna1][fc % 2]
                v.reduce_max(swcol[:, fc:fc + 1], wna[:, :], axis=AXX,
                             apply_absolute_value=True).then_inc(ve, 1)
                v.wait_ge(ve, vb + 1)
                v.tensor_scalar(swcol[:, fc:fc + 1], swcol[:, fc:fc + 1], EPS_Q, 1.0 / 127.0,
                                ALU.max, ALU.mult).then_inc(ve, 1)
                v.wait_ge(ve, vb + 2)
                v.reciprocal(rsw[:, :], swcol[:, fc:fc + 1]).then_inc(ve, 1)
                v.wait_ge(ve, vb + 3)
                v.tensor_scalar(wdiv[:, :], wna[:, :], rsw[:, 0:1], None,
                                ALU.mult).then_inc(ve, 1)
                v.wait_ge(ve, vb + 4)
                v.tensor_copy(wi32[:, :], wdiv[:, :]).then_inc(ve, 1)
                v.wait_ge(ve, vb + 5)
                if fc >= 2:
                    v.wait_ge(bs[fc % 2], 16 * (fc // 2))
                v.tensor_copy([wbf0, wbf1][fc % 2][:, :], wi32[:, :]).then_inc(ve, 1)
            vb = v_after_w
            v.wait_ge(ld, 48)
            v.tensor_tensor(s1sw[:, :], swcol[:, :], sb_ap(s1pp, 0, [[0, 32]]),
                            ALU.mult).then_inc(ve, 1)
            v.wait_ge(ve, vb + 1)
            v.reciprocal(rs1sw[:, :], s1sw[:, :]).then_inc(ve, 1)
            v.wait_ge(ve, vb + 2)
            v.tensor_tensor(bscr[:, :], b1pt_sb[:, :], rs1sw[:, :], ALU.mult).then_inc(ve, 1)
            v.wait_ge(ve, vb + 3)
            v.tensor_copy(bi32[:, :], bscr[:, :]).then_inc(ve, 1)
            v.wait_ge(ve, vb + 4)
            v.tensor_copy(bscr[:, :], bi32[:, :]).then_inc(ve, 1)
            v.wait_ge(ve, vb + 5)
            v.tensor_tensor(bq1[:, :], bscr[:, :], s1sw[:, :], ALU.mult).then_inc(ve, 1)
            for xt in range(16):
                vb = v_after_sc + xt * V_X
                off = (xt // 2) * NTOK + (xt % 2) * L
                v.wait_ge(ltx, 16 * 16)
                v.tensor_scalar(xscr[:, :], sb_ap(xqT, off, [[1, L]]), rs1pp[:, 0:1], None,
                                ALU.mult).then_inc(ve, 1)
                v.wait_ge(ve, vb + 1)
                v.tensor_copy(xi32[:, :], xscr[:, :]).then_inc(ve, 1)
                v.wait_ge(ve, vb + 2)
                v.tensor_copy(xscr[:, :], xi32[:, :]).then_inc(ve, 1)
                v.wait_ge(ve, vb + 3)
                v.tensor_scalar(sb_ap(xqT, off, [[1, L]]), xscr[:, :], 255.0, 0.0,
                                ALU.min, ALU.max).then_inc(ve, 1)
            for i in range(NTILE):
                v.wait_ge(se, i + 1)
                v.reduce_max(hmaxcol[:, i:i + 1], ht[i % 4][:, :], axis=AXX).then_inc(ve, 1)
            vb = v_after_x + NTILE
            v.wait_ge(ve, vb)
            v.reduce_max(hmaxcol[:, NTILE:NTILE + 1], hmaxcol[:, 0:NTILE], axis=AXX).then_inc(ve, 1)
            v.wait_ge(ve, vb + 1)
            v.tensor_copy(hmaxb[:, :], sb_ap(hmaxcol, NTILE, [[0, 16]])).then_inc(ve, 1)

        @block.scalar
        def _(s):
            s = AutoWait(s, se)
            for i in range(NTILE):
                fc = i // 8
                s.wait_ge(mm, T_S0 + i + 1)
                if i >= 4:
                    s.wait_ge(st[i % 4], 16 * (i // 4))
                    s.wait_ge(ve, v_after_x + (i - 4) + 1)
                s.activation(ht[i % 4][:, :], hp[i % 4][:, :], AF.Relu,
                             scale=s1sw[:, fc:fc + 1], bias=bq1[:, fc:fc + 1]).then_inc(se, 1)


def build_p5():
    return _standalone(emit_p5)



def emit_p6(fx):
    h_s = fx.dram("h_s", [4 * D, NTOK], BF16, kind="ExternalInput")
    hmax = fx.dram("hmax_local", [128, 16], F32, kind="ExternalInput")
    w2 = fx.dram("w2_0", [D, 4 * D], F32, kind="ExternalInput")
    b2pt = fx.dram("b2_0pt", [128, 8], F32, kind="ExternalInput")
    idn = fx.dram("idn_f32", [128, 128], F32, kind="ExternalInput")
    x1_s = fx.dram("x1_s", [BPC, L, D], BF16, kind="ExternalInput")
    x2_s = fx.dram("x2_s", [BPC, L, D], BF16, kind="ExternalOutput")
    wqb = fx.dram("w2q_bounce", [D, 4 * D], BF16)
    rowb = fx.dram("rowb6", [16, 128], F32)
    cc_in = fx.dram("cc_in6", [128, 16], F32)
    cc_out = fx.dram("cc_out6", [128, 16], F32)
    F = 4 * D
    NTC = NTOK // 128

    V0, V_W2, V_SC2 = 5, 9, 8
    NWC = 8
    vq_done, vln = {}, {}
    vC = V0 + NWC * V_W2 + V_SC2 + 1 + 1
    vC = V0 + NWC * V_W2 + V_SC2 + 3
    v_pre = vC
    VQ, VLN = 8, 11
    vq_done[0] = vC + VQ
    vC += VQ
    for tc in range(NTC):
        if tc + 1 < NTC:
            vq_done[tc + 1] = vC + VQ
            vC += VQ
        vln[tc] = (vC, vC + VLN)
        vC += VLN
    V_TOTAL = vC

    with fx.phase() as al:
        mx_sb = al.sb("mx_sb", [128, 16], F32)
        row_sb = al.sb("row_sb", [1, 128], F32)
        sc_sb = al.sb("sc_sb", [1, 8], F32)
        onesf = al.sb("onesf", [1, 128], F32)
        idn_sb = al.sb("idn_sb", [128, 128], F32)
        s2pp = al.sb("s2pp", [128, 1], F32)
        rs2pp = al.sb("rs2pp", [128, 1], F32)
        b2pt_sb = al.sb("b2pt_sb", [128, 8], F32)
        wna = al.sb("wna", [128, F], F32)
        wdiv = al.sb("wdiv", [128, F // 2], F32)
        wi32 = al.sb("wi32", [128, F // 2], I32)
        wbf = al.sb("wbf", [128, F], BF16)
        swcol = al.sb("swcol", [128, 8], F32)
        rsw = al.sb("rsw", [128, 1], F32)
        s2sw = al.sb("s2sw", [128, 8], F32)
        rs2sw = al.sb("rs2sw", [128, 8], F32)
        bq2 = al.sb("bq2", [128, 8], F32)
        bscr = al.sb("bscr", [128, 8], F32)
        bi32 = al.sb("bi32", [128, 8], I32)
        comb = al.sb("comb", [128, 16], F32)
        combT = al.sb("combT", [16, 128], F32)
        rows2 = al.sb("rows2", [1, 2048], F32)
        scale2rep = al.sb("scale2rep", [128, D], F32)
        bq2rep = al.sb("bq2rep", [128, D], F32)
        w2T_sb = al.sb("w2T_sb", [128, 32 * D], BF16)
        hstrip = [al.sb(f"hstrip{j}", [128, F], BF16) for j in range(2)]
        hscr = al.sb("hscr", [128, F // 2], F32)
        hi32 = al.sb("hi32", [128, F // 2], I32)
        hq = [al.sb(f"hq{j}", [128, F], BF16) for j in range(2)]
        x1t = [al.sb(f"x1t{j}", [128, D], BF16) for j in range(2)]
        t1_ = [al.sb(f"t6a{j}", [128, D], F32) for j in range(2)]
        t2_ = al.sb("t6b", [128, D], F32)
        xo = [al.sb(f"xo6{j}", [128, D], BF16) for j in range(2)]
        srx = al.sb("srx6", [128, 16], F32)
        pscr = al.ps("pscr6", [128, 128])
        prow = al.ps("prow6", [128, D])
        po = [al.ps(f"po6{j}", [128, D]) for j in range(2)]
        trash = prow
        (ini, cc, ccd, rbs, ld, lw, bs, lt, mm, ve, se) = [al.sem(n) for n in
            ("ini", "cc", "ccd", "rbs", "ld", "lw", "bs", "lt", "mm", "ve", "se")]
        lh = [al.sem(f"lh6{j}") for j in range(2)]
        lx = [al.sem(f"lx6{j}") for j in range(2)]
        st = [al.sem(f"st6{j}") for j in range(2)]
        block = al.block()

        @block.gpsimd
        def _(g):
            g.memset(onesf[:, :], 1.0).then_inc(ini, 1)
            g.dma_start(out=cc_in.ap()[:, :], in_=hmax.ap()[:, :]).then_inc(cc, 16)
            g.wait_ge(cc, 16)
            g.collective_compute("AllReduce", ALU.max, replica_groups=[list(range(NCORES))],
                                 ins=[cc_in.ap().opt()], outs=[cc_out.ap().opt()]).then_inc(ccd, 1)
            g.wait_ge(ccd, 1)
            g.dma_start(out=mx_sb[:, :], in_=cc_out.ap()[:, :]).then_inc(ld, 16)
            g.wait_ge(ld, 16)
            g.dma_start(out=idn_sb[:, :], in_=idn.ap()[:, :]).then_inc(ld, 16)
            g.wait_ge(ld, 32)
            g.dma_start(out=b2pt_sb[:, :], in_=b2pt.ap()[:, :]).then_inc(ld, 16)
            for wc in range(NWC):
                if wc >= 1:
                    g.wait_ge(ve, V0 + wc * V_W2)
                g.dma_start(out=wna[:, :],
                            in_=w2.ap()[wc * 128:(wc + 1) * 128, :]).then_inc(lw, 16)
                g.wait_ge(ve, V0 + wc * V_W2 + 9)
                g.dma_start(out=wqb.ap()[wc * 128:(wc + 1) * 128, :],
                            in_=wbf[:, :]).then_inc(bs, 16)
            g.wait_ge(ve, v_pre - 2)
            g.dma_start(out=rowb.ap()[:, :], in_=combT[:, :]).then_inc(rbs, 16)
            g.wait_ge(rbs, 16)
            g.dma_start(out=rows2[0:1, :], in_=ap_of(rowb, 0, [[2048, 1], [1, 2048]])).then_inc(rbs, 16)
            for tc in range(NTC):
                if tc >= 2:
                    g.wait_ge(ve, vq_done[tc - 1])
                g.dma_start(out=hstrip[tc % 2][:, :].rearrange("p (c n) -> p c n", c=32),
                            in_=ap_of(h_s, tc * 128, [[NTOK, 128], [128 * NTOK, 32], [1, 128]])).then_inc(lh[tc % 2], 16)
                if tc >= 2:
                    g.wait_ge(ve, vln[tc - 2][0] + 3)
                b, t_ = tc // (NTC // BPC), tc % (NTC // BPC)
                g.dma_start(out=x1t[tc % 2][:, :],
                            in_=ap_of(x1_s, b * L * D + t_ * 128 * D, [[D, 128], [1, D]])).then_inc(lx[tc % 2], 16)

        @block.sync
        def _(sy):
            sy.wait_ge(bs, 16 * NWC)
            for fc in range(32):
                dst = sb_ap(w2T_sb, fc * D, [[1, D]])
                sy.dma_start(out=dst, in_=wqb.ap()[:, fc * 128:(fc + 1) * 128],
                             transpose=True).then_inc(lt, 16)
            for tc in range(NTC):
                b, t_ = tc // (NTC // BPC), tc % (NTC // BPC)
                sy.wait_ge(se, 4 * tc + 4)
                sy.dma_start(out=ap_of(x2_s, b * L * D + t_ * 128 * D, [[D, 128], [1, D]]),
                             in_=xo[tc % 2][:, :]).then_inc(st[tc % 2], 16)
            sy.wait_ge(st[0], 16 * (NTC // 2))
            sy.wait_ge(st[1], 16 * (NTC // 2))

        @block.tensor
        def _(t):
            t.wait_ge(ini, 1)
            t.wait_ge(ld, 48)
            t.transpose(pscr[0:1, 0:128], mx_sb[:, 0:1], idn_sb[:, :]).then_inc(mm, 1)
            t.wait_ge(ve, 3)
            t.matmul(pscr[:, 0:1], onesf[0:1, :], sc_sb[0:1, 1:2], start=True, stop=True).then_inc(mm, 1)
            t.wait_ge(ve, v_pre - 3 + 0)
            t.transpose(pscr[0:16, 0:128], comb[:, 0:16], idn_sb[:, :]).then_inc(mm, 1)
            t.wait_ge(rbs, 32)
            t.matmul(prow[:, 0:512], onesf[0:1, :], rows2[0:1, 0:512], start=True, stop=True)
            t.matmul(prow[:, 512:1024], onesf[0:1, :], rows2[0:1, 512:1024], start=True, stop=True).then_inc(mm, 1)
            t.wait_ge(ve, v_pre - 1)
            t.matmul(prow[:, 0:512], onesf[0:1, :], rows2[0:1, 1024:1536], start=True, stop=True)
            t.matmul(prow[:, 512:1024], onesf[0:1, :], rows2[0:1, 1536:2048], start=True, stop=True).then_inc(mm, 1)
            t.wait_ge(lt, 16 * 32)
            for tc in range(NTC):
                t.wait_ge(ve, vq_done[tc])
                if tc >= 2:
                    t.wait_ge(ve, vln[tc - 2][0] + 1)
                last = None
                for fc in range(32):
                    lhsT = sb_ap(hq[tc % 2], fc * 128, [[1, 128]])
                    for cchunk in range(2):
                        last = t.matmul(po[tc % 2][:, cchunk * 512:(cchunk + 1) * 512], lhsT,
                                        sb_ap(w2T_sb, fc * D + cchunk * 512, [[1, 512]]),
                                        start=(fc == 0), stop=(fc == 31))
                last.then_inc(mm, 1)

        @block.vector
        def _(v):
            v = AutoWait(v, ve)
            v.wait_ge(mm, 1)
            v.tensor_copy(row_sb[0:1, :], pscr[0:1, 0:128]).then_inc(ve, 1)
            v.wait_ge(ve, 1)
            v.reduce_max(sc_sb[0:1, 0:1], row_sb[0:1, :], axis=AXX).then_inc(ve, 1)
            v.wait_ge(ve, 2)
            v.tensor_scalar(sc_sb[0:1, 1:2], sc_sb[0:1, 0:1], EPS_Q, 1.0 / 255.0,
                            ALU.max, ALU.mult).then_inc(ve, 1)
            v.wait_ge(mm, 2)
            v.tensor_copy(s2pp[:, :], pscr[:, 0:1]).then_inc(ve, 1)
            v.wait_ge(ve, 4)
            v.reciprocal(rs2pp[:, :], s2pp[:, :]).then_inc(ve, 1)
            for wc in range(NWC):
                vb = V0 + wc * V_W2
                v.wait_ge(lw, 16 * (wc + 1))
                v.reduce_max(swcol[:, wc:wc + 1], wna[:, :], axis=AXX,
                             apply_absolute_value=True).then_inc(ve, 1)
                v.wait_ge(ve, vb + 1)
                v.tensor_scalar(swcol[:, wc:wc + 1], swcol[:, wc:wc + 1], EPS_Q, 1.0 / 127.0,
                                ALU.max, ALU.mult).then_inc(ve, 1)
                v.wait_ge(ve, vb + 2)
                v.reciprocal(rsw[:, :], swcol[:, wc:wc + 1]).then_inc(ve, 1)
                for hh in range(2):
                    o = hh * (F // 2)
                    v.wait_ge(ve, vb + 3 + 3 * hh)
                    v.tensor_scalar(wdiv[:, :], wna[:, o:o + F // 2], rsw[:, 0:1], None,
                                    ALU.mult).then_inc(ve, 1)
                    v.wait_ge(ve, vb + 4 + 3 * hh)
                    v.tensor_copy(wi32[:, :], wdiv[:, :]).then_inc(ve, 1)
                    v.wait_ge(ve, vb + 5 + 3 * hh)
                    if wc >= 1 and hh == 0:
                        v.wait_ge(bs, 16 * wc)
                    v.tensor_copy(wbf[:, o:o + F // 2], wi32[:, :]).then_inc(ve, 1)
            vb = V0 + NWC * V_W2
            v.wait_ge(ld, 48)
            v.tensor_tensor(s2sw[:, :], swcol[:, :], sb_ap(s2pp, 0, [[0, 8]]), ALU.mult).then_inc(ve, 1)
            v.wait_ge(ve, vb + 1)
            v.reciprocal(rs2sw[:, :], s2sw[:, :]).then_inc(ve, 1)
            v.wait_ge(ve, vb + 2)
            v.tensor_tensor(bscr[:, :], b2pt_sb[:, :], rs2sw[:, :], ALU.mult).then_inc(ve, 1)
            v.wait_ge(ve, vb + 3)
            v.tensor_copy(bi32[:, :], bscr[:, :]).then_inc(ve, 1)
            v.wait_ge(ve, vb + 4)
            v.tensor_copy(bscr[:, :], bi32[:, :]).then_inc(ve, 1)
            v.wait_ge(ve, vb + 5)
            v.tensor_tensor(bq2[:, :], bscr[:, :], s2sw[:, :], ALU.mult).then_inc(ve, 1)
            v.wait_ge(ve, vb + 6)
            v.tensor_copy(comb[:, 0:8], s2sw[:, :]).then_inc(ve, 1)
            v.wait_ge(ve, vb + 7)
            v.tensor_copy(comb[:, 8:16], bq2[:, :]).then_inc(ve, 1)
            v.wait_ge(mm, 3)
            v.tensor_copy(combT[:, :], pscr[0:16, 0:128]).then_inc(ve, 1)
            v.wait_ge(mm, 4)
            v.tensor_copy(scale2rep[:, :], prow[:, :]).then_inc(ve, 1)
            v.wait_ge(mm, 5)
            v.tensor_copy(bq2rep[:, :], prow[:, :]).then_inc(ve, 1)
            def q_ops(tc):
                vbq = vq_done[tc] - VQ
                v.wait_ge(lh[tc % 2], 16 * (tc // 2 + 1))
                for hh in range(2):
                    o = hh * (F // 2)
                    vbh = vbq + 4 * hh
                    if hh:
                        v.wait_ge(ve, vbh)
                    v.tensor_scalar(hscr[:, :], sb_ap(hstrip[tc % 2], o, [[1, F // 2]]),
                                    rs2pp[:, 0:1], None, ALU.mult).then_inc(ve, 1)
                    v.wait_ge(ve, vbh + 1)
                    v.tensor_copy(hi32[:, :], hscr[:, :]).then_inc(ve, 1)
                    v.wait_ge(ve, vbh + 2)
                    v.tensor_copy(hscr[:, :], hi32[:, :]).then_inc(ve, 1)
                    v.wait_ge(ve, vbh + 3)
                    if tc >= 2 and hh == 0:
                        v.wait_ge(mm, 5 + tc - 1)
                    v.tensor_scalar(sb_ap(hq[tc % 2], o, [[1, F // 2]]), hscr[:, :], 255.0, 0.0,
                                    ALU.min, ALU.max).then_inc(ve, 1)

            def ln_ops(tc):
                vb0 = vln[tc][0]
                v.wait_ge(mm, 5 + tc + 1)
                v.tensor_tensor(t1_[tc % 2][:, :], po[tc % 2][:, :], scale2rep[:, :],
                                ALU.mult).then_inc(ve, 1)
                v.wait_ge(ve, vb0 + 1)
                v.tensor_add(t2_[:, :], t1_[tc % 2][:, :], bq2rep[:, :]).then_inc(ve, 1)
                v.wait_ge(ve, vb0 + 2)
                v.wait_ge(lx[tc % 2], 16 * (tc // 2 + 1))
                v.tensor_add(t1_[tc % 2][:, :], t2_[:, :], x1t[tc % 2][:, :]).then_inc(ve, 1)
                v.wait_ge(ve, vb0 + 3)
                v.reduce_sum(srx[:, 0:1], t1_[tc % 2][:, :], axis=AXX).then_inc(ve, 1)
                v.wait_ge(ve, vb0 + 4)
                v.wait_ge(ve, vb0 + 4)
                v.tensor_scalar_mul(srx[:, 2:3], srx[:, 0:1], 1.0 / D).then_inc(ve, 1)
                v.wait_ge(ve, vb0 + 5)
                v.tensor_mul(srx[:, 3:4], srx[:, 2:3], srx[:, 2:3]).then_inc(ve, 1)
                v.wait_ge(se, 4 * tc + 1)
                v.tensor_scalar_mul(srx[:, 4:5], srx[:, 1:2], 1.0 / D).then_inc(ve, 1)
                v.wait_ge(ve, vb0 + 7)
                v.tensor_sub(srx[:, 5:6], srx[:, 4:5], srx[:, 3:4]).then_inc(ve, 1)
                v.wait_ge(ve, vb0 + 8)
                v.tensor_scalar_add(srx[:, 6:7], srx[:, 5:6], 1e-5).then_inc(ve, 1)
                v.wait_ge(se, 4 * tc + 3)
                v.tensor_mul(srx[:, 8:9], srx[:, 7:8], srx[:, 2:3]).then_inc(ve, 1)
                v.wait_ge(ve, vb0 + 10)
                v.tensor_scalar_mul(srx[:, 9:10], srx[:, 8:9], -1.0).then_inc(ve, 1)

            q_ops(0)
            for tc in range(NTC):
                if tc + 1 < NTC:
                    q_ops(tc + 1)
                ln_ops(tc)

        @block.scalar
        def _(s):
            s = AutoWait(s, se)
            for tc in range(NTC):
                vb0 = vln[tc][0]
                s.wait_ge(ve, vb0 + 3)
                s.activation(trash[:, :], t1_[tc % 2][:, :], AF.Square,
                             accum_out=srx[:, 1:2]).then_inc(se, 1)
                s.wait_ge(ve, vb0 + 9)
                s.activation(srx[:, 7:8], srx[:, 6:7], AF.Ln).then_inc(se, 1)
                s.wait_ge(se, 4 * tc + 2)
                s.activation(srx[:, 7:8], srx[:, 7:8], AF.Exp, scale=-0.5).then_inc(se, 1)
                s.wait_ge(ve, vb0 + 11)
                if tc >= 2:
                    s.wait_ge(st[tc % 2], 16 * (tc // 2))
                s.activation(xo[tc % 2][:, :], t1_[tc % 2][:, :], AF.Identity,
                             scale=srx[:, 7:8], bias=srx[:, 9:10]).then_inc(se, 1)


def build_p6():
    return _standalone(emit_p6)



def emit_p7(fx):
    x2_s = fx.dram("x2_s", [BPC, L, D], BF16, kind="ExternalInput")
    wT = fx.dram("qkvw1T", [D, 3 * D], F32, kind="ExternalInput")
    bpt = fx.dram("qkvb1_pt", [128, 16], F32, kind="ExternalInput")
    bvrow = fx.dram("bv1_row", [1, D], F32, kind="ExternalInput")
    qk1_s = fx.dram("qk1_s", [2 * D, NTOK], BF16, kind="ExternalOutput")
    v1_s = fx.dram("v1_s", [BPC, L, D], BF16, kind="ExternalOutput")

    with fx.phase() as al:
        x2T = al.sb("x2T", [128, 8 * NTOK], BF16)
        w_sb = al.sb("w7", [128, 8 * 3 * D], BF16)
        bpt_sb = al.sb("bpt7", [128, 16], F32)
        bvrow_sb = al.sb("bvrow7", [1, D], BF16)
        ones_sb = al.sb("ones7", [1, 128], BF16)
        qe = [al.sb(f"qe{j}", [128, 512], BF16) for j in range(4)]
        vev = [al.sb(f"vev{j}", [128, D], BF16) for j in range(2)]
        pq = [al.ps(f"pq{j}", [128, 512]) for j in range(4)]
        pv = [al.ps(f"pv{j}", [128, D]) for j in range(2)]
        ini, ld, lt, mm, se = [al.sem(n) for n in ("ini", "ld", "lt", "mm", "se")]
        stq = [al.sem(f"stq{j}") for j in range(4)]
        stv = [al.sem(f"stv{j}") for j in range(2)]
        block = al.block()
        NDOC = 16

        @block.gpsimd
        def _(g):
            g.memset(ones_sb[:, :], 1.0).then_inc(ini, 1)
            src = ap_of(wT, 0, [[3 * D, 128], [128 * 3 * D, 8], [1, 3 * D]])
            g.dma_start(out=w_sb[:, :].rearrange("p (c n) -> p c n", c=8), in_=src).then_inc(ld, 16)
            g.wait_ge(ld, 16)
            g.dma_start(out=bpt_sb[:, :], in_=bpt.ap()[:, :]).then_inc(ld, 16)
            g.wait_ge(ld, 32)
            g.dma_start(out=bvrow_sb[0:1, :], in_=bvrow.ap()[0:1, :]).then_inc(ld, 16)

        @block.sync
        def _(sy):
            for b in range(BPC):
                for dc in range(8):
                    dst = sb_ap(x2T, dc * NTOK + b * L, [[1, L]])
                    sy.dma_start(out=dst, in_=x2_s.ap()[b, :, dc * 128:(dc + 1) * 128],
                                 transpose=True).then_inc(lt, 16)
            nse = 0
            for doc in range(NDOC):
                for half in range(2):
                    for q4 in range(4):
                        nse += 1
                        sy.wait_ge(se, nse)
                        col = half * 2048 + q4 * 512
                        sy.dma_start(out=ap_of(qk1_s, doc * 128 * NTOK + col, [[NTOK, 128], [1, 512]]),
                                     in_=qe[q4][:, :]).then_inc(stq[q4], 16)
            for i in range(32):
                nse += 1
                sy.wait_ge(se, nse)
                b, tc = i // 16, i % 16
                sy.dma_start(out=ap_of(v1_s, b * L * D + tc * 128 * D, [[D, 128], [1, D]]),
                             in_=vev[i % 2][:, :]).then_inc(stv[i % 2], 16)
            for j in range(4):
                sy.wait_ge(stq[j], 16 * 32)
            sy.wait_ge(stv[0], 16 * 16)
            sy.wait_ge(stv[1], 16 * 16)

        @block.tensor
        def _(t):
            t.wait_ge(ini, 1)
            t.wait_ge(ld, 48)
            t.wait_ge(lt, 16 * 16)
            nmm = 0
            for doc in range(NDOC):
                for half in range(2):
                    for dc in range(8):
                        lhsT = sb_ap(w_sb, dc * 3 * D + doc * 128, [[1, 128]])
                        for q4 in range(4):
                            n = doc * 2 + half
                            if dc == 0 and n >= 1:
                                t.wait_ge(se, (n - 1) * 4 + q4 + 1)
                            last = t.matmul(pq[q4][:, :], lhsT,
                                            sb_ap(x2T, dc * NTOK + half * 2048 + q4 * 512, [[1, 512]]),
                                            start=(dc == 0), stop=(dc == 7))
                    nmm += 1
                    last.then_inc(mm, 1)
            for i in range(32):
                b, tc = i // 16, i % 16
                if i >= 2:
                    t.wait_ge(se, 128 + (i - 1))
                p = pv[i % 2]
                t.matmul(p[:, 0:512], ones_sb[0:1, :], bvrow_sb[0:1, 0:512], start=True, stop=False)
                t.matmul(p[:, 512:1024], ones_sb[0:1, :], bvrow_sb[0:1, 512:1024], start=True, stop=False)
                for dc in range(8):
                    lhsT = sb_ap(x2T, dc * NTOK + b * L + tc * 128, [[1, 128]])
                    for cchunk in range(2):
                        last = t.matmul(p[:, cchunk * 512:(cchunk + 1) * 512], lhsT,
                                        sb_ap(w_sb, dc * 3 * D + 2 * D + cchunk * 512, [[1, 512]]),
                                        start=False, stop=(dc == 7))
                last.then_inc(mm, 1)

        @block.scalar
        def _(s):
            s = AutoWait(s, se)
            nse = 0
            for doc in range(NDOC):
                for half in range(2):
                    n = doc * 2 + half
                    s.wait_ge(mm, n + 1)
                    n2 = doc * 2 + half
                    for q4 in range(4):
                        nse += 1
                        if n2 >= 1:
                            s.wait_ge(stq[q4], 16 * n2)
                        s.activation(qe[q4][:, :], pq[q4][:, :], AF.Identity,
                                     bias=bpt_sb[:, doc:doc + 1]).then_inc(se, 1)
            for i in range(32):
                s.wait_ge(mm, 32 + i + 1)
                nse += 1
                if i >= 2:
                    s.wait_ge(stv[i % 2], 16 * (i // 2))
                s.copy(vev[i % 2][:, :], pv[i % 2][:, :]).then_inc(se, 1)


def build_p7():
    return _standalone(emit_p7)



def emit_p9(fx):
    oT_s = fx.dram("oT_s", [D, BPC * 512], BF16, kind="ExternalInput")
    pT = fx.dram("projT", [D, D], F32, kind="ExternalInput")
    pbrow = fx.dram("projb_row", [1, D], F32, kind="ExternalInput")
    pbrep = fx.dram("projb_rep", [128, D], F32, kind="ExternalInput")
    x2_s = fx.dram("x2_s", [BPC, L, D], BF16, kind="ExternalInput")
    x3_s = fx.dram("x3_s", [BPC, L, D], BF16, kind="ExternalOutput")

    NT = L // 128
    chunks = [(b, tc) for b in range(BPC) for tc in range(NT)]
    NCH = len(chunks)
    VPC, SPC = 9, 4

    with fx.phase() as al:
        ot_sb = al.sb("ot_sb", [128, 8 * BPC * 512], BF16)
        w_sb = al.sb("w9", [128, 8 * D], BF16)
        pbrow_sb = al.sb("pbrow9", [1, D], BF16)
        pbrep_sb = al.sb("pbrep9", [128, D], F32)
        ones_sb = al.sb("ones9", [1, 128], BF16)
        x2t = [al.sb(f"x2t9{j}", [128, D], BF16) for j in range(2)]
        t1_ = [al.sb(f"t9a{j}", [128, D], F32) for j in range(2)]
        xo = [al.sb(f"xo9{j}", [128, D], BF16) for j in range(2)]
        srx = al.sb("srx9", [128, 16], F32)
        pm = [al.ps(f"pm9{j}", [128, D]) for j in range(2)]
        trash = al.ps("trash9", [128, D])
        ini, ld, mm, ve, se = [al.sem(n) for n in ("ini", "ld", "mm", "ve", "se")]
        lx = [al.sem(f"lx9{j}") for j in range(2)]
        st = [al.sem(f"st9{j}") for j in range(2)]
        block = al.block()
        is_mm = [tc >= 12 for (b, tc) in chunks]
        mm_idx = np.cumsum([1 if m else 0 for m in is_mm]).tolist()

        @block.gpsimd
        def _(g):
            g.memset(ones_sb[:, :], 1.0).then_inc(ini, 1)
            src = ap_of(pT, 0, [[D, 128], [128 * D, 8], [1, D]])
            g.dma_start(out=w_sb[:, :].rearrange("p (c n) -> p c n", c=8), in_=src).then_inc(ld, 16)
            g.wait_ge(ld, 16)
            g.dma_start(out=pbrow_sb[0:1, :], in_=pbrow.ap()[0:1, :]).then_inc(ld, 16)
            g.wait_ge(ld, 32)
            g.dma_start(out=pbrep_sb[:, :], in_=pbrep.ap()[:, :]).then_inc(ld, 16)
            g.wait_ge(ld, 48)
            src = ap_of(oT_s, 0, [[BPC * 512, 128], [128 * BPC * 512, 8], [1, BPC * 512]])
            g.dma_start(out=ot_sb[:, :].rearrange("p (c n) -> p c n", c=8), in_=src).then_inc(ld, 16)
            for i, (b, tc) in enumerate(chunks):
                if i >= 2:
                    g.wait_ge(ve, VPC * (i - 2) + 1)
                g.dma_start(out=x2t[i % 2][:, :],
                            in_=ap_of(x2_s, b * L * D + tc * 128 * D, [[D, 128], [1, D]])).then_inc(lx[i % 2], 16)

        @block.tensor
        def _(t):
            t.wait_ge(ini, 1)
            t.wait_ge(ld, 64)
            for i, (b, tc) in enumerate(chunks):
                if not is_mm[i]:
                    continue
                if i >= 2:
                    t.wait_ge(ve, VPC * (i - 2) + 1)
                p = pm[i % 2]
                t.matmul(p[:, 0:512], ones_sb[0:1, :], pbrow_sb[0:1, 0:512], start=True, stop=False)
                t.matmul(p[:, 512:1024], ones_sb[0:1, :], pbrow_sb[0:1, 512:1024], start=True, stop=False)
                for dc in range(8):
                    col = b * 512 + (tc - 12) * 128
                    lhsT = sb_ap(ot_sb, dc * BPC * 512 + col, [[1, 128]])
                    for cchunk in range(2):
                        last = t.matmul(p[:, cchunk * 512:(cchunk + 1) * 512], lhsT,
                                        sb_ap(w_sb, dc * D + cchunk * 512, [[1, 512]]),
                                        start=False, stop=(dc == 7))
                last.then_inc(mm, 1)

        @block.vector
        def _(v):
            v = AutoWait(v, ve)
            v.wait_ge(ld, 64)
            for i in range(NCH):
                vb, sb_ = VPC * i, SPC * i
                v.wait_ge(lx[i % 2], 16 * (i // 2 + 1))
                if i >= 1:
                    v.wait_ge(se, SPC * i)
                t1 = t1_[i % 2]
                if is_mm[i]:
                    v.wait_ge(mm, mm_idx[i])
                    v.tensor_add(t1[:, :], pm[i % 2][:, :], x2t[i % 2][:, :]).then_inc(ve, 1)
                else:
                    v.tensor_add(t1[:, :], pbrep_sb[:, :], x2t[i % 2][:, :]).then_inc(ve, 1)
                v.wait_ge(ve, vb + 1)
                v.reduce_sum(srx[:, 0:1], t1[:, :], axis=AXX).then_inc(ve, 1)
                v.wait_ge(ve, vb + 2)
                v.tensor_scalar_mul(srx[:, 2:3], srx[:, 0:1], 1.0 / D).then_inc(ve, 1)
                v.wait_ge(ve, vb + 3)
                v.tensor_mul(srx[:, 3:4], srx[:, 2:3], srx[:, 2:3]).then_inc(ve, 1)
                v.wait_ge(se, sb_ + 1)
                v.tensor_scalar_mul(srx[:, 4:5], srx[:, 1:2], 1.0 / D).then_inc(ve, 1)
                v.wait_ge(ve, vb + 5)
                v.tensor_sub(srx[:, 5:6], srx[:, 4:5], srx[:, 3:4]).then_inc(ve, 1)
                v.wait_ge(ve, vb + 6)
                v.tensor_scalar_add(srx[:, 6:7], srx[:, 5:6], 1e-5).then_inc(ve, 1)
                v.wait_ge(se, sb_ + 3)
                v.tensor_mul(srx[:, 8:9], srx[:, 7:8], srx[:, 2:3]).then_inc(ve, 1)
                v.wait_ge(ve, vb + 8)
                v.tensor_scalar_mul(srx[:, 9:10], srx[:, 8:9], -1.0).then_inc(ve, 1)

        @block.scalar
        def _(s):
            s = AutoWait(s, se)
            for i in range(NCH):
                vb, sb_ = VPC * i, SPC * i
                s.wait_ge(ve, vb + 1)
                s.activation(trash[:, :], t1_[i % 2][:, :], AF.Square,
                             accum_out=srx[:, 1:2]).then_inc(se, 1)
                s.wait_ge(ve, vb + 7)
                s.activation(srx[:, 7:8], srx[:, 6:7], AF.Ln).then_inc(se, 1)
                s.wait_ge(se, sb_ + 2)
                s.activation(srx[:, 7:8], srx[:, 7:8], AF.Exp, scale=-0.5).then_inc(se, 1)
                s.wait_ge(ve, vb + 9)
                if i >= 2:
                    s.wait_ge(st[i % 2], 16 * (i // 2))
                s.activation(xo[i % 2][:, :], t1_[i % 2][:, :], AF.Identity,
                             scale=srx[:, 7:8], bias=srx[:, 9:10]).then_inc(se, 1)

        @block.sync
        def _(sy):
            for i, (b, tc) in enumerate(chunks):
                sy.wait_ge(se, SPC * i + 4)
                sy.dma_start(out=ap_of(x3_s, b * L * D + tc * 128 * D, [[D, 128], [1, D]]),
                             in_=xo[i % 2][:, :]).then_inc(st[i % 2], 16)
            sy.wait_ge(st[0], 16 * (NCH // 2))
            sy.wait_ge(st[1], 16 * (NCH // 2))


def build_p9():
    return _standalone(emit_p9)



def emit_p10(fx):
    x3_s = fx.dram("x3_s", [BPC, L, D], BF16, kind="ExternalInput")
    w1T = fx.dram("w1T_1", [D, 4 * D], F32, kind="ExternalInput")
    w1n = fx.dram("w1_1", [4 * D, D], F32, kind="ExternalInput")
    b1pt = fx.dram("b1_1pt", [128, 32], F32, kind="ExternalInput")
    w2T = fx.dram("w2T_1", [4 * D, D], F32, kind="ExternalInput")
    w2n = fx.dram("w2_1", [D, 4 * D], F32, kind="ExternalInput")
    b2rep = fx.dram("b2_1rep", [128, D], F32, kind="ExternalInput")
    idn = fx.dram("idn_f32", [128, 128], F32, kind="ExternalInput")
    out_s = fx.dram("out_s", [BPC, L, D], F32, kind="ExternalOutput")
    rowb = fx.dram("rowb10", [8, 128], F32)
    F = 4 * D
    TT_ = 256
    NTT = NTOK // TT_

    v_cnt = 0
    va1 = {}
    for fc in range(32):
        va1[fc] = v_cnt + 2
        v_cnt += 2
    va2 = {}
    for wc in range(8):
        va2[wc] = v_cnt + 4
        v_cnt += 4
    v_a2T = v_cnt + 1
    v_a2rep = v_cnt + 2
    v_cnt += 2
    VLN = 11
    vln = {}
    for n in range(NTT * 2):
        vln[n] = v_cnt
        v_cnt += VLN
    s_sign = 48

    def gel_cnt(gidx):
        ti, fc = gidx // 32, gidx % 32
        return s_sign + ti * 40 + fc + 1

    sgel = {ti: s_sign + ti * 40 + 32 for ti in range(NTT)}
    sln = {}
    for ti in range(NTT):
        for sub in range(2):
            sln[ti * 2 + sub] = s_sign + ti * 40 + 32 + sub * 4
    mm_t = {ti: 2 + ti * 34 for ti in range(NTT)}

    with fx.phase() as al:
        w1s = al.sb("w1s", [128, 8 * F], BF16)
        w2s = al.sb("w2s", [128, 32 * D], BF16)
        wld = [al.sb(f"wld10{j}", [128, 2048], F32) for j in range(2)]
        a1col = al.sb("a1col", [128, 32], F32)
        a2col = al.sb("a2col", [128, 8], F32)
        a2colT = al.sb("a2colT", [8, 128], F32)
        a2row = al.sb("a2row", [1, D], F32)
        a2rep = al.sb("a2rep", [128, D], F32)
        b1pt_sb = al.sb("b1pt10", [128, 32], F32)
        b2rep_sb = al.sb("b2rep10", [128, D], F32)
        idn_sb = al.sb("idn10", [128, 128], F32)
        onesf = al.sb("onesf10", [1, 128], F32)
        x3T = [al.sb(f"x3T{j}", [128, 8 * TT_], BF16) for j in range(2)]
        hstrip = al.sb("hst10", [128, 32 * TT_], BF16)
        x3t = [al.sb(f"x3t10{j}", [128, D], BF16) for j in range(2)]
        t1_ = [al.sb(f"t10a{j}", [128, D], F32) for j in range(2)]
        xo = [al.sb(f"xo10{j}", [128, D], F32) for j in range(2)]
        srx = al.sb("srx10", [128, 16], F32)
        strash = al.sb("strash10", [128, D], F32)
        pscr = al.ps("pscr10", [128, 128])
        ph = [al.ps(f"ph10{j}", [128, TT_]) for j in range(2)]
        po = [al.ps(f"po10{j}", [128, D]) for j in range(2)]
        (ini, ld, mm, ve, se, rb) = [al.sem(n) for n in
            ("ini", "ld", "mm", "ve", "se", "rb")]
        lw = [al.sem(f"lw10{j}") for j in range(2)]
        lt = [al.sem(f"lt10{j}") for j in range(2)]
        lx = [al.sem(f"lx10{j}") for j in range(2)]
        st = [al.sem(f"st10{j}") for j in range(2)]
        block = al.block()

        @block.gpsimd
        def _(g):
            g.memset(onesf[:, :], 1.0).then_inc(ini, 1)
            g.dma_start(out=idn_sb[:, :], in_=idn.ap()[:, :]).then_inc(ld, 16)
            g.wait_ge(ld, 16)
            g.dma_start(out=b1pt_sb[:, :], in_=b1pt.ap()[:, :]).then_inc(ld, 16)
            g.wait_ge(ld, 32)
            g.dma_start(out=b2rep_sb[:, :], in_=b2rep.ap()[:, :]).then_inc(ld, 16)
            for hdc in range(16):
                if hdc >= 2:
                    g.wait_ge(se, hdc - 1)
                dc, hh = hdc // 2, hdc % 2
                g.dma_start(out=wld[hdc % 2][:, :], in_=w1T.ap()[dc * 128:(dc + 1) * 128,
                            hh * 2048:(hh + 1) * 2048]).then_inc(lw[hdc % 2], 16)
            for fc in range(32):
                g.wait_ge(se, 15 + fc)
                g.dma_start(out=wld[fc % 2][:, 0:D],
                            in_=w2T.ap()[fc * 128:(fc + 1) * 128, :]).then_inc(lw[fc % 2], 16)
            for fc in range(32):
                if fc < 2:
                    g.wait_ge(se, 48)
                else:
                    g.wait_ge(ve, va1[fc - 2])
                g.dma_start(out=wld[fc % 2][:, D:2 * D],
                            in_=w1n.ap()[fc * 128:(fc + 1) * 128, :]).then_inc(lw[fc % 2], 16)
            for hwc in range(16):
                wc, hh = hwc // 2, hwc % 2
                if hwc < 2:
                    g.wait_ge(ve, va1[30 + hwc])
                else:
                    pwc, phh = (hwc - 2) // 2, (hwc - 2) % 2
                    g.wait_ge(ve, va2[pwc] - 3 + phh)
                g.dma_start(out=wld[hwc % 2][:, :], in_=w2n.ap()[wc * 128:(wc + 1) * 128,
                            hh * 2048:(hh + 1) * 2048]).then_inc(lw[hwc % 2], 16)
            g.wait_ge(ve, v_a2T)
            g.dma_start(out=rowb.ap()[:, :], in_=a2colT[:, :]).then_inc(rb, 16)
            g.wait_ge(rb, 16)
            g.dma_start(out=a2row[0:1, :], in_=ap_of(rowb, 0, [[D, 1], [1, D]])).then_inc(rb, 16)
            def emit_store(n):
                g.wait_ge(se, sln[n] + 4)
                tok = n * 128
                b, t_ = tok // L, tok % L
                g.dma_start(out=ap_of(out_s, b * L * D + t_ * D, [[D, 128], [1, D]]),
                            in_=xo[n % 2][:, :]).then_inc(st[n % 2], 16)
            for n in range(NTT * 2):
                if n >= 2:
                    g.wait_ge(ve, vln[n - 2] + 3)
                tok = n * 128
                b, t_ = tok // L, tok % L
                g.dma_start(out=x3t[n % 2][:, :],
                            in_=ap_of(x3_s, b * L * D + t_ * D, [[D, 128], [1, D]])).then_inc(lx[n % 2], 16)
                if n >= 1:
                    emit_store(n - 1)
            emit_store(NTT * 2 - 1)
            g.wait_ge(st[0], 16 * NTT)
            g.wait_ge(st[1], 16 * NTT)

        @block.sync
        def _(sy):
            for ti in range(NTT):
                tok = ti * TT_
                b, t_ = tok // L, tok % L
                if ti >= 2:
                    sy.wait_ge(mm, mm_t[ti - 2] + 32)
                for dc in range(8):
                    dst = sb_ap(x3T[ti % 2], dc * TT_, [[1, TT_]])
                    sy.dma_start(out=dst, in_=x3_s.ap()[b, t_:t_ + TT_, dc * 128:(dc + 1) * 128],
                                 transpose=True).then_inc(lt[ti % 2], 16)


        @block.tensor
        def _(t):
            t.wait_ge(ini, 1)
            t.wait_ge(ld, 48)
            t.wait_ge(ve, va2[7])
            t.transpose(pscr[0:8, 0:128], a2col[:, 0:8], idn_sb[:, :]).then_inc(mm, 1)
            t.wait_ge(rb, 32)
            t.matmul(po[0][:, 0:512], onesf[0:1, :], a2row[0:1, 0:512], start=True, stop=True)
            t.matmul(po[0][:, 512:1024], onesf[0:1, :], a2row[0:1, 512:1024], start=True, stop=True).then_inc(mm, 1)
            t.wait_ge(se, s_sign)
            t.wait_ge(ve, v_a2rep)
            for ti in range(NTT):
                t.wait_ge(lt[ti % 2], 16 * 8 * (ti // 2 + 1))
                for fc in range(32):
                    gidx = ti * 32 + fc
                    if gidx >= 2:
                        t.wait_ge(se, gel_cnt(gidx - 2))
                    last = None
                    for dc in range(8):
                        last = t.matmul(ph[fc % 2][:, :],
                                        sb_ap(w1s, dc * F + fc * 128, [[1, 128]]),
                                        sb_ap(x3T[ti % 2], dc * TT_, [[1, TT_]]),
                                        start=(dc == 0), stop=(dc == 7))
                    last.then_inc(mm, 1)
                for sub in range(2):
                    n = ti * 2 + sub
                    t.wait_ge(se, sgel[ti])
                    if n >= 2:
                        t.wait_ge(ve, vln[n - 2] + 1)
                    last = None
                    for fc in range(32):
                        lhsT = sb_ap(hstrip, fc * TT_ + sub * 128, [[1, 128]])
                        for cchunk in range(2):
                            last = t.matmul(po[n % 2][:, cchunk * 512:(cchunk + 1) * 512], lhsT,
                                            sb_ap(w2s, fc * D + cchunk * 512, [[1, 512]]),
                                            start=(fc == 0), stop=(fc == 31))
                    last.then_inc(mm, 1)

        @block.vector
        def _(v):
            v = AutoWait(v, ve)
            for fc in range(32):
                v.wait_ge(lw[fc % 2], 16 * ((48 + fc) // 2 + 1))
                v.reduce_sum(a1col[:, fc:fc + 1], wld[fc % 2][:, D:2 * D], axis=AXX,
                             apply_absolute_value=True).then_inc(ve, 1)
                v.wait_ge(ve, va1[fc] - 1)
                v.tensor_scalar_mul(a1col[:, fc:fc + 1], a1col[:, fc:fc + 1], 1.0 / D).then_inc(ve, 1)
            for wc in range(8):
                v.wait_ge(lw[0], 16 * ((80 + 2 * wc) // 2 + 1))
                v.reduce_sum(srx[:, 10:11], wld[0][:, :], axis=AXX,
                             apply_absolute_value=True).then_inc(ve, 1)
                v.wait_ge(lw[1], 16 * ((81 + 2 * wc) // 2 + 1))
                v.wait_ge(ve, va2[wc] - 3)
                v.reduce_sum(srx[:, 11:12], wld[1][:, :], axis=AXX,
                             apply_absolute_value=True).then_inc(ve, 1)
                v.wait_ge(ve, va2[wc] - 2)
                v.tensor_add(a2col[:, wc:wc + 1], srx[:, 10:11], srx[:, 11:12]).then_inc(ve, 1)
                v.wait_ge(ve, va2[wc] - 1)
                v.tensor_scalar_mul(a2col[:, wc:wc + 1], a2col[:, wc:wc + 1], 1.0 / F).then_inc(ve, 1)
            v.wait_ge(mm, 1)
            v.tensor_copy(a2colT[:, :], pscr[0:8, 0:128]).then_inc(ve, 1)
            v.wait_ge(mm, 2)
            v.wait_ge(ld, 48)
            v.tensor_copy(a2rep[:, :], po[0][:, :]).then_inc(ve, 1)
            for n in range(NTT * 2):
                ti, sub = n // 2, n % 2
                vb = vln[n]
                sb_ = sln[n]
                v.wait_ge(mm, mm_t[ti] + 32 + sub + 1)
                v.wait_ge(lx[n % 2], 16 * (n // 2 + 1))
                if n >= 1:
                    v.wait_ge(se, sln[n - 1] + 4)
                t1 = t1_[n % 2]
                v.tensor_mul(t1[:, :], po[n % 2][:, :], a2rep[:, :]).then_inc(ve, 1)
                v.wait_ge(ve, vb + 1)
                v.tensor_add(strash[:, :], t1[:, :], b2rep_sb[:, :]).then_inc(ve, 1)
                v.wait_ge(ve, vb + 2)
                v.tensor_add(t1[:, :], strash[:, :], x3t[n % 2][:, :]).then_inc(ve, 1)
                v.wait_ge(ve, vb + 3)
                v.reduce_sum(srx[:, 0:1], t1[:, :], axis=AXX).then_inc(ve, 1)
                v.wait_ge(ve, vb + 4)
                v.tensor_scalar_mul(srx[:, 2:3], srx[:, 0:1], 1.0 / D).then_inc(ve, 1)
                v.wait_ge(ve, vb + 5)
                v.tensor_mul(srx[:, 3:4], srx[:, 2:3], srx[:, 2:3]).then_inc(ve, 1)
                v.wait_ge(se, sb_ + 1)
                v.tensor_scalar_mul(srx[:, 4:5], srx[:, 1:2], 1.0 / D).then_inc(ve, 1)
                v.wait_ge(ve, vb + 7)
                v.tensor_sub(srx[:, 5:6], srx[:, 4:5], srx[:, 3:4]).then_inc(ve, 1)
                v.wait_ge(ve, vb + 8)
                v.tensor_scalar_add(srx[:, 6:7], srx[:, 5:6], 1e-5).then_inc(ve, 1)
                v.wait_ge(se, sb_ + 3)
                v.tensor_mul(srx[:, 8:9], srx[:, 7:8], srx[:, 2:3]).then_inc(ve, 1)
                v.wait_ge(ve, vb + 10)
                v.tensor_scalar_mul(srx[:, 9:10], srx[:, 8:9], -1.0).then_inc(ve, 1)

        @block.scalar
        def _(s):
            s = AutoWait(s, se)
            for hdc in range(16):
                s.wait_ge(lw[hdc % 2], 16 * (hdc // 2 + 1))
                dc, hh = hdc // 2, hdc % 2
                s.activation(sb_ap(w1s, dc * F + hh * 2048, [[1, 2048]]), wld[hdc % 2][:, :],
                             AF.Sign).then_inc(se, 1)
            for fc in range(32):
                s.wait_ge(lw[fc % 2], 16 * ((16 + fc) // 2 + 1))
                s.activation(sb_ap(w2s, fc * D, [[1, D]]), wld[fc % 2][:, 0:D],
                             AF.Sign).then_inc(se, 1)
            s.wait_ge(ld, 48)
            for ti in range(NTT):
                for fc in range(32):
                    gidx = ti * 32 + fc
                    s.wait_ge(mm, mm_t[ti] + fc + 1)
                    if ti >= 1 and fc == 0:
                        s.wait_ge(mm, mm_t[ti - 1] + 34)
                    s.activation(sb_ap(hstrip, fc * TT_, [[1, TT_]]), ph[fc % 2][:, :],
                                 AF.Gelu, scale=a1col[:, fc:fc + 1],
                                 bias=b1pt_sb[:, fc:fc + 1]).then_inc(se, 1)
                for sub in range(2):
                    n = ti * 2 + sub
                    vb = vln[n]
                    sb_ = sln[n]
                    s.wait_ge(ve, vb + 3)
                    s.activation(strash[:, :], t1_[n % 2][:, :], AF.Square,
                                 accum_out=srx[:, 1:2]).then_inc(se, 1)
                    s.wait_ge(ve, vb + 9)
                    s.activation(srx[:, 7:8], srx[:, 6:7], AF.Ln).then_inc(se, 1)
                    s.wait_ge(se, sb_ + 2)
                    s.activation(srx[:, 7:8], srx[:, 7:8], AF.Exp, scale=-0.5).then_inc(se, 1)
                    s.wait_ge(ve, vb + 11)
                    if n >= 2:
                        s.wait_ge(st[n % 2], 16 * (n // 2))
                    s.activation(xo[n % 2][:, :], t1_[n % 2][:, :], AF.Identity,
                                 scale=srx[:, 7:8], bias=srx[:, 9:10]).then_inc(se, 1)


def build_p10():
    return _standalone(emit_p10)



def emit_p3(fx):
    kp_s = fx.dram("kp_s", [BPC, L, D], BF16, kind="ExternalInput")
    v_s = fx.dram("v_s", [BPC, L, D], BF16, kind="ExternalInput")
    q_s = fx.dram("q_s", [BPC, L, D], BF16, kind="ExternalInput")
    toep = fx.dram("toep", [128, 16, 128, 128], F32, kind="ExternalInput")
    drep = fx.dram("drep", [128, 128], F32, kind="ExternalInput")
    y_s = fx.dram("y_s", [BPC, L, D], BF16, kind="ExternalOutput")

    NBLK = 8
    NQB = L // 128
    CPB = 16

    def chunks_for(delta):
        cols = (16 - delta) * 128
        return (cols + 511) // 512
    NMM_C = sum(chunks_for(d_) for d_ in range(16))

    V_BLK = 6
    V_C = 6
    vdiag = {}
    vc_end = {}
    vc_prod = {}
    vc_kv = {}
    v_cnt = 2
    v_cnt += 1
    vc_kv[(0, 0)] = v_cnt
    for m in range(NBLK):
        vdiag[m] = v_cnt + V_BLK
        v_cnt += V_BLK
        for cl in range(CPB):
            nxt = (m, cl + 1) if cl < CPB - 1 else ((m + 1, 0) if m < NBLK - 1 else None)
            if nxt is not None:
                v_cnt += 1
                vc_kv[nxt] = v_cnt
            v_cnt += 1
            vc_prod[(m, cl)] = v_cnt
            v_cnt += 4
            vc_end[(m, cl)] = v_cnt
    V_TOTAL = v_cnt

    with fx.phase() as al:
        ybuf = al.sb("ybuf", [128, NQB * BPC * D], BF16)
        kt = [al.sb(f"kt{j}", [128, NQB * BPC * 128], BF16) for j in range(2)]
        vt = [al.sb(f"vt{j}", [128, NQB * BPC * 128], BF16) for j in range(2)]
        qt = [al.sb(f"qt{j}", [128, NQB * BPC * 128], BF16) for j in range(2)]
        tp = [al.sb(f"tp{j}", [128, 16 * 128], BF16) for j in range(2)]
        drep_sb = al.sb("drep_sb", [128, 128], F32)
        kv = [al.sb(f"kv_sb{j}", [128, 4096], BF16) for j in range(2)]
        prod = al.sb("prod", [128, 4096], BF16)
        tr1 = al.sb("tr1", [128, 2048], BF16)
        tr2 = al.sb("tr2", [128, 1024], BF16)
        tr3 = al.sb("tr3", [128, 256], BF16)
        qks = al.sb("qks", [128, 512], BF16)
        qks2 = al.sb("qks2", [128, 512], BF16)
        py = [al.ps(f"py{j}", [128, 2048]) for j in range(2)]
        ini, ld, mm, ve, st = [al.sem(n) for n in ("ini", "ld", "mm", "ve", "st")]
        lk = [al.sem(f"lk3{j}") for j in range(2)]
        ltp = [al.sem(f"ltp{j}") for j in range(2)]
        block = al.block()

        @block.sync
        def _(sy):
            for m in range(NBLK):
                if m >= 2:
                    sy.wait_ge(ve, vc_end[(m - 2, CPB - 1)])
                for (buf, src) in ((kt, kp_s), (vt, v_s), (qt, q_s)):
                    for b in range(BPC):
                        ap = ap_of(src, b * L * D + m * 128, [[D, 128], [128 * D, NQB], [1, 128]])
                        dst = ap_of(buf[m % 2], b * 128, [[NQB * BPC * 128, 128], [BPC * 128, NQB], [1, 128]])
                        sy.dma_start(out=dst, in_=ap).then_inc(lk[m % 2], 16)
            sy.wait_ge(ve, V_TOTAL)
            for b in range(BPC):
                dst = ap_of(y_s, b * L * D, [[D, 128], [128 * D, NQB], [1, D]])
                src = ap_of(ybuf, b * D, [[NQB * BPC * D, 128], [BPC * D, NQB], [1, D]])
                sy.dma_start(out=dst, in_=src).then_inc(st, 16)
            sy.wait_ge(st, 32)

        @block.gpsimd
        def _(g):
            g.dma_start(out=drep_sb[:, :], in_=drep.ap()[:, :]).then_inc(ld, 16)
            for m in range(NBLK):
                for cl in range(CPB):
                    ci = m * CPB + cl
                    n = m * CPB + cl
                    if n >= 2:
                        g.wait_ge(mm, n - 1)
                    src = ap_of(toep, ci * 16 * 128 * 128, [[128, 128], [128 * 128, 16], [1, 128]])
                    g.dma_start(out=tp[n % 2][:, :].rearrange("p (d n) -> p d n", d=16),
                                in_=src).then_inc(ltp[n % 2], 16)

        @block.tensor
        def _(t):
            for m in range(NBLK):
                for cl in range(CPB):
                    n = m * CPB + cl
                    t.wait_ge(ltp[n % 2], 16 * (n // 2 + 1))
                    t.wait_ge(ve, vc_kv[(m, cl)])
                    if n >= 2:
                        t.wait_ge(ve, vc_prod[(m, cl - 2) if cl >= 2 else (m - 1, cl + CPB - 2)])
                    last = None
                    for delta in range(16):
                        lhsT = sb_ap(tp[n % 2], delta * 128, [[1, 128]])
                        for bk in range(4):
                            if delta > 4 * bk + 3:
                                continue
                            o0 = bk * 512
                            last = t.matmul(py[n % 2][:, o0:o0 + 512],
                                            lhsT, sb_ap(kv[n % 2], 2048 + o0 - delta * 128, [[1, 512]]),
                                            start=(delta == 0), stop=(delta == 4 * bk + 3))
                    last.then_inc(mm, 1)

        @block.vector
        def _(v):
            v = AutoWait(v, ve)
            v.wait_ge(ld, 16)
            v.memset(kv[0][:, 0:2048], 0.0).then_inc(ve, 1)
            v.memset(kv[1][:, 0:2048], 0.0).then_inc(ve, 1)

            def kv_build(m2, cl2):
                n2 = m2 * CPB + cl2
                k2_, v2_ = kt[m2 % 2], vt[m2 % 2]
                if cl2 == 0:
                    v.wait_ge(lk[m2 % 2], 96 * (m2 // 2 + 1))
                v.tensor_tensor(kv[n2 % 2][:, 2048:4096].rearrange("p (q b j i) -> p q b j i", q=16, b=2, j=8),
                                ap_of(k2_, cl2 * 8, [[NQB * BPC * 128, 128], [256, 16], [128, 2], [0, 8], [1, 8]]),
                                ap_of(v2_, cl2 * 8, [[NQB * BPC * 128, 128], [256, 16], [128, 2], [1, 8], [0, 8]]),
                                ALU.mult).then_inc(ve, 1)

            kv_build(0, 0)
            for m in range(NBLK):
                k_, v_, q_ = kt[m % 2], vt[m % 2], qt[m % 2]
                v.wait_ge(lk[m % 2], 96 * (m // 2 + 1))
                v.tensor_tensor(prod[:, 0:4096], sb_ap(k_, 0, [[1, 4096]]),
                                sb_ap(q_, 0, [[1, 4096]]), ALU.mult).then_inc(ve, 1)
                v.tensor_tensor(ap_of(tr1, 0, [[2048, 128], [4, 512], [1, 4]]),
                                ap_of(prod, 0, [[4096, 128], [8, 512], [1, 4]]),
                                ap_of(prod, 4, [[4096, 128], [8, 512], [1, 4]]), ALU.add).then_inc(ve, 1)
                v.tensor_tensor(ap_of(tr2, 0, [[1024, 128], [2, 512], [1, 2]]),
                                ap_of(tr1, 0, [[2048, 128], [4, 512], [1, 2]]),
                                ap_of(tr1, 2, [[2048, 128], [4, 512], [1, 2]]), ALU.add).then_inc(ve, 1)
                v.tensor_tensor(qks[:, 0:512],
                                ap_of(tr2, 0, [[1024, 128], [2, 512]]),
                                ap_of(tr2, 1, [[1024, 128], [2, 512]]), ALU.add).then_inc(ve, 1)
                v.tensor_tensor(ap_of(qks2, 0, [[512, 128], [16, 32], [1, 16]]),
                                ap_of(qks, 0, [[512, 128], [16, 32], [1, 16]]),
                                ap_of(drep_sb, m * 16, [[128, 128], [0, 32], [1, 16]]), ALU.mult).then_inc(ve, 1)
                v.tensor_tensor(ap_of(ybuf, m * 16, [[NQB * BPC * D, 128], [BPC * D, 16], [D, 2], [1, 16], [128, 8]]),
                                ap_of(v_, 0, [[NQB * BPC * 128, 128], [256, 16], [128, 2], [8, 16], [1, 8]]),
                                ap_of(qks2, 0, [[512, 128], [32, 16], [16, 2], [1, 16], [0, 8]]), ALU.mult).then_inc(ve, 1)
                for cl in range(CPB):
                    n = m * CPB + cl
                    if cl < CPB - 1:
                        kv_build(m, cl + 1)
                    elif m < NBLK - 1:
                        kv_build(m + 1, 0)
                    v.wait_ge(mm, n + 1)
                    v.tensor_tensor(prod[:, 0:2048].rearrange("p (q b j i) -> p q b j i", q=16, b=2, j=8),
                                    py[n % 2][:, :].rearrange("p (q b j i) -> p q b j i", q=16, b=2, j=8),
                                    ap_of(q_, cl * 8, [[NQB * BPC * 128, 128], [256, 16], [128, 2], [0, 8], [1, 8]]),
                                    ALU.mult).then_inc(ve, 1)
                    v.tensor_tensor(ap_of(tr1, 0, [[2048, 128], [4, 256], [1, 4]]),
                                    ap_of(prod, 0, [[4096, 128], [8, 256], [1, 4]]),
                                    ap_of(prod, 4, [[4096, 128], [8, 256], [1, 4]]), ALU.add).then_inc(ve, 1)
                    v.tensor_tensor(ap_of(tr2, 0, [[1024, 128], [2, 256], [1, 2]]),
                                    ap_of(tr1, 0, [[2048, 128], [4, 256], [1, 2]]),
                                    ap_of(tr1, 2, [[2048, 128], [4, 256], [1, 2]]), ALU.add).then_inc(ve, 1)
                    v.tensor_tensor(tr3[:, 0:256],
                                    ap_of(tr2, 0, [[1024, 128], [2, 256]]),
                                    ap_of(tr2, 1, [[1024, 128], [2, 256]]), ALU.add).then_inc(ve, 1)
                    ysl = ap_of(ybuf, m * 16 + cl, [[NQB * BPC * D, 128], [BPC * D, 16], [D, 2], [128, 8]])
                    v.tensor_tensor(ysl, ysl,
                                    ap_of(tr3, 0, [[256, 128], [16, 16], [8, 2], [1, 8]]), ALU.add).then_inc(ve, 1)


def build_p3():
    return _standalone(emit_p3)


def prod3_ap(tr3):
    return ap_of(tr3, 0, [[256, 128], [16, 16], [8, 2], [1, 8]])


P2_NCH = 8
P2_TPC, P2_VPC, P2_SPC = 112, 140, 56


def emit_p2(fx):
    k_s = fx.dram("k_s", [BPC, L, D], BF16, kind="ExternalInput")
    kkT = fx.dram("ssmkkT", [L, D], F32, kind="ExternalInput")
    dkrep = fx.dram("dkrep", [128, D], F32, kind="ExternalInput")
    wa2 = fx.dram("fft_wa2", [32, 16384], BF16, kind="ExternalInput")
    va2 = fx.dram("fft_va2", [64, 16384], BF16, kind="ExternalInput")
    wbst = fx.dram("fft_wbst", [64, 256], BF16, kind="ExternalInput")
    va3 = fx.dram("fft_va3", [64, 128], BF16, kind="ExternalInput")
    idn = fx.dram("idn_bf16", [128, 128], BF16, kind="ExternalInput")
    kp_s = fx.dram("kp_s", [BPC, L, D], BF16, kind="ExternalOutput")

    with fx.phase() as al:
        xre = al.sb("xre", [32, 8192], BF16)
        xim = al.sb("xim", [32, 8192], BF16)
        xkr = al.sb("xkr", [32, 8192], BF16)
        wa2_sb = al.sb("wa2_sb", [32, 16384], BF16)
        va2_sb = al.sb("va2_sb", [64, 16384], BF16)
        wb_sb = al.sb("wbst_sb", [64, 256], BF16)
        va3_sb = al.sb("va3_sb", [64, 128], BF16)
        idn_sb = al.sb("idn_p2", [128, 128], BF16)
        dk_sb = al.sb("dk_p2", [128, D], F32)
        af = al.sb("af", [128, 8192], BF16)
        at2r = al.sb("at2r", [64, 8192], BF16)
        at2i = al.sb("at2i", [64, 8192], BF16)
        kff = al.sb("kff", [128, 8192], BF16)
        zko2sb = al.sb("zko2sb", [32, 8192], BF16)
        pa = al.ps("pa_p2", [128, 1024])
        pb = al.ps("pb_p2", [128, 1024])
        pt = [al.ps(f"pt_p2{j}", [64, 1024], dtype=BF16) for j in range(2)]
        pc = [al.ps(f"pc_p2{j}", [128, 512]) for j in range(2)]
        ld, lx, mm, ve, se, st = [al.sem(n) for n in ("ld", "lx", "mm", "ve", "se", "st")]
        block = al.block()

        zko = xkr

        def tKA(ch, f): return P2_TPC * ch + f + 1
        def tKT(ch, p): return P2_TPC * ch + 8 + p + 1
        def tKB(ch, j): return P2_TPC * ch + 24 + j + 1
        def tDA(ch, f): return P2_TPC * ch + 40 + f + 1
        def tDT(ch, p): return P2_TPC * ch + 48 + p + 1
        def tDB(ch, j): return P2_TPC * ch + 64 + j + 1
        def tI1(ch, f): return P2_TPC * ch + 80 + f + 1
        def tIT(ch, p): return P2_TPC * ch + 88 + p + 1
        def tI2(ch, j): return P2_TPC * ch + 104 + j + 1

        def sKA(ch, f): return P2_SPC * ch + f + 1
        def sDA(ch, f): return P2_SPC * ch + 8 + f + 1
        def sI1(ch, f): return P2_SPC * ch + 16 + f + 1
        def sI2(ch, j): return P2_SPC * ch + 24 + 2 * (j + 1)

        def vKT(ch, i): return P2_VPC * ch + i + 1
        def vKB(ch, j): return P2_VPC * ch + 32 + j + 1
        def vDT(ch, i): return P2_VPC * ch + 48 + i + 1
        def vDB(ch, j): return P2_VPC * ch + 80 + j + 1
        def vPROD(ch): return P2_VPC * ch + 104
        def vIT(ch, i): return P2_VPC * ch + 104 + i + 1

        @block.gpsimd
        def _(g):
            consts = [(wa2_sb, wa2), (va2_sb, va2), (wb_sb, wbst), (va3_sb, va3),
                      (idn_sb, idn), (dk_sb, dkrep)]
            for i, (dst, src) in enumerate(consts):
                if i:
                    g.wait_ge(ld, 16 * i)
                g.dma_start(out=dst[0:src.shape[0], :], in_=src.ap()[:, :]).then_inc(ld, 16)
            for ch in range(P2_NCH):
                if ch > 0:
                    g.wait_ge(ve, P2_VPC * ch)
                    g.wait_ge(st, 32 * ch)
                for j, (bi, dst) in enumerate(((0, xre), (1, xim))):
                    if ch or j:
                        g.wait_ge(lx, 48 * ch + 16 * j)
                    src = ap_of(k_s, bi * L * D + ch * 128, [[64 * D, 32], [D, 64], [1, 128]])
                    g.dma_start(out=dst[:, :].rearrange("p (a b) -> p a b", a=64), in_=src).then_inc(lx, 16)
                g.wait_ge(lx, 48 * ch + 32)
                src = ap_of(kkT, ch * 128, [[64 * D, 32], [D, 64], [1, 128]])
                g.dma_start(out=xkr[:, :].rearrange("p (a b) -> p a b", a=64), in_=src).then_inc(lx, 16)

        @block.sync
        def _(sy):
            for ch in range(P2_NCH):
                sy.wait_ge(ve, P2_VPC * (ch + 1))
                src0 = ap_of(zko, 0, [[8192, 32], [128, 64], [1, 128]])
                dst0 = ap_of(kp_s, ch * 128, [[64 * D, 32], [D, 64], [1, 128]])
                sy.dma_start(out=dst0, in_=src0).then_inc(st, 16)
                src1 = ap_of(zko2sb, 0, [[8192, 32], [128, 64], [1, 128]])
                dst1 = ap_of(kp_s, L * D + ch * 128, [[64 * D, 32], [D, 64], [1, 128]])
                sy.dma_start(out=dst1, in_=src1).then_inc(st, 16)
            sy.wait_ge(st, 32 * P2_NCH)

        @block.tensor
        def _(t):
            def wa_st(n1, half):
                return sb_ap(wa2_sb, n1 * 256 + half * 128, [[1, 128]], nparts=32)

            def va2_st(k1, half):
                return sb_ap(va2_sb, k1 * 256 + half * 128, [[1, 128]], nparts=64)

            def stageA(ch, kernel_only):
                for f in range(8):
                    if kernel_only:
                        if f >= 2:
                            t.wait_ge(se, sKA(ch, f - 2))
                    else:
                        t.wait_ge(se, sDA(ch, f - 2) if f >= 2 else sKA(ch, 7))
                    P = (pa, pb)[f % 2]
                    for l in range(8):
                        n1 = 8 * f + l
                        out = P[:, l * 128:(l + 1) * 128]
                        if kernel_only:
                            r = sb_ap(xkr, n1 * 128, [[1, 128]], nparts=32)
                            op = t.matmul(out, wa_st(n1, 0), r, start=True, stop=True)
                        else:
                            r0 = sb_ap(xre, n1 * 128, [[1, 128]], nparts=32)
                            r1 = sb_ap(xim, n1 * 128, [[1, 128]], nparts=32)
                            t.matmul(out, wa_st(n1, 0), r0, start=True, stop=False)
                            op = t.matmul(out, wa_st(n1, 1), r1, start=False, stop=True)
                    op.then_inc(mm, 1)

            def stageT(ch, sewait, vfun, pfun):
                for p in range(16):
                    t.wait_ge(se, sewait)
                    t.wait_ge(ve, pfun(p))
                    op = None
                    for j in range(8):
                        d_ = 8 * p + j
                        src = AP(af, d_, [[8192, 128], [128, 64]])
                        op = t.transpose(pt[p % 2][:, j * 128:(j + 1) * 128], src, idn_sb[:, :])
                    op.then_inc(mm, 1)

            def stageB(ch, first_wait, pcfun):
                for j in range(16):
                    if j == 0:
                        t.wait_ge(ve, first_wait)
                    if j >= 2:
                        t.wait_ge(ve, pcfun(j))
                    rre = sb_ap(at2r, j * 512, [[1, 512]], nparts=64)
                    rim = sb_ap(at2i, j * 512, [[1, 512]], nparts=64)
                    t.matmul(pc[j % 2][:, :], sb_ap(wb_sb, 0, [[1, 128]], nparts=64),
                             rre, start=True, stop=False)
                    t.matmul(pc[j % 2][:, :], sb_ap(wb_sb, 128, [[1, 128]], nparts=64),
                             rim, start=False, stop=True).then_inc(mm, 1)

            t.wait_ge(ld, 96)
            for ch in range(P2_NCH):
                t.wait_ge(lx, 48 * (ch + 1))
                if ch > 0:
                    t.wait_ge(ve, P2_VPC * ch)
                    t.wait_ge(se, P2_SPC * ch)
                stageA(ch, True)
                stageT(ch, sKA(ch, 7), vKT,
                       lambda p: (vIT(ch - 1, 2 * (14 + p) + 1) if ch > 0 else 0)
                       if p < 2 else vKT(ch, 2 * (p - 2) + 1))
                stageB(ch, vKT(ch, 31), lambda j: vKB(ch, j - 2))
                stageA(ch, False)
                stageT(ch, sDA(ch, 7), vDT,
                       lambda p: vKT(ch, 2 * (14 + p) + 1) if p < 2
                       else vDT(ch, 2 * (p - 2) + 1))
                stageB(ch, vDT(ch, 31), lambda j: vDB(ch, j - 2))
                for f in range(8):
                    if f == 0:
                        t.wait_ge(ve, vPROD(ch))
                    elif f >= 2:
                        t.wait_ge(se, sI1(ch, f - 2))
                    P = (pa, pb)[f % 2]
                    for l in range(8):
                        k1 = 8 * f + l
                        out = P[:, l * 128:(l + 1) * 128]
                        r0 = sb_ap(at2r, k1 * 128, [[1, 128]], nparts=64)
                        r1 = sb_ap(at2i, k1 * 128, [[1, 128]], nparts=64)
                        t.matmul(out, va2_st(k1, 0), r0, start=True, stop=False)
                        op = t.matmul(out, va2_st(k1, 1), r1, start=False, stop=True)
                    op.then_inc(mm, 1)
                stageT(ch, sI1(ch, 7), vIT,
                       lambda p: vDT(ch, 2 * (14 + p) + 1) if p < 2
                       else vIT(ch, 2 * (p - 2) + 1))
                for f in range(8):
                    if f == 0:
                        t.wait_ge(ve, vIT(ch, 31))
                    elif f >= 2:
                        t.wait_ge(se, sI2(ch, 2 * (f - 2) + 1))
                    P = (pa, pb)[f % 2]
                    for j2 in range(2):
                        j = 2 * f + j2
                        out = P[0:64, j2 * 512:(j2 + 1) * 512]
                        rre = sb_ap(at2r, j * 512, [[1, 512]], nparts=64)
                        rim = sb_ap(at2i, j * 512, [[1, 512]], nparts=64)
                        t.matmul(out, sb_ap(va3_sb, 0, [[1, 64]], nparts=64),
                                 rre, start=True, stop=False)
                        op = t.matmul(out, sb_ap(va3_sb, 64, [[1, 64]], nparts=64),
                                      rim, start=False, stop=True)
                    op.then_inc(mm, 1)

        @block.scalar
        def _(s):
            s = AutoWait(s, se)
            for ch in range(P2_NCH):
                if ch > 0:
                    s.wait_ge(st, 32 * ch)
                for f in range(8):
                    s.wait_ge(mm, tKA(ch, f))
                    s.copy(sb_ap(af, f * 1024, [[1, 1024]]),
                           (pa, pb)[f % 2][:, :]).then_inc(se, 1)
                for f in range(8):
                    s.wait_ge(mm, tDA(ch, f))
                    s.copy(sb_ap(af, f * 1024, [[1, 1024]]),
                           (pa, pb)[f % 2][:, :]).then_inc(se, 1)
                for f in range(8):
                    s.wait_ge(mm, tI1(ch, f))
                    s.copy(sb_ap(af, f * 1024, [[1, 1024]]),
                           (pa, pb)[f % 2][:, :]).then_inc(se, 1)
                for j in range(16):
                    f, j2 = j // 2, j % 2
                    s.wait_ge(mm, tI2(ch, f))
                    P = (pa, pb)[f % 2]
                    src_re = sb_ap(P, j2 * 512, [[1, 512]], base_part=0, nparts=32)
                    src_im = sb_ap(P, j2 * 512, [[1, 512]], base_part=32, nparts=32)
                    dre = ap_of(xkr, j * 8, [[8192, 32], [1, 8], [128, 64]])
                    dim = ap_of(zko2sb, j * 8, [[8192, 32], [1, 8], [128, 64]])
                    s.copy(dre, src_re.rearrange("p (a b) -> p a b", a=8)).then_inc(se, 1)
                    s.copy(dim, src_im.rearrange("p (a b) -> p a b", a=8)).then_inc(se, 1)

        @block.vector
        def _(v):
            v = AutoWait(v, ve)

            def tevac(p, tfun, ch):
                v.wait_ge(mm, tfun(ch, p))
                src_re = ap_of(pt[p % 2], 0, [[1024, 64], [128, 8], [1, 64]])
                src_im = ap_of(pt[p % 2], 64, [[1024, 64], [128, 8], [1, 64]])
                dre = sb_ap(at2r, p * 512, [[64, 8], [1, 64]], nparts=64)
                dim = sb_ap(at2i, p * 512, [[64, 8], [1, 64]], nparts=64)
                v.tensor_copy(dre, src_re).then_inc(ve, 1)
                v.tensor_copy(dim, src_im).then_inc(ve, 1)

            for ch in range(P2_NCH):
                for p in range(16):
                    tevac(p, tKT, ch)
                for j in range(16):
                    v.wait_ge(mm, tKB(ch, j))
                    dful = ap_of(kff, j * 8, [[8192, 128], [1, 8], [128, 64]])
                    v.tensor_copy(dful, pc[j % 2][:, :].rearrange("p (a b) -> p a b", a=8)).then_inc(ve, 1)
                for p in range(16):
                    tevac(p, tDT, ch)
                for j in range(16):
                    v.wait_ge(mm, tDB(ch, j))
                    dd = ap_of(af, j * 8, [[8192, 128], [1, 8], [128, 64]])
                    v.tensor_copy(dd, pc[j % 2][:, :].rearrange("p (a b) -> p a b", a=8)).then_inc(ve, 1)
                v.wait_ge(mm, tDB(ch, 15))
                dr = sb_ap(af, 0, [[1, 8192]], base_part=0, nparts=64)
                di = sb_ap(af, 0, [[1, 8192]], base_part=64, nparts=64)
                kfr0 = sb_ap(kff, 0, [[1, 8192]], base_part=0, nparts=64)
                kfi64 = sb_ap(kff, 0, [[1, 8192]], base_part=64, nparts=64)
                v.tensor_tensor(at2r[:, :], dr, kfr0, ALU.mult).then_inc(ve, 1)
                v.tensor_tensor(at2i[:, :], di, kfi64, ALU.mult).then_inc(ve, 1)
                v.tensor_tensor(at2r[:, :], at2r[:, :], at2i[:, :], ALU.subtract).then_inc(ve, 1)
                v.tensor_copy(at2i[:, :], kfi64).then_inc(ve, 1)
                v.tensor_tensor(at2i[:, :], dr, at2i[:, :], ALU.mult).then_inc(ve, 1)
                v.tensor_copy(kfi64, kfr0).then_inc(ve, 1)
                v.tensor_tensor(kfr0, di, kfi64, ALU.mult).then_inc(ve, 1)
                v.tensor_tensor(at2i[:, :], at2i[:, :], kfr0, ALU.add).then_inc(ve, 1)
                for p in range(16):
                    tevac(p, tIT, ch)
                v.wait_ge(se, sI2(ch, 15))
                zv = sb_ap(af, 0, [[128, 64], [1, 128]], base_part=0, nparts=32)
                xr = ap_of(xre, 0, [[8192, 32], [128, 64], [1, 128]])
                xi = ap_of(xim, 0, [[8192, 32], [128, 64], [1, 128]])
                dk1 = ap_of(dk_sb, ch * 128, [[1024, 32], [0, 64], [1, 128]])
                zk = ap_of(zko, 0, [[8192, 32], [128, 64], [1, 128]])
                zk2 = ap_of(zko2sb, 0, [[8192, 32], [128, 64], [1, 128]])
                v.tensor_tensor(zv, xr, dk1, ALU.mult).then_inc(ve, 1)
                v.tensor_tensor(zk, zk, zv, ALU.add).then_inc(ve, 1)
                v.tensor_tensor(zv, xi, dk1, ALU.mult).then_inc(ve, 1)
                v.tensor_tensor(zk2, zk2, zv, ALU.add).then_inc(ve, 1)


def build_p2():
    return _standalone(emit_p2)


NBH = BPC * 16
TPC8, VPC8, SPC8 = 42, 81, 30
SCALE8 = float(64 ** -0.5)


def emit_p8(fx):
    qk1_s = fx.dram("qk1_s", [2 * D, NTOK], BF16, kind="ExternalInput")
    v1_s = fx.dram("v1_s", [BPC, L, D], BF16, kind="ExternalInput")
    maskB = fx.dram("maskB", [128, 2048], F32, kind="ExternalInput")
    maskA = fx.dram("maskA", [128, 2048], F32, kind="ExternalInput")
    idn = fx.dram("idn_f32", [128, 128], F32, kind="ExternalInput")
    oT_s = fx.dram("oT_s", [D, BPC * 512], BF16, kind="ExternalOutput")
    rowb = fx.dram("rowb8", [12, 128], F32)

    with fx.phase() as al:
        qt = [al.sb(f"qt8{j}", [64, 512], BF16) for j in range(2)]
        kt = [al.sb(f"kt8{j}", [64, 2048], BF16) for j in range(2)]
        vt = [al.sb(f"vt8{j}", [128, 1024], BF16) for j in range(2)]
        mB_sb = al.sb("mB_sb", [128, 2048], F32)
        mA_sb = al.sb("mA_sb", [128, 2048], F32)
        idn_sb = al.sb("idn8", [128, 128], F32)
        onesf = al.sb("onesf8", [1, 128], F32)
        pAt = [al.sb(f"pAt8{j}", [128, 512], BF16) for j in range(2)]
        mx = al.sb("mx8", [128, 16], F32)
        M_ = al.sb("M8", [128, 16], F32)
        nM = al.sb("nM8", [128, 16], F32)
        eM = al.sb("eM8", [128, 16], F32)
        P_ = al.sb("P8c", [128, 16], F32)
        dM = al.sb("dM8", [128, 16], F32)
        ee = al.sb("ee8", [128, 16], F32)
        lc = al.sb("lc8", [128, 16], F32)
        ri_ = al.sb("ri8", [128, 16], F32)
        rr = al.sb("rr8", [128, 16], F32)
        qq_ = al.sb("qq8", [128, 16], F32)
        ww = al.sb("ww8", [128, 16], F32)
        sc_ = al.sb("sc8", [128, 16], F32)
        scol = al.sb("scol8", [128, 12], F32)
        scolT = al.sb("scolT8", [12, 128], F32)
        srow = al.sb("srow8", [1, 1536], F32)
        preps = al.sb("preps8", [64, 1536], F32)
        etr = al.sb("etr8", [128, 512], F32)
        oacc = al.sb("oacc8", [64, 512], F32)
        otmp = al.sb("otmp8", [64, 512], F32)
        otile = [al.sb(f"otile8{j}", [64, 512], BF16) for j in range(2)]
        pscr = al.ps("pscr8", [128, 1536])
        pA = al.ps("pA8", [128, 1024])
        po = [al.ps(f"po8{j}", [64, 512]) for j in range(3)]
        (mm, ve, se, rb, ldc, ini8) = [al.sem(n) for n in ("mm", "ve", "se", "rb", "ldc", "ini8")]
        stp = [al.sem(f"stp8{j}") for j in range(2)]
        lqp = [al.sem(f"lqp{j}") for j in range(2)]
        block = al.block()

        def tSB(n, tqc, j): return TPC8 * n + 4 * tqc + j + 1
        def tTR(n): return TPC8 * n + 17
        def tRP(n): return TPC8 * n + 18
        def tSA(n, idx): return TPC8 * n + 18 + 2 * idx + 1
        def tPV(n, idx): return TPC8 * n + 18 + 2 * idx + 2

        def vstat(n, tqc): return VPC8 * n + 13 * tqc
        def vSTe(n): return VPC8 * n + 52
        def vTL(n, i): return VPC8 * n + 52 + i + 1
        def vTRe(n): return VPC8 * n + 70
        def vMK(n, tkc): return VPC8 * n + 70 + tkc + 1
        def vPR(n): return VPC8 * n + 75
        def vCB(n, i): return VPC8 * n + 75 + i + 1

        def sP(n, tqc, j): return SPC8 * n + 4 * tqc + j + 1
        def sEE(n): return SPC8 * n + 17
        def sEM(n): return SPC8 * n + 18
        def sX(n, idx): return SPC8 * n + 18 + idx + 1

        @block.gpsimd
        def _(g):
            for i, (dst, src) in enumerate(((mB_sb, maskB), (mA_sb, maskA), (idn_sb, idn))):
                if i:
                    g.wait_ge(ldc, 16 * i)
                g.dma_start(out=dst[:, :], in_=src.ap()[:, :]).then_inc(ldc, 16)
            g.wait_ge(ldc, 48)
            g.memset(onesf[:, :], 1.0).then_inc(ini8, 1)
            for n in range(NBH):
                b, h = n // 16, n % 16
                if n >= 2:
                    g.wait_ge(mm, tPV(n - 2, 11))
                g.dma_start(out=qt[n % 2][:, :],
                            in_=ap_of(qk1_s, (h * 64) * NTOK + b * L + 1536,
                                      [[NTOK, 64], [1, 512]])).then_inc(lqp[n % 2], 16)
                g.wait_ge(lqp[n % 2], 48 * (n // 2) + 16)
                g.dma_start(out=kt[n % 2][:, :],
                            in_=ap_of(qk1_s, (D + h * 64) * NTOK + b * L,
                                      [[NTOK, 64], [1, 2048]])).then_inc(lqp[n % 2], 16)
                g.wait_ge(lqp[n % 2], 48 * (n // 2) + 32)
                g.dma_start(out=vt[n % 2][:, :].rearrange("p (a b) -> p a b", a=16),
                            in_=ap_of(v1_s, b * L * D + h * 64,
                                      [[D, 128], [128 * D, 16], [1, 64]])).then_inc(lqp[n % 2], 16)
                g.wait_ge(ve, vTRe(n))
                g.dma_start(out=rowb.ap()[:, :], in_=scolT[:, :]).then_inc(rb, 16)
                g.wait_ge(rb, 32 * n + 16)
                g.dma_start(out=srow[0:1, :],
                            in_=ap_of(rowb, 0, [[1536, 1], [1, 1536]])).then_inc(rb, 16)

        @block.tensor
        def _(t):
            t.wait_ge(ldc, 48)
            t.wait_ge(ini8, 1)
            for n in range(NBH):
                t.wait_ge(lqp[n % 2], 48 * (n // 2 + 1))
                if n >= 1:
                    t.wait_ge(ve, vPR(n - 1))
                for tqc in range(4):
                    for j in range(4):
                        i16 = 4 * tqc + j
                        if n > 0 or i16 >= 2:
                            p16, pn = i16 - 2, n
                            if p16 < 0:
                                p16, pn = 14 + i16, n - 1
                            t.wait_ge(se, sP(pn, p16 // 4, p16 % 4))
                        bk = (i16 % 2) * 512
                        t.matmul(pscr[:, bk:bk + 512], sb_ap(qt[n % 2], tqc * 128, [[1, 128]], nparts=64),
                                 sb_ap(kt[n % 2], j * 512, [[1, 512]], nparts=64),
                                 start=True, stop=True).then_inc(mm, 1)
                t.wait_ge(ve, vTL(n, 16))
                t.transpose(pscr[0:12, 1024:1152], scol[:, 0:12], idn_sb[:, :]).then_inc(mm, 1)
                t.wait_ge(rb, 32 * (n + 1))
                t.wait_ge(se, sP(n, 3, 3))
                t.matmul(pscr[0:64, 0:512], onesf[0:1, 0:64], srow[0:1, 0:512], start=True, stop=True)
                t.matmul(pscr[0:64, 512:1024], onesf[0:1, 0:64], srow[0:1, 512:1024], start=True, stop=True)
                t.matmul(pscr[0:64, 1024:1536], onesf[0:1, 0:64], srow[0:1, 1024:1536],
                         start=True, stop=True).then_inc(mm, 1)
                for idx in range(12):
                    j, tkc = idx // 4 + 1, idx % 4
                    if idx >= 2:
                        t.wait_ge(se, sX(n, idx - 2))
                    elif n > 0:
                        t.wait_ge(se, sX(n - 1, 10 + idx))
                    if idx == 0:
                        t.wait_ge(ve, VPC8 * n)
                    bkA = (idx % 2) * 512
                    t.matmul(pA[:, bkA:bkA + 512],
                             sb_ap(kt[n % 2], j * 512 + tkc * 128, [[1, 128]], nparts=64),
                             sb_ap(qt[n % 2], 0, [[1, 512]], nparts=64),
                             start=True, stop=True).then_inc(mm, 1)
                    t.wait_ge(se, sX(n, idx))
                    t.matmul(po[j - 1][:, :], sb_ap(vt[n % 2], (j * 4 + tkc) * 64, [[1, 64]]),
                             pAt[idx % 2][:, :], start=(tkc == 0), stop=(tkc == 3)).then_inc(mm, 1)

        @block.vector
        def _(v):
            v = AutoWait(v, ve)
            AV = lambda t_, o=0: ap_of(t_, o, [[16, 128], [4, 4], [1, 3]])
            BV = lambda t_, o=1: ap_of(t_, o, [[16, 128], [4, 4], [1, 3]])
            for n in range(NBH):
                for tqc in range(4):
                    for j in range(4):
                        c = 4 * tqc + j
                        v.wait_ge(mm, tSB(n, tqc, j))
                        bk = (c % 2) * 512
                        if j == 3:
                            v.tensor_tensor(pscr[:, bk:bk + 512], pscr[:, bk:bk + 512],
                                            sb_ap(mB_sb, tqc * 512, [[1, 512]]), ALU.add).then_inc(ve, 1)
                        v.reduce_max(mx[:, c:c + 1], pscr[:, bk:bk + 512], axis=AXX).then_inc(ve, 1)
                        if j == 0:
                            v.tensor_copy(M_[:, c:c + 1], mx[:, c:c + 1]).then_inc(ve, 1)
                        else:
                            v.tensor_tensor(M_[:, c:c + 1], M_[:, c - 1:c], mx[:, c:c + 1],
                                            ALU.max).then_inc(ve, 1)
                        v.tensor_scalar_mul(nM[:, c:c + 1], M_[:, c:c + 1], -SCALE8).then_inc(ve, 1)
                v.tensor_tensor(AV(dM), AV(M_), BV(M_), ALU.subtract).then_inc(ve, 1)
                v.wait_ge(se, sP(n, 3, 3))
                v.tensor_copy(ap_of(lc, 0, [[16, 128], [4, 4], [1, 1]]),
                              ap_of(P_, 0, [[16, 128], [4, 4], [1, 1]])).then_inc(ve, 1)
                v.wait_ge(se, sEE(n))
                cw0 = lambda t_, j: ap_of(t_, j, [[16, 128], [4, 4], [1, 1]])
                for j in range(3):
                    v.tensor_tensor(cw0(mx, j), cw0(lc, j), cw0(ee, j), ALU.mult).then_inc(ve, 1)
                    v.tensor_tensor(cw0(lc, j + 1), cw0(mx, j), cw0(P_, j + 1), ALU.add).then_inc(ve, 1)
                v.reciprocal(AV(ri_), BV(lc)).then_inc(ve, 1)
                v.tensor_tensor(AV(rr), AV(lc), AV(ri_), ALU.mult).then_inc(ve, 1)
                v.tensor_scalar(AV(qq_), AV(rr), -1.0, 1.0, ALU.mult, ALU.add).then_inc(ve, 1)
                cw = lambda t_, j: ap_of(t_, j, [[16, 128], [4, 4], [1, 1]])
                v.tensor_copy(cw(ww, 2), cw(rr, 2)).then_inc(ve, 1)
                v.tensor_tensor(cw(ww, 1), cw(rr, 1), cw(qq_, 2), ALU.mult).then_inc(ve, 1)
                v.tensor_tensor(cw(ww, 0), cw(rr, 0), cw(qq_, 1), ALU.mult).then_inc(ve, 1)
                v.tensor_tensor(cw(ww, 0), cw(ww, 0), cw(qq_, 2), ALU.mult).then_inc(ve, 1)
                v.wait_ge(se, sEM(n))
                v.tensor_tensor(AV(sc_), AV(ww), BV(eM), ALU.mult).then_inc(ve, 1)
                v.tensor_tensor(ap_of(scol, 0, [[12, 128], [1, 4], [4, 3]]),
                                AV(sc_), ap_of(ri_, 2, [[16, 128], [4, 4], [0, 3]]),
                                ALU.mult).then_inc(ve, 1)
                v.wait_ge(mm, tTR(n))
                v.tensor_copy(scolT[:, :], pscr[0:12, 1024:1152]).then_inc(ve, 1)
                for tkc in range(4):
                    v.wait_ge(mm, tSA(n, 8 + tkc))
                    bkA = ((8 + tkc) % 2) * 512
                    v.tensor_tensor(pA[:, bkA:bkA + 512], pA[:, bkA:bkA + 512],
                                    sb_ap(mA_sb, tkc * 512, [[1, 512]]), ALU.add).then_inc(ve, 1)
                v.wait_ge(mm, tRP(n))
                v.tensor_copy(preps[:, :], pscr[0:64, 0:1536]).then_inc(ve, 1)
                v.wait_ge(mm, tPV(n, 11))
                v.tensor_tensor(oacc[:, :], po[0][:, :], sb_ap(preps, 0, [[1, 512]], nparts=64),
                                ALU.mult).then_inc(ve, 1)
                v.tensor_tensor(otmp[:, :], po[1][:, :], sb_ap(preps, 512, [[1, 512]], nparts=64),
                                ALU.mult).then_inc(ve, 1)
                v.tensor_tensor(oacc[:, :], oacc[:, :], otmp[:, :], ALU.add).then_inc(ve, 1)
                v.tensor_tensor(otmp[:, :], po[2][:, :], sb_ap(preps, 1024, [[1, 512]], nparts=64),
                                ALU.mult).then_inc(ve, 1)
                v.tensor_tensor(oacc[:, :], oacc[:, :], otmp[:, :], ALU.add).then_inc(ve, 1)
                if n >= 2:
                    v.wait_ge(stp[n % 2], 16 * (n // 2))
                v.tensor_copy(otile[n % 2][:, :], oacc[:, :]).then_inc(ve, 1)

        @block.scalar
        def _(s):
            s = AutoWait(s, se)
            for n in range(NBH):
                for tqc in range(4):
                    vb = vstat(n, tqc)
                    for j in range(4):
                        c = 4 * tqc + j
                        v_ready = vb + (3 * (j + 1) if j < 3 else 13)
                        s.wait_ge(ve, v_ready)
                        bk = (c % 2) * 512
                        s.activation(etr[:, :], pscr[:, bk:bk + 512], AF.Exp,
                                     bias=nM[:, c:c + 1], scale=SCALE8,
                                     accum_out=P_[:, c:c + 1]).then_inc(se, 1)
                s.wait_ge(ve, vTL(n, 0))
                s.activation(ap_of(ee, 0, [[16, 128], [4, 4], [1, 3]]),
                             ap_of(dM, 0, [[16, 128], [4, 4], [1, 3]]),
                             AF.Exp, scale=SCALE8).then_inc(se, 1)
                s.wait_ge(ve, vSTe(n))
                s.activation(eM[:, :], nM[:, :], AF.Exp).then_inc(se, 1)
                for idx in range(12):
                    j, tkc = idx // 4 + 1, idx % 4
                    s.wait_ge(mm, tSA(n, idx))
                    if j == 3:
                        s.wait_ge(ve, vMK(n, tkc))
                    if idx >= 2:
                        s.wait_ge(mm, tPV(n, idx - 2))
                    elif n > 0:
                        s.wait_ge(mm, tPV(n - 1, 10 + idx))
                    bkA = (idx % 2) * 512
                    s.activation(pAt[idx % 2][:, :], pA[:, bkA:bkA + 512], AF.Exp,
                                 scale=SCALE8).then_inc(se, 1)

        @block.sync
        def _(sy):
            for n in range(NBH):
                b, h = n // 16, n % 16
                sy.wait_ge(ve, vCB(n, 5))
                sy.dma_start(out=ap_of(oT_s, (h * 64) * BPC * 512 + b * 512,
                                       [[BPC * 512, 64], [1, 512]]),
                             in_=otile[n % 2][:, :]).then_inc(stp[n % 2], 16)
            sy.wait_ge(stp[0], 16 * (NBH // 2))
            sy.wait_ge(stp[1], 16 * (NBH // 2))


def build_p8():
    return _standalone(emit_p8)



def stage_inputs(inp):
    f32 = np.float32
    u = np.ascontiguousarray(np.asarray(inp["L0_attn_u"], f32))
    shared = {}
    wqkv0 = np.concatenate([np.asarray(inp["L0_attn_Wq"], f32),
                            np.asarray(inp["L0_attn_Wk"], f32),
                            np.asarray(inp["L0_attn_Wv"], f32)], 0)
    shared["wqkv0T"] = np.ascontiguousarray(wqkv0.T)
    shared["bqkv0_row"] = np.ascontiguousarray(np.concatenate(
        [np.asarray(inp["L0_attn_bq"], f32), np.asarray(inp["L0_attn_bk"], f32),
         np.asarray(inp["L0_attn_bv"], f32)])[None, :])
    shared["woT"] = np.ascontiguousarray(np.asarray(inp["L0_attn_Wo"], f32).T)
    shared["bo_row"] = np.ascontiguousarray(np.asarray(inp["L0_attn_bo"], f32)[None, :])
    shared["idn_f32"] = np.eye(128, dtype=f32)
    shared["w1_0"] = np.ascontiguousarray(np.asarray(inp["L0_ffn_w1"], f32))
    shared["b1_0pt"] = np.ascontiguousarray(np.asarray(inp["L0_ffn_b1"], f32).reshape(32, 128).T)
    shared["w2_0"] = np.ascontiguousarray(np.asarray(inp["L0_ffn_w2"], f32))
    shared["b2_0pt"] = np.ascontiguousarray(np.asarray(inp["L0_ffn_b2"], f32).reshape(8, 128).T)
    qkvb = np.asarray(inp["L1_attn_qkv_b"], f32)
    shared["qkvw1T"] = np.ascontiguousarray(np.asarray(inp["L1_attn_qkv_w"], f32).T)
    shared["qkvb1_pt"] = np.ascontiguousarray(qkvb[:2048].reshape(16, 128).T)
    shared["bv1_row"] = np.ascontiguousarray(qkvb[2048:][None, :])
    shared["projT"] = np.ascontiguousarray(np.asarray(inp["L1_attn_proj_w"], f32).T)
    shared["projb_row"] = np.ascontiguousarray(np.asarray(inp["L1_attn_proj_b"], f32)[None, :])
    shared["projb_rep"] = np.ascontiguousarray(
        np.broadcast_to(np.asarray(inp["L1_attn_proj_b"], f32)[None, :], (128, D)))
    shared["w1_1"] = np.ascontiguousarray(np.asarray(inp["L1_ffn_w1"], f32))
    shared["w1T_1"] = np.ascontiguousarray(np.asarray(inp["L1_ffn_w1"], f32).T)
    shared["b1_1pt"] = np.ascontiguousarray(np.asarray(inp["L1_ffn_b1"], f32).reshape(32, 128).T)
    shared["w2_1"] = np.ascontiguousarray(np.asarray(inp["L1_ffn_w2"], f32))
    shared["w2T_1"] = np.ascontiguousarray(np.asarray(inp["L1_ffn_w2"], f32).T)
    shared["b2_1rep"] = np.ascontiguousarray(
        np.broadcast_to(np.asarray(inp["L1_ffn_b2"], f32)[None, :], (128, D)))
    per_core = []
    for c in range(NCORES):
        sl = u[c * BPC:(c + 1) * BPC]
        per_core.append({
            "uT": np.ascontiguousarray(sl.transpose(0, 2, 1)),
            "u_tok": np.ascontiguousarray(sl),
        })
    shared["ssmkkT"] = np.ascontiguousarray(np.asarray(inp["L0_attn_ssm_k_kernel"], f32).T)
    shared["dkrep"] = np.ascontiguousarray(
        np.broadcast_to(np.asarray(inp["L0_attn_ssm_k_D"], f32)[None, :], (128, D)))
    kern = np.asarray(inp["L0_attn_ssm_kernel"], f32)
    dt_i = np.arange(128)[None, None, :]
    ds_i = np.arange(128)[None, :, None]
    dl_i = np.arange(16)[:, None, None]
    idx = 128 * dl_i + dt_i - ds_i
    valid = (idx >= 0) & (idx < L)
    shared["toep"] = np.ascontiguousarray(
        np.where(valid[None], kern[:, np.clip(idx, 0, L - 1)], 0.0).astype(f32))
    shared["drep"] = np.ascontiguousarray(
        np.broadcast_to(np.asarray(inp["L0_attn_D"], f32)[None, :], (128, 128)))
    n2a = np.arange(32)[:, None, None]
    n1a = np.arange(64)[None, :, None]
    k1a = np.arange(64)[None, None, :]
    Wf = np.exp(-2j * np.pi * (n2a * k1a / 64.0 + n1a * k1a / 4096.0))
    wa2 = np.zeros((32, 16384), np.float64)
    for a_ in range(64):
        wa2[:, a_*256:a_*256+64] = Wf[:, a_, :].real
        wa2[:, a_*256+64:a_*256+128] = Wf[:, a_, :].imag
        wa2[:, a_*256+128:a_*256+192] = -Wf[:, a_, :].imag
        wa2[:, a_*256+192:a_*256+256] = Wf[:, a_, :].real
    shared["fft_wa2"] = wa2.astype(BF)
    k2a = np.arange(64)[:, None, None]
    n1b = np.arange(64)[None, None, :]
    k1b = np.arange(64)[None, :, None]
    Vf = np.exp(2j * np.pi * (k2a * n1b / 64.0 + n1b * k1b / 4096.0))
    va2 = np.zeros((64, 16384), np.float64)
    for a_ in range(64):
        va2[:, a_*256:a_*256+64] = Vf[:, a_, :].real
        va2[:, a_*256+64:a_*256+128] = Vf[:, a_, :].imag
        va2[:, a_*256+128:a_*256+192] = -Vf[:, a_, :].imag
        va2[:, a_*256+192:a_*256+256] = Vf[:, a_, :].real
    shared["fft_va2"] = va2.astype(BF)
    Bf = np.exp(-2j * np.pi * np.arange(64)[:, None] * np.arange(64)[None, :] / 64.0)
    shared["fft_wbst"] = np.ascontiguousarray(
        np.concatenate([Bf.real, Bf.imag, -Bf.imag, Bf.real], 1).astype(BF))
    Cf = np.exp(2j * np.pi * np.arange(64)[:, None] * np.arange(32)[None, :] / 64.0) / 4096.0
    shared["fft_va3"] = np.ascontiguousarray(
        np.concatenate([Cf.real, Cf.imag, -Cf.imag, Cf.real], 1).astype(BF))
    shared["idn_bf16"] = np.eye(128).astype(BF)
    p_i = np.arange(128)[:, None]; x_i = np.arange(512)[None, :]
    mB = np.zeros((128, 2048), f32)
    mA = np.zeros((128, 2048), f32)
    for c4 in range(4):
        mB[:, c4 * 512:(c4 + 1) * 512] = np.where(x_i <= c4 * 128 + p_i, 0.0, -1e9)
        mA[:, c4 * 512:(c4 + 1) * 512] = np.where(c4 * 128 + p_i <= x_i, 0.0, -1e9)
    shared["maskB"] = mB
    shared["maskA"] = mA
    return shared, per_core


def _launch(nc, in_maps):
    return run_bass_kernel_spmd(nc, in_maps, core_ids=list(range(NCORES))).results


PHASES = None


def build_fused():
    nc = bass.Bass("TRN2", target_bir_lowering=False, debug=False)
    fx = FX(nc, "fused")
    with ExitStack() as es:
        fx.open(es)
        for emit in (emit_p1, emit_p2, emit_p3, emit_p4, emit_p5,
                     emit_p6, emit_p7, emit_p8, emit_p9, emit_p10):
            emit(fx)
    return nc


def kernel(**inputs):
    shared, per_core = stage_inputs(inputs)
    keys = sorted(FUSED_INPUTS - {"uT", "u_tok"})
    in_maps = [{**{k: shared[k] for k in keys}, **per_core[c]} for c in range(NCORES)]
    res = _launch(build_fused(), in_maps)
    out = np.concatenate([np.asarray(res[c]["out_s"]) for c in range(NCORES)], 0)
    return out.astype(np.float32)
